# version 25
# baseline (speedup 1.0000x reference)
import contextlib
import numpy as np
import ml_dtypes
import concourse.bass as bass
import concourse.mybir as mybir
from concourse.bass_isa import ReduceOp
from concourse.bass_utils import run_bass_kernel_spmd

F32 = mybir.dt.float32
BF16 = mybir.dt.bfloat16
I32 = mybir.dt.int32
AF = mybir.ActivationFunctionType
ALU = mybir.AluOpType
AX = mybir.AxisListType

NCORE = 8
T = 2048
D = 1024
DC = 8
H = 16
NHD = 64
Q = 256
NCH = T // Q
SLAB = 512
NSLAB = T // SLAB
HID = 4096
HCC = 32
EPS = 1e-5
HB = SLAB + 1

ENGS = ("sync", "tensor", "vector", "scalar", "gpsimd")


def build(upto="p1", dbg=(), loopback=False):
    nc = bass.Bass("TRN2", target_bir_lowering=False,
                   detect_race_conditions=False)
    es = contextlib.ExitStack()

    def din(name, shape, dt=F32):
        return nc.dram_tensor(name, shape, dt, kind="ExternalInput").ap()

    dram = {}
    for name, shape in [
            ("x_fm", [DC, 128, T]), ("x_halo", [128, DC]), ("halovalid", [128, 1]),
            ("lrvalid", [128, 2]), ("selmat", [16, 2]), ("exps2048", [1, 8]),
            ("td_r", [1, H]), ("fa_r", [1, H]), ("c_fm", [128, DC]),
            ("wmod_r", [12, 128, DC * 512]), ("maa_k", [128, DC]),
            ("maa_v", [128, DC]), ("maa_r", [128, DC]), ("br_c", [128, DC]),
            ("bk_c", [128, DC]), ("bo_c", [128, DC]), ("bv_r", [1, D]),
            ("wrT", [DC, 128, D]), ("wkT", [DC, 128, D]), ("wvT", [DC, 128, D]),
            ("woT", [DC, 128, D]), ("w1T", [DC, 128, HID]), ("w2T", [HCC, 128, D]),
            ("cw_c", [128, HCC * 3]), ("cb_c", [128, HCC]),
            ("x_cm", [128, NCH, DC * Q])]:
        dram[name] = din(name, shape)
    dram["out_fm"] = nc.dram_tensor("out_fm", [DC, 128, T], F32,
                                    kind="ExternalOutput").ap()
    dram["sloc"] = nc.dram_tensor("sloc", [128, 512], F32).ap()
    dram["sglob"] = nc.dram_tensor("sglob", [NCORE * 128, 512], F32,
                                   addr_space="Shared").ap()
    dram["rows2"] = nc.dram_tensor("rows2", [2, D], BF16).ap()
    dram["rowsg"] = nc.dram_tensor("rowsg", [16, D], BF16,
                                   addr_space="Shared").ap()
    dram["w2q_d"] = nc.dram_tensor("w2q_d", [HCC, 128, D], BF16).ap()
    dram["y_d"] = nc.dram_tensor(
        "y_d", [128, NCH, DC * Q], BF16,
        kind="ExternalOutput" if "y_d" in dbg else "Internal").ap()
    dram["rw_d"] = nc.dram_tensor(
        "rw_d", [128, NCH, DC * Q], BF16,
        kind="ExternalOutput" if "rw_d" in dbg else "Internal").ap()

    dbg_shapes = {
        "modrow": ([1, 6144], F32), "amask": ([128, H * 2 * Q], BF16),
        "wkF": ([128, DC * Q], BF16), "wbF": ([128, DC * Q], BF16),
        "e2h": ([128, DC], F32), "ws_pp": ([128, DC], F32),
        "hbuf": ([128, DC * HB], BF16),
        "kfs": ([128, DC * SLAB], BF16), "vTs": ([128, 4 * D], BF16),
        "rbf_sl": ([128, DC * SLAB], BF16),
        "sst": ([128, DC * NHD], F32),
        "ga_c": ([128, DC], F32), "gb_c": ([128, DC], F32),
        "wsmT": ([128, DC * NCH], F32), "combW": ([128, DC * 8], F32),
        "u128": ([128, H], F32), "negE128": ([128, H], F32),
        "sa_c": ([128, DC], F32), "ca1_c": ([128, DC], F32),
        "siluc": ([128, DC], BF16),
        "sb2_c": ([128, DC], F32), "cb1_c": ([128, DC], F32),
        "xres": ([128, DC * (T + 2)], BF16), "m2B": ([128, T + 2], BF16),
        "r2B": ([128, T + 2], BF16), "sgin": ([128, 512], F32),
        "w1qb": ([128, DC * HID], BF16), "wscl": ([128, 2], F32),
        "swc": ([128, 2], F32), "ybf3": ([128, DC * Q], BF16),
        "h2w": ([128, DC * SLAB], BF16), "udq": ([128, HCC * SLAB], BF16),
        "s1bb": ([128, SLAB], BF16), "uib": ([128, SLAB], BF16),
        "s2bb": ([128, SLAB], BF16), "ds2b": ([128, SLAB], BF16),
    }
    dbg_out = {}
    for nm in dbg:
        if nm in ("y_d", "rw_d"):
            continue
        shp, dt = dbg_shapes[nm]
        dbg_out[nm] = nc.dram_tensor("dbg_" + nm, shp, dt, kind="ExternalOutput").ap()

    sbt = {}
    with es:
        def _SB(stack, name, cols, dt=F32, parts=128):
            sbt[name] = stack.enter_context(nc.sbuf_tensor(name, [parts, cols], dt))
            return sbt[name]
        SB = lambda *a, **k: _SB(es, *a, **k)

        onesrow = SB("onesrow", 128, BF16, parts=1)
        ones64 = SB("ones64", 64, BF16, parts=1)
        allones = SB("allones", 128, BF16)
        identb = SB("identb", 128, BF16)
        negE128 = SB("negE128", H)
        u128 = SB("u128", H)
        e2h = SB("e2h", DC)
        ws_pp = SB("ws_pp", DC)
        wsmT = SB("wsmT", DC * NCH)
        combW = SB("combW", DC * 8)
        siluc = SB("siluc", DC, BF16)
        sc128 = SB("sc128", DC)
        sc128b = SB("sc128b", DC)
        sa_c = SB("sa_c", DC); ca1_c = SB("ca1_c", DC)
        ga_c = SB("ga_c", DC); sb2_c = SB("sb2_c", DC)
        cb1_c = SB("cb1_c", DC); gb_c = SB("gb_c", DC)
        gabo_c = SB("gabo_c", DC)
        maa_k = SB("maa_ks", DC); maa_v = SB("maa_vs", DC); maa_r = SB("maa_rs", DC)
        br_c = SB("br_s", DC); bk_c = SB("bk_s", DC); bo_c = SB("bo_s", DC)
        padt = SB("padt", 24 + 3 * (D // 2), parts=1)
        hvalid = SB("hvalid", 1)
        lrv = SB("lrv", 2)
        bvF = SB("bvF", D, BF16)
        mB = SB("mB", SLAB, BF16)
        mF = SB("mF", SLAB)
        rstdB = SB("rstdB", SLAB, BF16)
        statf = SB("statf", SLAB)
        statf2 = SB("statf2", SLAB)
        statf3 = SB("statf3", SLAB)
        btmp = SB("btmp", SLAB, BF16)
        btmp2 = SB("btmp2", SLAB, BF16)
        sst = SB("sst", DC * NHD)
        wstat = SB("wstat", 4 * 512)
        wqi = SB("wqi", 512, I32)
        sbt["wqi2"] = wqi
        wacc = SB("wacc", 128)
        wabs = SB("wabs", 512, BF16)
        sbt["wabs"] = wabs
        sbt["allonesF"] = SB("allonesF", 128)
        sbt["s128"] = SB("s128", 2)
        sbt["swc"] = SB("swc", 2)
        sbt["wscl"] = SB("wscl", 2)
        sbt["sw127"] = SB("sw127", 2)
        esR = contextlib.ExitStack()
        SBR = lambda *a, **k: _SB(esR, *a, **k)
        sbfT = SBR("sbfT", DC * NHD, BF16)
        wkF = SBR("wkF", DC * Q, BF16)
        wbF = SBR("wbF", DC * Q, BF16)
        amask = SBR("amask", H * 2 * Q, BF16)
        hbuf = SBR("hbuf", DC * HB, BF16)

        sem = {}
        for s_ in ["bar", "dxb", "dcst", "dwgt", "dwmod", "gv", "vme", "s0", "v0",
                   "vgrow", "drow", "t0",
                   "vsq", "tstat", "vstat", "vh", "vmix", "gmix",
                   "tr", "tk", "tv", "sr", "sk", "vvT",
                   "vkwk", "tkwkT", "vkwkTe", "tA", "vA",
                   "ty", "sy", "tds", "vstate", "vrw", "dyo", "dro", "dbgd",
                   "dwmf", "t0f", "vmef", "dbgd2",
                   "dsl", "ccs", "dsg", "vcmb", "dld", "ty2", "vy2", "to",
                   "so3", "vxr", "dr2", "ccr", "drg", "tsel", "vrow", "dmh",
                   "gq3", "t2s", "v2s", "s2r", "t2h", "v2h", "s2h",
                   "dws", "sws", "vwsc", "dwq", "vq1", "gq2", "dw2o",
                   "vh4", "vax", "tbc", "vbc", "tu", "vud", "gcv", "vcv",
                   "ssl", "vax2", "tbc2", "vbc2", "vxq2", "dwl", "tw2",
                   "vo4", "dout", "vsc3", "gmn", "dtx", "gvc"]:
            sem[s_] = es.enter_context(nc.semaphore(s_))

        blk = es.enter_context(nc.Block())

        pP0 = es.enter_context(nc.psum_tensor("pP0", [128, 512], F32))
        pP1 = es.enter_context(nc.psum_tensor("pP1", [128, 512], F32))
        pA = es.enter_context(nc.psum_tensor("pA", [128, 2048], F32))
        pY0 = es.enter_context(nc.psum_tensor("pY0", [128, 512], F32))
        pY1 = es.enter_context(nc.psum_tensor("pY1", [128, 512], F32))
        vDS = nc.virtual_psum_tensor("vDS", [128, 512], F32)
        nc.back_virtual_tensor_with(vDS, pP0)
        psums = dict(pP0=pP0, pP1=pP1, pA=pA, pY0=pY0, pY1=pY1, vDS=vDS)

        barn = [0]
        def barrier():
            barn[0] += 1
            tgt = 5 * barn[0]
            for eng_name in ENGS:
                def mk(e, tgt=tgt):
                    e.drain().then_inc(sem["bar"], 1)
                    e.wait_ge(sem["bar"], tgt)
                getattr(blk, eng_name)(mk)

        esP0 = contextlib.ExitStack()
        SBP0 = lambda *a, **k: _SB(esP0, *a, **k)
        tdr = SBP0("tdr", H, parts=1)
        erow = SBP0("erow", H, parts=1)
        negE = SBP0("negE", H, parts=1)
        negE_hi = SBP0("negE_hi", H, BF16, parts=1)
        negE_lo = SBP0("negE_lo", H, BF16, parts=1)
        wsrow = SBP0("wsrow", H, parts=1)
        ws_hi = SBP0("ws_hi", H, BF16, parts=1)
        ws_lo = SBP0("ws_lo", H, BF16, parts=1)
        urow = SBP0("urow", H, parts=1)
        urow_b = SBP0("urow_b", H, BF16, parts=1)
        expsrow = SBP0("expsrow", 8, parts=1)
        expsrow_b = SBP0("expsrow_b", 8, BF16, parts=1)
        iotQr = SBP0("iotQr", Q, parts=1)
        iotQr2 = SBP0("iotQr2", Q, parts=1)
        argk = SBP0("argk", H * Q, parts=1)
        argb = SBP0("argb", H * Q, parts=1)
        wkrow_b = SBP0("wkrow_b", H * Q, BF16, parts=1)
        wbrow_b = SBP0("wbrow_b", H * Q, BF16, parts=1)
        modrow = SBP0("modrow", 6144, parts=1)
        garow = SBP0("garow", 2 * D, parts=1)
        rtmp = SBP0("rtmp", 2 * D, parts=1)

        identi = SBP0("identi", 128, I32)
        iot8 = SBP0("iot8", NCH)
        expsB = SBP0("expsB", 8)
        halosq = SBP0("halosq", DC, BF16)
        crow = SBP0("crow", DC)
        chalo = SBP0("chalo", DC)
        chalob = SBP0("chalob", DC, BF16)
        deltf = SBP0("deltf", 4 * Q)
        tri01 = SBP0("tri01", 2 * Q, BF16)
        i01 = SBP0("i01", 2 * Q, BF16)
        ftmpa = SBP0("ftmpa", 2 * Q)
        wmbuf = SBP0("wmbuf", 2 * DC * 512, BF16)
        wmf32 = SBP0("wmf32", DC * 512)
        sc128c = SBP0("sc128c", DC)


        T0_NEGE, T0_U, T0_E2H, T0_WS2H, T0_EXPSB, T0_BV = 9, 10, 11, 12, 13, 14
        T0_WKF = lambda cc: 15 + cc
        T0_WBF = lambda cc: 23 + cc
        T0_HS, T0_HQ = 31, 32

        @blk.sync
        def _(e):
            for dst, src in [
                    (tdr[:, :], dram["td_r"]), (urow[:, :], dram["fa_r"]),
                    (crow[:, :], dram["c_fm"]), (chalo[:, :], dram["x_halo"]),
                    (hvalid[:, :], dram["halovalid"]), (lrv[:, :], dram["lrvalid"]),
                    (maa_k[:, :], dram["maa_k"]), (maa_v[:, :], dram["maa_v"]),
                    (maa_r[:, :], dram["maa_r"]), (br_c[:, :], dram["br_c"]),
                    (bk_c[:, :], dram["bk_c"]), (bo_c[:, :], dram["bo_c"]),
                    (expsrow[:, :], dram["exps2048"]),
                    (padt[:, 0:8], dram["exps2048"]),
                    (padt[:, 8:16], dram["exps2048"]),
                    (padt[:, 16:24], dram["exps2048"])]:
                e.dma_start(out=dst, in_=src).then_inc(sem["dcst"], 16)
            e.wait_ge(sem["dcst"], 16 * 16)

        @blk.gpsimd
        def _(e):
            e.iota(identi[:, :], pattern=[[1, 128]], base=0, channel_multiplier=-1)
            e.iota(deltf[:, 0:Q], pattern=[[1, Q]], base=-1, channel_multiplier=-1,
                   allow_small_or_imprecise_dtypes=True)
            e.iota(deltf[:, Q:2 * Q], pattern=[[1, Q]], base=-129, channel_multiplier=-1,
                   allow_small_or_imprecise_dtypes=True)
            e.iota(iotQr[:, :], pattern=[[1, Q]], base=0, channel_multiplier=0,
                   allow_small_or_imprecise_dtypes=True)
            e.iota(iotQr2[:, :], pattern=[[-1, Q]], base=Q - 1, channel_multiplier=0,
                   allow_small_or_imprecise_dtypes=True)
            e.iota(iot8[:, :], pattern=[[Q, NCH]], base=0, channel_multiplier=0,
                   allow_small_or_imprecise_dtypes=True).then_inc(sem["gv"], 1)
            e.dma_start(out=bvF[0:1, :], in_=dram["bv_r"]).then_inc(sem["dwgt"], 16)
            for j in range(3):
                e.dma_start(out=padt[:, 24 + j * D // 2:24 + (j + 1) * D // 2],
                            in_=dram["bv_r"][0:1, 0:D // 2]).then_inc(sem["dwgt"], 16)

        @blk.vector
        def _(e):
            e.memset(onesrow[:, :], 1.0)
            e.memset(ones64[:, :], 1.0)
            e.memset(allones[:, :], 1.0)
            e.memset(pA[:, :], 0.0)
            e.memset(pY0[:, :], 0.0)
            e.memset(pY1[:, :], 0.0)
            e.memset(pP0[:, :], 0.0)
            e.memset(pP1[:, :], 0.0)
            e.memset(sst[:, :], 0.0)
            e.memset(sbfT[:, :], 0.0)
            e.wait_ge(sem["gv"], 1)
            e.tensor_scalar(out=identb[:, :], in0=identi[:, :], scalar1=0,
                            scalar2=None, op0=ALU.is_equal)

        @blk.scalar
        def _(e):
            e.wait_ge(sem["dcst"], 16 * 16)
            e.activation(out=erow[:, :], in_=tdr[:, :], func=AF.Exp)
            e.drain()
        barrier()

        @blk.vector
        def _(e):
            e.tensor_scalar(out=negE[:, :], in0=erow[:, :], scalar1=-1.0,
                            scalar2=None, op0=ALU.mult)
            e.drain()
            e.tensor_copy(out=negE_hi[:, :], in_=negE[:, :])
            e.tensor_copy(out=urow_b[:, :], in_=urow[:, :])
            e.tensor_copy(out=expsrow_b[:, :], in_=expsrow[:, :])
            e.drain()
            for h in range(H):
                e.tensor_scalar(out=argk[:, h * Q:(h + 1) * Q], in0=iotQr2[:, :],
                                scalar1=negE[:, h:h + 1], scalar2=None, op0=ALU.mult)
                e.tensor_scalar(out=argb[:, h * Q:(h + 1) * Q], in0=iotQr[:, :],
                                scalar1=negE[:, h:h + 1], scalar2=None, op0=ALU.mult)
            e.tensor_scalar(out=tri01[:, :], in0=deltf[:, 0:2 * Q], scalar1=0.0,
                            scalar2=None, op0=ALU.is_ge)
            e.tensor_scalar(out=i01[:, :], in0=deltf[:, 0:2 * Q], scalar1=-1.0,
                            scalar2=None, op0=ALU.is_equal)
            e.tensor_scalar(out=deltf[:, 2 * Q:4 * Q], in0=deltf[:, 0:2 * Q],
                            scalar1=-1.0, scalar2=None, op0=ALU.max)
            e.drain().then_inc(sem["v0"], 1)

        @blk.scalar
        def _(e):
            e.wait_ge(sem["v0"], 1)
            e.activation(out=wkrow_b[:, :], in_=argk[:, :], func=AF.Exp)
            e.activation(out=wbrow_b[:, :], in_=argb[:, :], func=AF.Exp)
            e.activation(out=sc128[:, :], in_=crow[:, :], func=AF.Exp,
                         scale=np.float32(-1.0)).then_inc(sem["s0"], 1)

        @blk.vector
        def _(e):
            e.wait_ge(sem["s0"], 1)
            e.tensor_scalar(out=sc128b[:, :], in0=sc128[:, :], scalar1=1.0,
                            scalar2=None, op0=ALU.add)
            e.drain()
            e.reciprocal(out=sc128[:, :], in_=sc128b[:, :])
            e.drain()
            e.tensor_tensor(out=sc128b[:, :], in0=sc128[:, :], in1=crow[:, :],
                            op=ALU.mult)
            e.drain()
            e.tensor_copy(out=siluc[:, :], in_=sc128b[:, :])
            e.tensor_copy(out=chalob[:, :], in_=chalo[:, :])
            e.drain()
            e.tensor_tensor(out=halosq[:, :], in0=chalob[:, :], in1=chalob[:, :],
                            op=ALU.mult)
            e.drain()
        barrier()

        @blk.gpsimd
        def _(e):
            for i, seg in enumerate((0, 1, 2, 3, 6, 7, 8, 9)):
                par = i % 2
                if i >= 2:
                    e.wait_ge(sem["t0"], i - 1)
                for j in range(4):
                    e.dma_start(out=wmbuf[:, par * DC * 512 + j * 1024:
                                        par * DC * 512 + (j + 1) * 1024],
                                in_=dram["wmod_r"][seg][:, j * 1024:(j + 1) * 1024]
                                ).then_inc(sem["dwmod"], 16)

        @blk.sync
        def _(e):
            for j, seg in enumerate((4, 5, 10, 11)):
                if j >= 1:
                    e.wait_ge(sem["t0f"], j)
                for jj in range(4):
                    e.dma_start(out=wmf32[:, jj * 1024:(jj + 1) * 1024],
                                in_=dram["wmod_r"][seg][:, jj * 1024:(jj + 1) * 1024]
                                ).then_inc(sem["dwmf"], 16)

        @blk.tensor
        def _(e):
            for i, seg in enumerate((0, 1, 2, 3, 6, 7, 8, 9)):
                par = i % 2
                e.wait_ge(sem["dwmod"], 64 * (i + 1))
                if i >= 1:
                    e.wait_ge(sem["vme"], i)
                for cc in range(DC):
                    mm = e.matmul(pA[0:1, 0:512], siluc[:, cc:cc + 1],
                                  wmbuf[:, par * DC * 512 + cc * 512:
                                        par * DC * 512 + (cc + 1) * 512],
                                  start=(cc == 0), stop=(cc == 7))
                mm.then_inc(sem["t0"], 1)
            for j, seg in enumerate((4, 5, 10, 11)):
                e.wait_ge(sem["dwmf"], 64 * (j + 1))
                if j >= 1:
                    e.wait_ge(sem["vmef"], j)
                for cc in range(DC):
                    mm = e.matmul(pA[0:1, 512:1024], sc128b[:, cc:cc + 1],
                                  wmf32[:, cc * 512:(cc + 1) * 512],
                                  start=(cc == 0), stop=(cc == 7))
                mm.then_inc(sem["t0f"], 1)
            e.matmul(pP1[:, 0:H], onesrow[:, :], negE_hi[:, :],
                     start=True, stop=True).then_inc(sem["t0"], 1)
            e.matmul(pP1[:, 16:16 + H], onesrow[:, :], urow_b[:, :],
                     start=True, stop=True).then_inc(sem["t0"], 1)
            e.nop().then_inc(sem["t0"], 1)
            e.nop().then_inc(sem["t0"], 1)
            e.matmul(pP1[:, 48:56], onesrow[:, :], expsrow_b[:, :],
                     start=True, stop=True).then_inc(sem["t0"], 1)
            e.wait_ge(sem["dwgt"], 16 * 4)
            e.wait_ge(sem["vmef"], 4)
            for sg in range(2):
                mm = e.matmul(pA[:, 512 + sg * 512:1024 + sg * 512], onesrow[:, :],
                              bvF[0:1, sg * 512:(sg + 1) * 512], start=True, stop=True)
            mm.then_inc(sem["t0"], 1)
            for cc in range(DC):
                if cc >= 1:
                    e.wait_ge(sem["v0"], 101 + cc)
                for hof in range(2):
                    h = 2 * cc + hof
                    mm = e.matmul(pP0[hof * 64:hof * 64 + 64,
                                      (cc % 2) * Q:(cc % 2 + 1) * Q],
                                  ones64[:, :], wkrow_b[:, h * Q:(h + 1) * Q],
                                  start=True, stop=True, tile_position=(0, hof * 64))
                mm.then_inc(sem["t0"], 1)
            for cc in range(DC):
                if cc >= 1:
                    e.wait_ge(sem["v0"], 109 + cc)
                for hof in range(2):
                    h = 2 * cc + hof
                    mm = e.matmul(pY1[hof * 64:hof * 64 + 64,
                                      (cc % 2) * Q:(cc % 2 + 1) * Q],
                                  ones64[:, :], wbrow_b[:, h * Q:(h + 1) * Q],
                                  start=True, stop=True, tile_position=(0, hof * 64))
                mm.then_inc(sem["t0"], 1)
            mm = e.matmul(pY0[:, 0:8], allones[:, :], chalob[:, :],
                          start=True, stop=True)
            mm.then_inc(sem["t0"], 1)
            mm = e.matmul(pY0[:, 8:16], allones[:, :], halosq[:, :],
                          start=True, stop=True)
            mm.then_inc(sem["t0"], 1)

        @blk.vector
        def _(e):
            for i, seg in enumerate((0, 1, 2, 3, 6, 7, 8, 9)):
                e.wait_ge(sem["t0"], i + 1)
                e.tensor_copy(out=modrow[:, seg * 512:(seg + 1) * 512],
                              in_=pA[0:1, 0:512]).then_inc(sem["vme"], 1)
            for j, seg in enumerate((4, 5, 10, 11)):
                e.wait_ge(sem["t0f"], j + 1)
                e.tensor_copy(out=modrow[:, seg * 512:(seg + 1) * 512],
                              in_=pA[0:1, 512:1024]).then_inc(sem["vmef"], 1)
            for j, base in enumerate((2048, 5120)):
                sl = slice(base, base + D)
                dsl = slice(j * D, (j + 1) * D)
                e.tensor_scalar(out=rtmp[:, dsl], in0=modrow[:, sl], scalar1=0.0,
                                scalar2=None, op0=ALU.is_ge)
                e.tensor_scalar(out=rtmp[:, dsl], in0=rtmp[:, dsl], scalar1=2.0,
                                scalar2=-1.0, op0=ALU.mult, op1=ALU.add)
                e.tensor_tensor(out=garow[:, dsl], in0=modrow[:, sl], in1=rtmp[:, dsl],
                                op=ALU.mult)
                e.tensor_scalar(out=garow[:, dsl], in0=garow[:, dsl], scalar1=0.25,
                                scalar2=None, op0=ALU.max)
                e.tensor_tensor(out=garow[:, dsl], in0=garow[:, dsl], in1=rtmp[:, dsl],
                                op=ALU.mult)
            e.drain().then_inc(sem["vgrow"], 1)
            e.wait_ge(sem["t0"], T0_EXPSB)
            e.tensor_copy(out=negE128[:, :], in_=pP1[:, 0:H])
            e.drain().then_inc(sem["v0"], 1)
            e.tensor_copy(out=u128[:, :], in_=pP1[:, 16:16 + H]).then_inc(sem["v0"], 1)
            for cc in range(DC):
                for hof in range(2):
                    e.tensor_copy(out=e2h[hof * 64:hof * 64 + 64, cc:cc + 1],
                                  in_=negE128[hof * 64:hof * 64 + 64,
                                              2 * cc + hof:2 * cc + hof + 1])
            e.drain().then_inc(sem["v0"], 1)
            e.nop().then_inc(sem["v0"], 1)
            e.tensor_copy(out=expsB[:, :], in_=pP1[:, 48:56]).then_inc(sem["v0"], 1)
            e.wait_ge(sem["t0"], T0_BV)
            e.tensor_copy(out=bvF[:, 0:512], in_=pA[:, 512:1024])
            e.tensor_copy(out=bvF[:, 512:1024], in_=pA[:, 1024:1536]
                          ).then_inc(sem["v0"], 95)
            for cc in range(DC):
                e.wait_ge(sem["t0"], T0_WKF(cc))
                e.tensor_copy(out=wkF[:, cc * Q:(cc + 1) * Q],
                              in_=pP0[:, (cc % 2) * Q:(cc % 2 + 1) * Q]
                              ).then_inc(sem["v0"], 1)
            for cc in range(DC):
                e.wait_ge(sem["t0"], T0_WBF(cc))
                e.tensor_copy(out=wbF[:, cc * Q:(cc + 1) * Q],
                              in_=pY1[:, (cc % 2) * Q:(cc % 2 + 1) * Q]
                              ).then_inc(sem["v0"], 1)

        @blk.scalar
        def _(e):
            e.wait_ge(sem["vme"], 8)
            e.wait_ge(sem["vmef"], 4)
            e.wait_ge(sem["vgrow"], 1)
            for dst, base in [(sa_c, 0), (ca1_c, 1024), (sb2_c, 3072), (cb1_c, 4096)]:
                for cc in range(DC):
                    e.dma_start(out=dst[:, cc:cc + 1],
                                in_=modrow[:, base + cc * 128:base + (cc + 1) * 128]
                                ).then_inc(sem["drow"], 16)
            for j, dst in enumerate([ga_c, gb_c]):
                for cc in range(DC):
                    e.dma_start(out=dst[:, cc:cc + 1],
                                in_=garow[:, j * D + cc * 128:j * D + (cc + 1) * 128]
                                ).then_inc(sem["drow"], 16)
            e.wait_ge(sem["v0"], 4)
            for cc in range(DC):
                for hof in range(2):
                    sl = slice(hof * 64, hof * 64 + 64)
                    e.activation(out=ws_pp[sl, cc:cc + 1], in_=negE128[sl, 2 * cc + hof:
                                                                      2 * cc + hof + 1],
                                 func=AF.Exp, scale=np.float32(Q))
                    e.activation(out=wsmT[sl, cc * NCH:(cc + 1) * NCH],
                                 in_=iot8[sl, :], func=AF.Exp,
                                 scale=e2h[sl, cc:cc + 1])
            e.wait_ge(sem["v0"], 6)
            for cc in range(DC):
                for hof in range(2):
                    sl = slice(hof * 64, hof * 64 + 64)
                    e.activation(out=combW[sl, cc * 8:(cc + 1) * 8], in_=expsB[sl, :],
                                 func=AF.Exp, scale=e2h[sl, cc:cc + 1])
            e.drain().then_inc(sem["s0"], 1)
        barrier()

        @blk.scalar
        def _(e):
            for i in range(H * 2):
                h, ph = i // 2, i % 2
                par = i % 2
                if i >= 2:
                    e.wait_ge(sem["v0"], 199 + i)
                e.activation(out=ftmpa[:, par * Q:(par + 1) * Q],
                             in_=deltf[:, 2 * Q + ph * Q:2 * Q + (ph + 1) * Q],
                             func=AF.Exp, scale=negE128[:, h:h + 1]
                             ).then_inc(sem["s0"], 1)

        @blk.vector
        def _(e):
            e.drain().then_inc(sem["v0"], 83)
            for i in range(H * 2):
                h, ph = i // 2, i % 2
                par = i % 2
                e.wait_ge(sem["s0"], 2 + i + 1)
                e.tensor_tensor(out=btmp[:, 0:Q], in0=ftmpa[:, par * Q:(par + 1) * Q],
                                in1=tri01[:, ph * Q:(ph + 1) * Q], op=ALU.mult)
                e.drain()
                e.scalar_tensor_tensor(
                    out=amask[:, h * 2 * Q + ph * Q:h * 2 * Q + (ph + 1) * Q],
                    in0=i01[:, ph * Q:(ph + 1) * Q], scalar=u128[:, h:h + 1],
                    in1=btmp[:, 0:Q], op0=ALU.mult, op1=ALU.add
                    ).then_inc(sem["v0"], 1)
            e.wait_ge(sem["drow"], 16 * 48)
            e.tensor_scalar(out=ca1_c[:, :], in0=ca1_c[:, :], scalar1=1.0,
                            scalar2=None, op0=ALU.add)
            e.tensor_scalar(out=cb1_c[:, :], in0=cb1_c[:, :], scalar1=1.0,
                            scalar2=None, op0=ALU.add)
            e.tensor_tensor(out=gabo_c[:, :], in0=ga_c[:, :], in1=bo_c[:, :],
                            op=ALU.mult)
            e.wait_ge(sem["t0"], T0_HQ)
            e.tensor_reduce(out=statf[:, 4:5], in_=pY0[:, 0:8], axis=AX.X, op=ALU.add)
            e.tensor_reduce(out=statf[:, 5:6], in_=pY0[:, 8:16], axis=AX.X, op=ALU.add)
            e.drain()
            e.tensor_scalar(out=statf[:, 0:1], in0=statf[:, 4:5], scalar1=1.0 / D,
                            scalar2=None, op0=ALU.mult)
            e.tensor_scalar(out=statf[:, 1:2], in0=statf[:, 5:6], scalar1=1.0 / D,
                            scalar2=None, op0=ALU.mult)
            e.drain()
            e.tensor_tensor(out=statf[:, 2:3], in0=statf[:, 0:1], in1=statf[:, 0:1],
                            op=ALU.mult)
            e.drain()
            e.tensor_tensor(out=statf[:, 1:2], in0=statf[:, 1:2], in1=statf[:, 2:3],
                            op=ALU.subtract)
            e.drain().then_inc(sem["v0"], 1)

        @blk.scalar
        def _(e):
            e.wait_ge(sem["v0"], 233)
            e.activation(out=statf[:, 3:4], in_=statf[:, 1:2], func=AF.Ln, bias=np.float32(EPS))
            e.drain()
            e.activation(out=statf[:, 2:3], in_=statf[:, 3:4], func=AF.Exp,
                         scale=np.float32(-0.5)).then_inc(sem["s0"], 40)

        @blk.vector
        def _(e):
            e.wait_ge(sem["s0"], 74)
            e.tensor_scalar(out=sc128[:, :], in0=chalo[:, :], scalar1=statf[:, 0:1],
                            scalar2=statf[:, 2:3], op0=ALU.subtract, op1=ALU.mult)
            e.drain()
            for cc in range(DC):
                e.tensor_scalar(out=sc128c[:, cc:cc + 1], in0=sc128[:, cc:cc + 1],
                                scalar1=ca1_c[:, cc:cc + 1], scalar2=sa_c[:, cc:cc + 1],
                                op0=ALU.mult, op1=ALU.add)
            e.drain()
            e.tensor_scalar(out=sc128c[:, :], in0=sc128c[:, :], scalar1=hvalid[:, 0:1],
                            scalar2=None, op0=ALU.mult)
            e.drain()
            for cc in range(DC):
                e.tensor_copy(out=hbuf[:, cc * HB:cc * HB + 1],
                              in_=sc128c[:, cc:cc + 1])
        if "modrow" in dbg_out:
            @blk.sync
            def _(e):
                e.wait_ge(sem["drow"], 16 * 48)
                e.dma_start(out=dbg_out["modrow"], in_=modrow[:, :]
                            ).then_inc(sem["dbgd2"], 16)
                e.wait_ge(sem["dbgd2"], 16)
        barrier()

        esP0.close()
        esP1 = contextlib.ExitStack()
        SBP1 = lambda *a, **k: _SB(esP1, *a, **k)
        wr_b = SBP1("wr_b", DC * D, BF16)
        wk_b = SBP1("wk_b", DC * D, BF16)
        wv_b = SBP1("wv_b", DC * D, BF16)
        xbf = SBP1("xbf", 2 * DC * SLAB, BF16)
        xsq = SBP1("xsq", DC * SLAB, BF16)
        xmr = SBP1("xmr", DC * SLAB, BF16)
        xmk = SBP1("xmk", DC * SLAB, BF16)
        xmv = SBP1("xmv", DC * SLAB, BF16)
        kfs = SBP1("kfs", DC * SLAB, BF16)
        rbf_sl = SBP1("rbf_sl", DC * SLAB, BF16)
        vTs = SBP1("vTs", 4 * D, BF16)
        abuf = SBP1("abuf", H * Q, BF16)
        ybf_c = SBP1("ybf_c", DC * Q, BF16)
        kwk = nc.virtual_sbuf_tensor("kwk", [128, DC * SLAB], BF16)
        nc.back_virtual_tensor_with(kwk, xsq)
        kwkT0 = nc.virtual_sbuf_tensor("kwkT0", [128, DC * SLAB], BF16)
        nc.back_virtual_tensor_with(kwkT0, xmr)
        kwkT1 = nc.virtual_sbuf_tensor("kwkT1", [128, DC * SLAB], BF16)
        nc.back_virtual_tensor_with(kwkT1, xmv)
        rwb = SBP1("rwb", DC * Q, BF16)
        for nm, t in (("wr_b", wr_b), ("wk_b", wk_b), ("wv_b", wv_b),
                      ("xbf", xbf), ("xsq", xsq), ("xmr", xmr), ("xmk", xmk),
                      ("xmv", xmv), ("kfs", kfs), ("rbf_sl", rbf_sl),
                      ("vTs", vTs), ("abuf", abuf), ("ybf_c", ybf_c),
                      ("kwk", kwk), ("kwkT0", kwkT0), ("kwkT1", kwkT1),
                      ("rwb", rwb)):
            sbt[nm] = t
        if upto in ("p1", "p3", "p4"):
            _emit_p1(blk, sem, sbt, psums, dram, upto in ("p3", "p4"))

        if upto in ("p3", "p4"):
            barrier()
            esP1.close()
            esR.close()
            xres = SB("xres", DC * (T + 2), BF16)
            m2B = SB("m2B", T + 2, BF16)
            r2B = SB("r2B", T + 2, BF16)
            w1qb = SB("w1qb", DC * HID, BF16)
            cwb = SB("cwb", HCC * 3)
            cbb = SB("cbb", HCC)
            for nm, t in (("xres", xres), ("m2B", m2B), ("r2B", r2B),
                          ("w1qb", w1qb), ("cwb", cwb), ("cbb", cbb)):
                sbt[nm] = t
            esP3 = contextlib.ExitStack()
            SBP3 = lambda *a, **k: _SB(esP3, *a, **k)
            for nm, cols, dt in (
                    ("wo_b", DC * D, BF16), ("sg", 4 * 512, F32),
                    ("sgin", 512, F32), ("sgmb", 2 * 512, BF16),
                    ("ybf3", DC * Q, BF16),
                    ("ybl", 2 * DC * Q, BF16), ("rwb3", 2 * DC * Q, BF16),
                    ("xf3", DC * Q, F32),
                    ("xh", 16, BF16),
                    ("xhsq", 16, BF16)):
                sbt[nm] = SBP3(nm, cols, dt)
            sbt["sq3"] = sbt["rwb3"]
            rg_sb = SBP3("rg_sb", D, BF16, parts=16)
            selb = SBP3("selb", 2, BF16, parts=16)
            rowpair = SBP3("rowpair", D, BF16, parts=2)
            padb = SBP3("padb", 400, BF16, parts=2)
            sbt["rg_sb"] = rg_sb; sbt["selb"] = selb
            sbt["rowpair"] = rowpair; sbt["padb"] = padb
            _emit_p3(nc, blk, sem, sbt, psums, dram, barrier, loopback)
            if upto == "p4":
                esP3.close()

        if upto == "p4":
            esP4 = contextlib.ExitStack()
            SBP4 = lambda *a, **k: _SB(esP4, *a, **k)
            for nm, cols, dt in (
                    ("udq", HCC * SLAB, BF16), ("h2w", DC * SLAB, BF16),
                    ("w2s", 2 * D, BF16),
                    ("sc13", 8 * SLAB, BF16),
                    ("scr2", 2 * SLAB, F32),
                    ("oout", 4 * SLAB, F32),
                    ("s1bb", SLAB, BF16), ("uib", SLAB, BF16)):
                sbt[nm] = SBP4(nm, cols, dt)
            sbt["s2bb"] = sbt["s1bb"]
            sbt["ds2b"] = sbt["uib"]
            sbt["xq1w"] = sbt["h2w"]
            sbt["qt"] = sbt["wqi"]
            sbt["mx4"] = sbt["btmp2"]
            sbt["mn4"] = sbt["btmp"]
            _emit_p4(nc, blk, sem, sbt, psums, dram, barrier)

        finals = dict(vstate=8 * NCH, sy=4 * NCH, sr=8 * NSLAB, sk=8 * NSLAB,
                      vvT=8 * NSLAB, vA=8 * NCH, vkwkTe=2 * NCH, ty=8 * NCH,
                      vh=8 * NSLAB, dyo=32 * NCH, dro=32 * NCH)
        p3_finals = dict(dsl=64, dsg=128, vcmb=3, vsc3=64, dld=512,
                         ty2=64, vy2=64, to=64, so3=64, vxr=64, dr2=256,
                         drg=64, tsel=1, vrow=1, dmh=256, gq3=32, t2s=4,
                         v2s=4, s2r=4, t2h=1, v2h=2, s2h=1, dws=2240,
                         sws=128, vwsc=4, dwq=2048, vq1=128, gq2=64,
                         dw2o=2048, tbc=2)
        if not loopback:
            p3_finals.update(ccs=1, ccr=1)

        @blk.gpsimd
        def _(e):
            if upto in ("p1", "p3", "p4"):
                for s_, v_ in finals.items():
                    e.wait_ge(sem[s_], v_)
                for s_, v_ in (("dcst", 256), ("dwgt", 448), ("dwmod", 512),
                               ("dwmf", 256), ("dxb", 512), ("drow", 768)):
                    e.wait_ge(sem[s_], v_)
            if upto in ("p3", "p4"):
                for s_, v_ in p3_finals.items():
                    e.wait_ge(sem[s_], v_)

        if dbg_out or "y_d" in dbg or "rw_d" in dbg:
            @blk.gpsimd
            def _(e):
                n = 0
                for nm, dst in dbg_out.items():
                    if nm == "modrow":
                        continue
                    e.dma_start(out=dst, in_=sbt[nm][:, :]).then_inc(sem["dbgd"], 16)
                    n += 1
                if n:
                    e.wait_ge(sem["dbgd"], 16 * n)
        if upto == "p1":
            esP1.close()
            esR.close()
        elif upto == "p3":
            esP3.close()
        elif upto == "p4":
            esP4.close()
    return nc


def _emit_p1(blk, sem, sbt, psums, dram, do_stats=False):
    g = lambda n: sbt[n]
    xbf, xsq = g("xbf"), g("xsq")
    mB, mF, rstdB = g("mB"), g("mF"), g("rstdB")
    statf2, statf3 = g("statf2"), g("statf3")
    hbuf, btmp, btmp2 = g("hbuf"), g("btmp"), g("btmp2")
    xmr, xmk, xmv = g("xmr"), g("xmk"), g("xmv")
    kfs, rbf_sl, vTs = g("kfs"), g("rbf_sl"), g("vTs")
    kwk, kwkT0, kwkT1 = g("kwk"), g("kwkT0"), g("kwkT1")
    abuf, rwb, ybf_c = g("abuf"), g("rwb"), g("ybf_c")
    sst, sbfT = g("sst"), g("sbfT")
    allones, identb = g("allones"), g("identb")
    wr_b, wk_b, wv_b = g("wr_b"), g("wk_b"), g("wv_b")
    wkF, wbF, amask = g("wkF"), g("wbF"), g("amask")
    maa_r, maa_k, maa_v = g("maa_rs"), g("maa_ks"), g("maa_vs")
    br_c, bk_c, bvF = g("br_s"), g("bk_s"), g("bvF")
    ca1_c, sa_c, ws_pp = g("ca1_c"), g("sa_c"), g("ws_pp")
    pP0, pP1, pA, pY0, pY1 = (psums[k] for k in ("pP0", "pP1", "pA", "pY0", "pY1"))
    vDS = psums["vDS"]
    x_fm, y_d, rw_d = dram["x_fm"], dram["y_d"], dram["rw_d"]

    @blk.gpsimd
    def _(e):
        for s in range(2):
            for cc in range(DC):
                e.dma_start(out=xbf[:, (s % 2) * DC * SLAB + cc * SLAB:
                                    (s % 2) * DC * SLAB + (cc + 1) * SLAB],
                            in_=x_fm[cc, :, s * SLAB:(s + 1) * SLAB]
                            ).then_inc(sem["dxb"], 16)
        for wsb, wnm in ((wr_b, "wrT"), (wk_b, "wkT"), (wv_b, "wvT")):
            for cc in range(DC):
                e.dma_start(out=wsb[:, cc * D:(cc + 1) * D],
                            in_=dram[wnm][cc]).then_inc(sem["dwgt"], 16)
        for s in range(NSLAB):
            if s + 2 < NSLAB:
                sn = s + 2
                e.wait_ge(sem["vh"], 8 * (s + 1))
                e.wait_ge(sem["tstat"], s + 1)
                for cc in range(DC):
                    e.dma_start(out=xbf[:, (sn % 2) * DC * SLAB + cc * SLAB:
                                        (sn % 2) * DC * SLAB + (cc + 1) * SLAB],
                                in_=x_fm[cc, :, sn * SLAB:(sn + 1) * SLAB]
                                ).then_inc(sem["dxb"], 16)
        if do_stats:
            for k in range(128):
                if k >= 4:
                    e.wait_ge(sem["sws"], k - 3)
                wt = _w1tile(dram, k) if k < 64 else _w2tile(dram, k - 64)
                e.dma_start(out=g("wstat")[:, (k % 4) * 512:(k % 4 + 1) * 512],
                            in_=wt).then_inc(sem["dws"], 16)

    @blk.sync
    def _(e):
        for c in range(NCH):
            e.wait_ge(sem["sy"], 4 * c + 4)
            e.dma_start(out=y_d[:, c], in_=ybf_c[:, :]).then_inc(sem["dyo"], 16)
            e.wait_ge(sem["vrw"], 8 * c + 8)
            e.dma_start(out=rw_d[:, c], in_=rwb[:, :]).then_inc(sem["dro"], 16)
            e.dma_start(out=g("wabs")[0:1, 0:16],
                        in_=ybf_c[0:1, 0:16]).then_inc(sem["dyo"], 16)
            e.dma_start(out=g("wabs")[0:1, 16:32],
                        in_=rwb[0:1, 0:16]).then_inc(sem["dro"], 16)

    @blk.tensor
    def _(e):
        e.wait_ge(sem["dwgt"], 192)
        for s in range(NSLAB):
            par = s % 2
            xo = par * DC * SLAB
            if s > 0:
                e.wait_ge(sem["vstate"], 8 * 2 * s)
                e.wait_ge(sem["sk"], 8 * s)
            e.wait_ge(sem["dxb"], 16 * DC * (s + 1))
            for cc in range(DC):
                mm = e.matmul(pP0[:, 0:SLAB], allones[:, :],
                              xbf[:, xo + cc * SLAB:xo + (cc + 1) * SLAB],
                              start=(cc == 0), stop=(cc == DC - 1))
            e.wait_ge(sem["vsq"], DC * s + DC)
            for cc in range(DC):
                mm = e.matmul(pP1[:, 0:SLAB], allones[:, :],
                              xsq[:, cc * SLAB:(cc + 1) * SLAB],
                              start=(cc == 0), stop=(cc == DC - 1))
            mm.then_inc(sem["tstat"], 1)
            e.wait_ge(sem["vstat"], s + 1)
            if s > 0:
                e.wait_ge(sem["vrw"], 16 * s)
                e.wait_ge(sem["tA"], 16 * s)
            for co in range(DC):
                bank = (pP0, pP1)[co % 2]
                if co >= 2:
                    e.wait_ge(sem["sr"], 8 * s + co - 1)
                for ci in range(DC):
                    if co == 0:
                        e.wait_ge(sem["vmix"], 16 * s + 2 * ci + 1)
                    mm = e.matmul(bank[:, 0:SLAB],
                                  wr_b[:, ci * D + co * 128:ci * D + (co + 1) * 128],
                                  xmr[:, ci * SLAB:(ci + 1) * SLAB],
                                  start=(ci == 0), stop=(ci == DC - 1))
                mm.then_inc(sem["tr"], 1)
            e.wait_ge(sem["dwgt"], 320)
            e.wait_ge(sem["sr"], 8 * s + 8)
            if s > 0:
                e.wait_ge(sem["vkwk"], 16 * s)
            for co in range(DC):
                bank = (pP0, pP1)[co % 2]
                if co >= 2:
                    e.wait_ge(sem["sk"], 8 * s + co - 1)
                for ci in range(DC):
                    if co == 0:
                        e.wait_ge(sem["vmix"], 16 * s + 2 * ci + 2)
                    mm = e.matmul(bank[:, 0:SLAB],
                                  wk_b[:, ci * D + co * 128:ci * D + (co + 1) * 128],
                                  xmk[:, ci * SLAB:(ci + 1) * SLAB],
                                  start=(ci == 0), stop=(ci == DC - 1))
                mm.then_inc(sem["tk"], 1)
            e.wait_ge(sem["dwgt"], 448)
            e.wait_ge(sem["vA"], 8 * 2 * s)
            for i in range(8):
                tsub, seg = i // 2, i % 2
                qo = (i % 4) * 512
                if i >= 4:
                    e.wait_ge(sem["vvT"], 8 * s + i - 3)
                for ci in range(DC):
                    if i == 0:
                        e.wait_ge(sem["gmix"], 8 * s + ci + 1)
                    mm = e.matmul(pA[:, qo:qo + 512],
                                  xmv[:, ci * SLAB + tsub * 128:
                                      ci * SLAB + (tsub + 1) * 128],
                                  wv_b[:, ci * D + seg * 512:ci * D + (seg + 1) * 512],
                                  start=(ci == 0), stop=(ci == DC - 1))
                mm.then_inc(sem["tv"], 1)
            for sc in range(2):
                c = 2 * s + sc
                e.wait_ge(sem["vvT"], 8 * s + 8)
                if c > 0:
                    e.wait_ge(sem["vA"], 8 * c)
                for cc in range(DC):
                    e.wait_ge(sem["vkwk"], 8 * c + cc + 1)
                    mm = e.matmul(pA[:, cc * 128:(cc + 1) * 128],
                                  kwk[:, cc * Q:cc * Q + 128], identb[:, :],
                                  start=True, stop=True)
                mm.then_inc(sem["tkwkT"], 1)
                for cc in range(DC):
                    mm = e.matmul(pA[:, 1024 + cc * 128:1024 + (cc + 1) * 128],
                                  kwk[:, cc * Q + 128:(cc + 1) * Q], identb[:, :],
                                  start=True, stop=True)
                mm.then_inc(sem["tkwkT"], 1)
                e.wait_ge(sem["vkwkTe"], 2 * c + 1)
                e.wait_ge(sem["vvT"], 8 * s + 8)
                e.wait_ge(sem["vstate"], 8 * c)
                e.wait_ge(sem["sk"], 8 * s + 8)
                for cc in range(DC):
                    for hof in range(2):
                        h = 2 * cc + hof
                        for qh in range(2):
                            if cc == 0 and hof == 0 and qh == 1:
                                e.wait_ge(sem["vkwkTe"], 2 * c + 2)
                            kT = (kwkT0, kwkT1)[qh]
                            tsub = sc * 2 + qh
                            mm = e.matmul(
                                vDS[hof * 64:hof * 64 + 64, cc * 64:(cc + 1) * 64],
                                kT[:, cc * 128 + hof * 64:cc * 128 + hof * 64 + 64],
                                vTs[:, tsub * D + h * NHD:tsub * D + (h + 1) * NHD],
                                start=(qh == 0), stop=(qh == 1),
                                tile_position=(0, hof * 64))
                    mm.then_inc(sem["tds"], 1)
                def emit_A(w):
                    if w >= 2:
                        e.wait_ge(sem["vA"], 8 * c + w - 1)
                    half = (w % 2) * 1024
                    for hof in range(2):
                        base = half + hof * 512
                        for ph in range(2):
                            mm = e.matmul(
                                pA[:, base + ph * Q:base + (ph + 1) * Q],
                                kfs[hof * 64:hof * 64 + 64,
                                    w * SLAB + sc * Q + ph * 128:
                                    w * SLAB + sc * Q + (ph + 1) * 128],
                                rbf_sl[hof * 64:hof * 64 + 64,
                                       w * SLAB + sc * Q:w * SLAB + (sc + 1) * Q],
                                start=True, stop=True, tile_position=(hof * 64, 0))
                    mm.then_inc(sem["tA"], 1)

                def emit_y(p):
                    gi = p // 2
                    bank = (pY0, pY1)[gi % 2]
                    colb = (p % 2) * Q
                    e.wait_ge(sem["vA"], 8 * c + p + 1)
                    e.wait_ge(sem["vrw"], 8 * c + p + 1)
                    if gi >= 2:
                        e.wait_ge(sem["sy"], 4 * c + gi - 1)
                    for hof in range(2):
                        h = 2 * p + hof
                        for ph in range(2):
                            tsub = sc * 2 + ph
                            e.matmul(bank[hof * 64:hof * 64 + 64, colb:colb + Q],
                                     vTs[:, tsub * D + h * NHD:tsub * D + (h + 1) * NHD],
                                     abuf[:, (p % 4) * 1024 + hof * 2 * Q + ph * Q:
                                          (p % 4) * 1024 + hof * 2 * Q + (ph + 1) * Q],
                                     start=(ph == 0), stop=False,
                                     tile_position=(0, hof * 64))
                        mm = e.matmul(bank[hof * 64:hof * 64 + 64, colb:colb + Q],
                                      sbfT[hof * 64:hof * 64 + 64, p * NHD:(p + 1) * NHD],
                                      rwb[hof * 64:hof * 64 + 64, p * Q:(p + 1) * Q],
                                      start=False, stop=True,
                                      tile_position=(hof * 64, hof * 64))
                    mm.then_inc(sem["ty"], 1)

                e.wait_ge(sem["sy"], 4 * c)
                emit_A(0)
                emit_A(1)
                emit_A(2)
                for p in range(DC):
                    if p + 3 < DC:
                        emit_A(p + 3)
                    emit_y(p)
        if do_stats:
            e.wait_ge(sem["vA"], 8 * NCH)
            e.wait_ge(sem["vwsc"], 1)
            e.matmul(pA[:, 0:1], g("allonesF")[:, :], g("s128")[:, 0:1],
                     start=True, stop=True).then_inc(sem["tbc"], 1)

    @blk.vector
    def _(e):
        for s in range(NSLAB):
            par = s % 2
            xo = par * DC * SLAB
            e.wait_ge(sem["dxb"], 16 * DC * (s + 1))
            if s > 0:
                e.wait_ge(sem["tkwkT"], 4 * s)
            e.tensor_tensor(out=xsq[:, :],
                            in0=xbf[:, xo:xo + DC * SLAB],
                            in1=xbf[:, xo:xo + DC * SLAB],
                            op=ALU.mult).then_inc(sem["vsq"], DC)
            e.wait_ge(sem["tstat"], s + 1)
            e.tensor_scalar(out=mF[:, :], in0=pP0[:, 0:SLAB], scalar1=1.0 / D,
                            scalar2=None, op0=ALU.mult)
            e.tensor_copy(out=mB[:, :], in_=mF[:, :])
            e.tensor_scalar(out=statf2[:, :], in0=pP1[:, 0:SLAB], scalar1=1.0 / D,
                            scalar2=None, op0=ALU.mult)
            e.tensor_tensor(out=statf3[:, :], in0=mF[:, :], in1=mF[:, :], op=ALU.mult)
            e.tensor_tensor(out=statf2[:, :], in0=statf2[:, :], in1=statf3[:, :],
                            op=ALU.subtract).then_inc(sem["vstat"], 1)
            e.wait_ge(sem["s0"], 74 + s + 1)
            if s > 0:
                e.wait_ge(sem["gmix"], 8 * s)
            for cc in range(DC):
                hsl = slice(cc * HB + 1, cc * HB + 1 + SLAB)
                e.tensor_tensor(out=btmp[:, :],
                                in0=xbf[:, xo + cc * SLAB:xo + (cc + 1) * SLAB],
                                in1=mB[:, :], op=ALU.subtract)
                e.tensor_tensor(out=btmp2[:, :], in0=btmp[:, :], in1=rstdB[:, :],
                                op=ALU.mult)
                e.tensor_scalar(out=btmp2[:, :], in0=btmp2[:, :],
                                scalar1=ca1_c[:, cc:cc + 1], scalar2=None,
                                op0=ALU.mult)
                e.tensor_scalar(out=hbuf[:, hsl], in0=btmp2[:, :],
                                scalar1=sa_c[:, cc:cc + 1], scalar2=None,
                                op0=ALU.add).then_inc(sem["vh"], 1)
            if s > 0:
                e.wait_ge(sem["tr"], 8 * s)
                e.wait_ge(sem["tk"], 8 * s)
                e.wait_ge(sem["tds"], 16 * s)
            for cc in range(DC):
                e.tensor_tensor(out=btmp[:, :],
                                in0=hbuf[:, cc * HB:cc * HB + SLAB],
                                in1=hbuf[:, cc * HB + 1:cc * HB + 1 + SLAB],
                                op=ALU.subtract)
                for dst, mvec in ((xmr, maa_r), (xmk, maa_k)):
                    e.scalar_tensor_tensor(out=dst[:, cc * SLAB:(cc + 1) * SLAB],
                                           in0=btmp[:, :],
                                           scalar=mvec[:, cc:cc + 1],
                                           in1=hbuf[:, cc * HB + 1:
                                                   cc * HB + 1 + SLAB],
                                           op0=ALU.mult,
                                           op1=ALU.add).then_inc(sem["vmix"], 1)
            if s >= 1:
                e.wait_ge(sem["tv"], 8 * s)
            for cc in range(DC):
                e.tensor_tensor(out=btmp[:, :],
                                in0=hbuf[:, cc * HB:cc * HB + SLAB],
                                in1=hbuf[:, cc * HB + 1:cc * HB + 1 + SLAB],
                                op=ALU.subtract)
                e.scalar_tensor_tensor(out=xmv[:, cc * SLAB:(cc + 1) * SLAB],
                                       in0=btmp[:, :],
                                       scalar=maa_v[:, cc:cc + 1],
                                       in1=hbuf[:, cc * HB + 1:
                                               cc * HB + 1 + SLAB],
                                       op0=ALU.mult,
                                       op1=ALU.add).then_inc(sem["gmix"], 1)
            if s > 0:
                e.wait_ge(sem["tds"], 16 * s)
                e.wait_ge(sem["ty"], 16 * s)
            for i in range(8):
                tsub, seg = i // 2, i % 2
                qo = (i % 4) * 512
                e.wait_ge(sem["tv"], 8 * s + i + 1)
                e.tensor_tensor(out=vTs[:, tsub * D + seg * 512:
                                        tsub * D + (seg + 1) * 512],
                                in0=pA[:, qo:qo + 512],
                                in1=bvF[:, seg * 512:(seg + 1) * 512],
                                op=ALU.add).then_inc(sem["vvT"], 1)
            for sc in range(2):
                c = 2 * s + sc
                for cc in range(DC):
                    e.wait_ge(sem["sk"], 8 * s + cc + 1)
                    e.tensor_tensor(out=kwk[:, cc * Q:(cc + 1) * Q],
                                    in0=kfs[:, cc * SLAB + sc * Q:
                                            cc * SLAB + (sc + 1) * Q],
                                    in1=wkF[:, cc * Q:(cc + 1) * Q],
                                    op=ALU.mult).then_inc(sem["vkwk"], 1)
                e.wait_ge(sem["tkwkT"], 2 * c + 1)
                e.tensor_copy(out=kwkT0[:, 0:DC * 128], in_=pA[:, 0:1024]).then_inc(sem["vkwkTe"], 1)
                e.wait_ge(sem["tkwkT"], 2 * c + 2)
                e.tensor_copy(out=kwkT1[:, 0:DC * 128], in_=pA[:, 1024:2048]).then_inc(sem["vkwkTe"], 1)
                if c > 0:
                    e.wait_ge(sem["ty"], 8 * c)
                    e.wait_ge(sem["dro"], 32 * c)
                for cc in range(DC):
                    e.wait_ge(sem["sr"], 8 * s + cc + 1)
                    e.tensor_tensor(out=rwb[:, cc * Q:(cc + 1) * Q],
                                    in0=rbf_sl[:, cc * SLAB + sc * Q:
                                               cc * SLAB + (sc + 1) * Q],
                                    in1=wbF[:, cc * Q:(cc + 1) * Q],
                                    op=ALU.mult).then_inc(sem["vrw"], 1)
                for pw in range(4):
                    e.wait_ge(sem["tA"], 8 * c + 2 * pw + 2)
                    if pw >= 2:
                        e.wait_ge(sem["ty"], 8 * c + 2 * pw - 2)
                    sl4 = ((2 * pw) % 4) * 1024
                    e.tensor_tensor(out=abuf[:, sl4:sl4 + 2048],
                                    in0=pA[:, 0:2048],
                                    in1=amask[:, 2 * pw * 1024:(2 * pw + 2) * 1024],
                                    op=ALU.mult).then_inc(sem["vA"], 2)
                for cc in range(DC):
                    e.wait_ge(sem["tds"], 8 * c + cc + 1)
                    e.wait_ge(sem["ty"], 8 * c + cc + 1)
                    e.scalar_tensor_tensor(out=sbfT[:, cc * NHD:(cc + 1) * NHD],
                                           in0=sst[:, cc * NHD:(cc + 1) * NHD],
                                           scalar=ws_pp[:, cc:cc + 1],
                                           in1=vDS[:, cc * NHD:(cc + 1) * NHD],
                                           op0=ALU.mult, op1=ALU.add)
                    e.scalar_tensor_tensor(out=sst[:, cc * NHD:(cc + 1) * NHD],
                                           in0=sst[:, cc * NHD:(cc + 1) * NHD],
                                           scalar=ws_pp[:, cc:cc + 1],
                                           in1=vDS[:, cc * NHD:(cc + 1) * NHD],
                                           op0=ALU.mult, op1=ALU.add
                                           ).then_inc(sem["vstate"], 1)
            e.wait_ge(sem["gmix"], 8 * s + 8)
            for cc in range(DC):
                e.tensor_copy(out=hbuf[:, cc * HB:cc * HB + 1],
                              in_=hbuf[:, cc * HB + SLAB:cc * HB + SLAB + 1])
        if do_stats:
            e.memset(g("allonesF")[:, :], 1.0)
            e.wait_ge(sem["sws"], 64)
            e.tensor_reduce(out=g("s128")[:, 0:1], in_=g("wacc")[:, 0:64],
                            axis=AX.X, op=ALU.add)
            e.drain().then_inc(sem["vwsc"], 1)
            e.wait_ge(sem["tbc"], 1)
            e.tensor_scalar(out=g("swc")[:, 0:1], in0=pA[:, 0:1],
                            scalar1=1.0 / (D * HID), scalar2=1e-5,
                            op0=ALU.mult, op1=ALU.max)
            e.drain()
            e.reciprocal(out=g("wscl")[:, 0:1], in_=g("swc")[:, 0:1])
            e.drain()
            e.tensor_scalar(out=g("sw127")[:, 0:1], in0=g("swc")[:, 0:1],
                            scalar1=1.0 / 127.0, scalar2=None, op0=ALU.mult)
            e.drain().then_inc(sem["vwsc"], 1)

    @blk.scalar
    def _(e):
        for s in range(NSLAB):
            e.wait_ge(sem["vstat"], s + 1)
            e.activation(out=statf3[:, :], in_=statf2[:, :], func=AF.Ln, bias=np.float32(EPS))
            e.drain()
            e.activation(out=rstdB[:, :], in_=statf3[:, :], func=AF.Exp,
                         scale=np.float32(-0.5)).then_inc(sem["s0"], 1)
            if s > 0:
                e.wait_ge(sem["tA"], 16 * s)
                e.wait_ge(sem["vrw"], 16 * s)
            for co in range(DC):
                bank = (pP0, pP1)[co % 2]
                e.wait_ge(sem["tr"], 8 * s + co + 1)
                e.activation(out=rbf_sl[:, co * SLAB:(co + 1) * SLAB],
                             in_=bank[:, 0:SLAB], func=AF.Identity,
                             bias=br_c[:, co:co + 1]).then_inc(sem["sr"], 1)
            if s > 0:
                e.wait_ge(sem["vkwk"], 16 * s)
            for co in range(DC):
                bank = (pP0, pP1)[co % 2]
                e.wait_ge(sem["tk"], 8 * s + co + 1)
                e.activation(out=kfs[:, co * SLAB:(co + 1) * SLAB],
                             in_=bank[:, 0:SLAB], func=AF.Identity,
                             bias=bk_c[:, co:co + 1]).then_inc(sem["sk"], 1)
            for sc in range(2):
                c = 2 * s + sc
                for gi in range(4):
                    bank = (pY0, pY1)[gi % 2]
                    e.wait_ge(sem["ty"], 8 * c + 2 * gi + 2)
                    if c > 0:
                        e.wait_ge(sem["dyo"], 32 * c)
                    for j in range(2):
                        p = 2 * gi + j
                        cp = e.activation(out=ybf_c[:, p * Q:(p + 1) * Q],
                                          in_=bank[:, j * Q:(j + 1) * Q],
                                          func=AF.Identity)
                    cp.then_inc(sem["sy"], 1)
            if do_stats and s >= 1:
                for k in range(32 * (s - 1), 32 * s if s < NSLAB - 1 else 128):
                    e.wait_ge(sem["dws"], 16 * (k + 1))
                    e.activation(out=g("wabs")[:, :],
                                 in_=g("wstat")[:, (k % 4) * 512:(k % 4 + 1) * 512],
                                 func=AF.Abs, accum_out=g("wacc")[:, k:k + 1])
                    e.drain().then_inc(sem["sws"], 1)



RG8 = [[0, 1, 2, 3, 4, 5, 6, 7]]


def _w1tile(dram, k):
    return dram["w1T"][k // 8][:, (k % 8) * 512:(k % 8 + 1) * 512]


def _w2tile(dram, k):
    return dram["w2T"][k // 2][:, (k % 2) * 512:(k % 2 + 1) * 512]


def _emit_p3(nc, blk, sem, sbt, psums, dram, barrier, loopback):
    g = lambda n: sbt[n]
    xres, m2B, r2B = g("xres"), g("m2B"), g("r2B")
    w1qb, swc, wscl, sw127 = g("w1qb"), g("swc"), g("wscl"), g("sw127")
    wo_b, sg, sgin, sgmb = g("wo_b"), g("sg"), g("sgin"), g("sgmb")
    cwb, cbb = g("cwb"), g("cbb")
    ybf3, ybl, rwb3, xf3 = g("ybf3"), g("ybl"), g("rwb3"), g("xf3")
    sq3, xh, xhsq = g("sq3"), g("xh"), g("xhsq")
    wstat, wacc, s128 = g("wstat"), g("wacc"), g("s128")
    mB, rstdB = g("mB"), g("rstdB")
    wqi, wqi2 = g("wqi"), g("wqi2")
    rg_sb, selb, rowpair, padb = g("rg_sb"), g("selb"), g("rowpair"), g("padb")
    sst = g("sst")
    allones = g("allones")
    combW, wsmT = g("combW"), g("wsmT")
    ga_c, gabo_c = g("ga_c"), g("gabo_c")
    statf, statf2, statf3, mF = g("statf"), g("statf2"), g("statf3"), g("mF")
    allonesF = g("allonesF")
    pP0, pP1, pA, pY0, pY1 = (psums[k] for k in ("pP0", "pP1", "pA", "pY0", "pY1"))
    y_d, rw_d, x_fm = dram["y_d"], dram["rw_d"], dram["x_fm"]
    w1T, w2T, w2q_d, woT = dram["w1T"], dram["w2T"], dram["w2q_d"], dram["woT"]
    TP = T + 2

    def w1tile(k):
        return w1T[k // 8][:, (k % 8) * 512:(k % 8 + 1) * 512]

    def w2tile(k):
        return w2T[k // 2][:, (k % 2) * 512:(k % 2 + 1) * 512]

    @blk.sync
    def _(e):
        e.dma_start(out=dram["sloc"], in_=sst[:, :]).then_inc(sem["dsl"], 16)
        for i in range(3):
            e.dma_start(out=xf3[0:2, i * 8:i * 8 + 8],
                        in_=sst[0:2, 0:8]).then_inc(sem["dsl"], 16)
        if not loopback:
            e.wait_ge(sem["ccs"], 1)
            for j in range(4):
                e.dma_start(out=sg[:, j * 512:(j + 1) * 512],
                            in_=dram["sglob"][j * 128:(j + 1) * 128, :]
                            ).then_inc(sem["dsg"], 16)
            e.wait_ge(sem["vcmb"], 1)
            for j in range(4, NCORE):
                e.dma_start(out=sg[:, (j - 4) * 512:(j - 3) * 512],
                            in_=dram["sglob"][j * 128:(j + 1) * 128, :]
                            ).then_inc(sem["dsg"], 16)
        for m in range(NCH):
            bb = m % 2
            if m >= 2:
                e.wait_ge(sem["vy2"], 8 * (m - 1))
                e.wait_ge(sem["ty2"], 8 * (m - 1))
                e.wait_ge(sem["vxr"], 8 * (m - 1))
            e.dma_start(out=ybl[:, bb * 2048:bb * 2048 + 2048],
                        in_=y_d[:, m]).then_inc(sem["dld"], 16)
            e.dma_start(out=rwb3[:, bb * 2048:bb * 2048 + 2048],
                        in_=rw_d[:, m]).then_inc(sem["dld"], 16)
            if m > 0:
                e.wait_ge(sem["vxr"], 8 * m)
            e.dma_start(out=xf3[:, :],
                        in_=dram["x_cm"][:, m]).then_inc(sem["dld"], 16)
            e.dma_start(out=padb[0:1, 0:16],
                        in_=dram["y_d"][0:1, m, 0:16]).then_inc(sem["dld"], 16)
        if not loopback:
            e.wait_ge(sem["ccr"], 1)
            e.dma_start(out=rg_sb[:, :], in_=dram["rowsg"]).then_inc(sem["drg"], 16)
            for i in range(3):
                e.dma_start(out=padb[0:2, i * 128:i * 128 + 128],
                            in_=dram["rowsg"][0:2, 0:128]).then_inc(sem["drg"], 16)

    @blk.gpsimd
    def _(e):
        for ci in range(DC):
            e.dma_start(out=wo_b[:, ci * D:(ci + 1) * D],
                        in_=woT[ci]).then_inc(sem["dws"], 16)
        e.dma_start(out=selb[:, :], in_=dram["selmat"]).then_inc(sem["dws"], 16)
        e.dma_start(out=cwb[:, :], in_=dram["cw_c"]).then_inc(sem["dws"], 16)
        e.dma_start(out=cbb[:, :], in_=dram["cb_c"]).then_inc(sem["dws"], 16)
        e.dma_start(out=padb[0:2, 384:386],
                    in_=dram["selmat"][0:2, 0:2]).then_inc(sem["dws"], 16)
        if not loopback:
            e.wait_ge(sem["dsl"], 64)
            e.collective_compute("AllGather", ALU.bypass, replica_groups=RG8,
                                 ins=[dram["sloc"]], outs=[dram["sglob"]]
                                 ).then_inc(sem["ccs"], 1)
        e.wait_ge(sem["vwsc"], 2)
        for k in range(64):
            if k >= 4:
                e.wait_ge(sem["vq1"], k - 3)
            e.dma_start(out=wstat[:, (k % 4) * 512:(k % 4 + 1) * 512],
                        in_=w1tile(k)).then_inc(sem["dwq"], 16)
        e.wait_ge(sem["vwsc"], 4)
        e.wait_ge(sem["vq1"], 64)

        def w2out(e, ci):
            e.wait_ge(sem["gq2"], 2 * (ci + 1))
            e.dma_start(out=w2q_d[ci][:, 0:512],
                        in_=statf2[:, :]).then_inc(sem["dw2o"], 16)
            e.dma_start(out=w2q_d[ci][:, 512:1024],
                        in_=statf3[:, :]).then_inc(sem["dw2o"], 16)
            e.dma_start(out=rstdB[0:1, 0:16],
                        in_=statf2[0:1, 0:16]).then_inc(sem["dw2o"], 16)
            e.dma_start(out=rstdB[1:2, 16:32],
                        in_=statf2[0:1, 16:32]).then_inc(sem["dw2o"], 16)

        for k in range(64):
            if k >= 4:
                e.wait_ge(sem["vq1"], 64 + k - 3)
            e.dma_start(out=wstat[:, (k % 4) * 512:(k % 4 + 1) * 512],
                        in_=w2tile(k)).then_inc(sem["dwq"], 16)
            if k % 2 == 1 and k // 2 >= 1:
                w2out(e, k // 2 - 1)
        w2out(e, HCC - 1)
        if not loopback:
            e.wait_ge(sem["dr2"], 16 * 16)
            e.collective_compute("AllGather", ALU.bypass, replica_groups=RG8,
                                 ins=[dram["rows2"]], outs=[dram["rowsg"]]
                                 ).then_inc(sem["ccr"], 1)

    @blk.tensor
    def _(e):
        for m in range(NCH):
            bb = m % 2
            if m == 4:
                e.wait_ge(sem["vwsc"], 3)
                e.matmul(pA[:, 16:17], allonesF[:, :], s128[:, 1:2],
                         start=True, stop=True).then_inc(sem["tbc"], 1)
            e.wait_ge(sem["dld"], 16 * (4 * m + 4))
            if m > 0:
                e.wait_ge(sem["vy2"], 8 * m)
            for cc in range(DC):
                e.wait_ge(sem["vsc3"], 8 * m + cc + 1)
                bank = (pP0, pP1, pY0, pY1)[cc // 2]
                colb = (cc % 2) * Q
                for hof in range(2):
                    mm = e.matmul(bank[hof * 64:hof * 64 + 64, colb:colb + Q],
                                  sgmb[hof * 64:hof * 64 + 64,
                                       bb * 512 + cc * NHD:
                                       bb * 512 + (cc + 1) * NHD],
                                  rwb3[hof * 64:hof * 64 + 64,
                                       bb * 2048 + cc * Q:bb * 2048 + (cc + 1) * Q],
                                  start=True, stop=True,
                                  tile_position=(hof * 64, hof * 64))
                mm.then_inc(sem["ty2"], 1)
            if m == 0:
                e.wait_ge(sem["vwsc"], 2)
                e.wait_ge(sem["dws"], 2048 + 144)
            else:
                if m == 4:
                    e.wait_ge(sem["vwsc"], 4)
                e.wait_ge(sem["vxr"], 8 * m)
            for co in range(DC):
                for ci in range(DC):
                    if co == 0:
                        e.wait_ge(sem["vy2"], 8 * m + ci + 1)
                    mm = e.matmul(pA[:, co * Q:(co + 1) * Q],
                                  wo_b[:, ci * D + co * 128:ci * D + (co + 1) * 128],
                                  ybf3[:, ci * Q:(ci + 1) * Q],
                                  start=(ci == 0), stop=(ci == DC - 1))
                mm.then_inc(sem["to"], 1)
        for grp in range(4):
            e.wait_ge(sem["vxr"], 16 * (grp + 1))
            if grp == 0:
                e.wait_ge(sem["vy2"], 8 * NCH)
            else:
                e.wait_ge(sem["v2s"], grp)
            for cc in range(DC):
                mm = e.matmul(pP0[:, 0:SLAB], allones[:, :],
                              xres[:, cc * TP + 1 + grp * SLAB:
                                   cc * TP + 1 + (grp + 1) * SLAB],
                              start=(cc == 0), stop=(cc == DC - 1))
            e.wait_ge(sem["gq3"], 8 * (grp + 1))
            for cc in range(DC):
                mm = e.matmul(pP1[:, 0:SLAB], allones[:, :],
                              sq3[:, cc * SLAB:(cc + 1) * SLAB],
                              start=(cc == 0), stop=(cc == DC - 1))
            mm.then_inc(sem["t2s"], 1)
        e.wait_ge(sem["drg"], 64)
        e.wait_ge(sem["dws"], 2048 + 160)
        e.matmul(pY0[0:2, 0:512], selb[:, :], rg_sb[:, 0:512],
                 start=True, stop=True)
        e.matmul(pY1[0:2, 0:512], selb[:, :], rg_sb[:, 512:1024],
                 start=True, stop=True).then_inc(sem["tsel"], 1)
        e.wait_ge(sem["v2h"], 1)
        e.wait_ge(sem["v2s"], 4)
        for cc in range(DC):
            mm = e.matmul(pP0[:, 0:2], allones[:, :], xh[:, 2 * cc:2 * cc + 2],
                          start=(cc == 0), stop=(cc == DC - 1))
        for cc in range(DC):
            mm = e.matmul(pP1[:, 0:2], allones[:, :], xhsq[:, 2 * cc:2 * cc + 2],
                          start=(cc == 0), stop=(cc == DC - 1))
        mm.then_inc(sem["t2h"], 1)


    @blk.vector
    def _(e):
        if loopback:
            e.memset(sg[:, :], 0.0)
            e.drain().then_inc(sem["dsg"], 128)
            e.memset(rg_sb[:, :], 0.0)
            e.drain().then_inc(sem["drg"], 64)
        def quant_batch(m, half=None):
            rng = range(16) if half is None else range(8 * half, 8 * half + 8)
            for i in rng:
                k = 16 * m + i
                if k < 64:
                    ci, seg = k // 8, k % 8
                    e.wait_ge(sem["dwq"], 16 * (k + 1))
                    e.tensor_scalar(out=wqi[:, :],
                                    in0=wstat[:, (k % 4) * 512:(k % 4 + 1) * 512],
                                    scalar1=wscl[:, 0:1], scalar2=None,
                                    op0=ALU.mult).then_inc(sem["vq1"], 1)
                    e.tensor_scalar(out=w1qb[:, ci * HID + seg * 512:
                                           ci * HID + (seg + 1) * 512],
                                    in0=wqi[:, :], scalar1=1.0, scalar2=-1.0,
                                    op0=ALU.min, op1=ALU.max)
                else:
                    k2 = k - 64
                    ci, half = k2 // 2, k2 % 2
                    e.wait_ge(sem["dwq"], 1024 + 16 * (k2 + 1))
                    if half == 0 and ci >= 1:
                        e.wait_ge(sem["dw2o"], 64 * ci)
                    dst = (statf2, statf3)[half]
                    e.tensor_scalar(out=wqi2[:, :],
                                    in0=wstat[:, (k2 % 4) * 512:(k2 % 4 + 1) * 512],
                                    scalar1=wscl[:, 1:2], scalar2=None,
                                    op0=ALU.mult).then_inc(sem["vq1"], 1)
                    e.tensor_scalar(out=dst[:, :], in0=wqi2[:, :],
                                    scalar1=1.0, scalar2=-1.0,
                                    op0=ALU.min,
                                    op1=ALU.max).then_inc(sem["gq2"], 1)

        quant_batch(0)
        e.wait_ge(sem["dsg"], 64)
        for j in range(4):
            for cc in range(DC):
                if j == 0:
                    e.tensor_scalar(out=sgin[:, cc * NHD:(cc + 1) * NHD],
                                    in0=sg[:, cc * NHD:(cc + 1) * NHD],
                                    scalar1=combW[:, cc * 8:cc * 8 + 1],
                                    scalar2=None, op0=ALU.mult)
                else:
                    e.scalar_tensor_tensor(
                        out=sgin[:, cc * NHD:(cc + 1) * NHD],
                        in0=sg[:, j * 512 + cc * NHD:j * 512 + (cc + 1) * NHD],
                        scalar=combW[:, cc * 8 + j:cc * 8 + j + 1],
                        in1=sgin[:, cc * NHD:(cc + 1) * NHD],
                        op0=ALU.mult, op1=ALU.add)
        e.drain().then_inc(sem["vcmb"], 1)
        e.wait_ge(sem["dsg"], 128)
        for j in range(4, NCORE):
            for cc in range(DC):
                e.scalar_tensor_tensor(
                    out=sgin[:, cc * NHD:(cc + 1) * NHD],
                    in0=sg[:, (j - 4) * 512 + cc * NHD:(j - 4) * 512 + (cc + 1) * NHD],
                    scalar=combW[:, cc * 8 + j:cc * 8 + j + 1],
                    in1=sgin[:, cc * NHD:(cc + 1) * NHD],
                    op0=ALU.mult, op1=ALU.add)
        e.drain().then_inc(sem["vcmb"], 2)

        e.wait_ge(sem["vcmb"], 3)
        for m in range(NCH):
            bb = m % 2
            if m == 4:
                e.wait_ge(sem["sws"], 128)
                e.tensor_reduce(out=s128[:, 1:2], in_=wacc[:, 64:128],
                                axis=AX.X, op=ALU.add)
                e.drain().then_inc(sem["vwsc"], 1)
                e.wait_ge(sem["tbc"], 2)
                e.tensor_scalar(out=swc[:, 1:2], in0=pA[:, 16:17],
                                scalar1=1.0 / (D * HID), scalar2=1e-5,
                                op0=ALU.mult, op1=ALU.max)
                e.drain()
                e.reciprocal(out=wscl[:, 1:2], in_=swc[:, 1:2])
                e.drain()
                e.tensor_scalar(out=sw127[:, 1:2], in0=swc[:, 1:2],
                                scalar1=1.0 / 127.0, scalar2=None, op0=ALU.mult)
                e.drain().then_inc(sem["vwsc"], 1)
            e.wait_ge(sem["ty2"], 8 * m)
            for cc in range(DC):
                e.tensor_scalar(out=sgmb[:, bb * 512 + cc * NHD:
                                        bb * 512 + (cc + 1) * NHD],
                                in0=sgin[:, cc * NHD:(cc + 1) * NHD],
                                scalar1=wsmT[:, cc * 8 + m:cc * 8 + m + 1],
                                scalar2=None,
                                op0=ALU.mult).then_inc(sem["vsc3"], 1)
            if m > 1:
                quant_batch(m - 1, 0)
            e.wait_ge(sem["dld"], 16 * (4 * m + 4))
            if m > 0:
                e.wait_ge(sem["to"], 8 * m)
            for cc in range(DC):
                e.wait_ge(sem["ty2"], 8 * m + cc + 1)
                bank = (pP0, pP1, pY0, pY1)[cc // 2]
                colb = (cc % 2) * Q
                e.tensor_tensor(out=ybf3[:, cc * Q:(cc + 1) * Q],
                                in0=bank[:, colb:colb + Q],
                                in1=ybl[:, bb * 2048 + cc * Q:
                                        bb * 2048 + (cc + 1) * Q],
                                op=ALU.add).then_inc(sem["vy2"], 1)
            if m > 1:
                quant_batch(m - 1, 1)
            e.wait_ge(sem["dld"], 16 * 4 * (m + 1))
            for cc in range(DC):
                e.tensor_scalar(out=xf3[:, cc * Q:(cc + 1) * Q],
                                in0=xf3[:, cc * Q:(cc + 1) * Q],
                                scalar1=gabo_c[:, cc:cc + 1], scalar2=None,
                                op0=ALU.add)
            e.wait_ge(sem["to"], 8 * (m + 1))
            for cc in range(DC):
                e.scalar_tensor_tensor(out=xres[:, cc * TP + 1 + m * Q:
                                               cc * TP + 1 + (m + 1) * Q],
                                       in0=pA[:, cc * Q:(cc + 1) * Q],
                                       scalar=ga_c[:, cc:cc + 1],
                                       in1=xf3[:, cc * Q:(cc + 1) * Q],
                                       op0=ALU.mult,
                                       op1=ALU.add).then_inc(sem["vxr"], 1)
            e.nop().then_inc(sem["so3"], 8)
        quant_batch(NCH - 1)
        e.wait_ge(sem["dw2o"], 2048)
        for grp in range(4):
            if grp > 0:
                e.wait_ge(sem["t2s"], grp)
                e.wait_ge(sem["s2r"], grp)
            for cc in range(DC):
                sl = slice(cc * TP + 1 + grp * SLAB, cc * TP + 1 + (grp + 1) * SLAB)
                e.tensor_tensor(out=sq3[:, cc * SLAB:(cc + 1) * SLAB],
                                in0=xres[:, sl], in1=xres[:, sl],
                                op=ALU.mult).then_inc(sem["gq3"], 1)
            e.wait_ge(sem["t2s"], grp + 1)
            e.tensor_scalar(out=mF[:, :], in0=pP0[:, 0:SLAB], scalar1=1.0 / D,
                            scalar2=None, op0=ALU.mult)
            e.tensor_copy(out=m2B[:, 1 + grp * SLAB:1 + (grp + 1) * SLAB],
                          in_=mF[:, :])
            e.tensor_scalar(out=statf2[:, :], in0=pP1[:, 0:SLAB], scalar1=1.0 / D,
                            scalar2=None, op0=ALU.mult)
            e.tensor_tensor(out=statf3[:, :], in0=mF[:, :], in1=mF[:, :],
                            op=ALU.mult)
            e.tensor_tensor(out=statf2[:, :], in0=statf2[:, :], in1=statf3[:, :],
                            op=ALU.subtract).then_inc(sem["v2s"], 1)
        e.wait_ge(sem["tsel"], 1)
        e.tensor_copy(out=rowpair[0:2, 0:512], in_=pY0[0:2, 0:512])
        e.tensor_copy(out=rowpair[0:2, 512:1024], in_=pY1[0:2, 0:512])
        e.drain().then_inc(sem["vrow"], 1)
        e.wait_ge(sem["dmh"], 16 * 16)
        for cc in range(DC):
            e.tensor_copy(out=xh[:, 2 * cc:2 * cc + 1],
                          in_=xres[:, cc * TP:cc * TP + 1])
            e.tensor_copy(out=xh[:, 2 * cc + 1:2 * cc + 2],
                          in_=xres[:, cc * TP + T + 1:cc * TP + T + 2])
        e.drain()
        e.tensor_tensor(out=xhsq[:, :], in0=xh[:, :], in1=xh[:, :], op=ALU.mult)
        e.drain().then_inc(sem["v2h"], 1)
        e.wait_ge(sem["t2h"], 1)
        e.wait_ge(sem["s2r"], 4)
        e.tensor_scalar(out=statf3[:, 0:2], in0=pP0[:, 0:2], scalar1=1.0 / D,
                        scalar2=None, op0=ALU.mult)
        e.drain()
        e.tensor_copy(out=m2B[:, 0:1], in_=statf3[:, 0:1])
        e.tensor_copy(out=m2B[:, T + 1:T + 2], in_=statf3[:, 1:2])
        e.tensor_tensor(out=statf3[:, 2:4], in0=statf3[:, 0:2],
                        in1=statf3[:, 0:2], op=ALU.mult)
        e.tensor_scalar(out=statf2[:, 0:2], in0=pP1[:, 0:2], scalar1=1.0 / D,
                        scalar2=None, op0=ALU.mult)
        e.drain()
        e.tensor_tensor(out=statf2[:, 0:2], in0=statf2[:, 0:2],
                        in1=statf3[:, 2:4], op=ALU.subtract)
        e.drain().then_inc(sem["v2h"], 1)


    @blk.scalar
    def _(e):
        e.wait_ge(sem["vxr"], 8 * NCH)
        for cc in range(DC):
            e.dma_start(out=dram["rows2"][0:1, cc * 128:(cc + 1) * 128],
                        in_=xres[:, cc * TP + 1:cc * TP + 2]
                        ).then_inc(sem["dr2"], 16)
        for cc in range(DC):
            e.dma_start(out=dram["rows2"][1:2, cc * 128:(cc + 1) * 128],
                        in_=xres[:, cc * TP + T:cc * TP + T + 1]
                        ).then_inc(sem["dr2"], 16)
        for grp in range(4):
            e.wait_ge(sem["v2s"], grp + 1)
            e.activation(out=statf3[:, :], in_=statf2[:, :], func=AF.Ln,
                         bias=np.float32(EPS))
            e.drain()
            e.activation(out=r2B[:, 1 + grp * SLAB:1 + (grp + 1) * SLAB],
                         in_=statf3[:, :], func=AF.Exp,
                         scale=np.float32(-0.5)).then_inc(sem["s2r"], 1)
        e.wait_ge(sem["vrow"], 1)
        for cc in range(DC):
            e.dma_start(out=xres[:, cc * TP:cc * TP + 1],
                        in_=rowpair[0:1, cc * 128:(cc + 1) * 128]
                        ).then_inc(sem["dmh"], 16)
        for cc in range(DC):
            e.dma_start(out=xres[:, cc * TP + T + 1:cc * TP + T + 2],
                        in_=rowpair[1:2, cc * 128:(cc + 1) * 128]
                        ).then_inc(sem["dmh"], 16)
        e.wait_ge(sem["v2h"], 2)
        e.activation(out=statf3[:, 4:6], in_=statf2[:, 0:2], func=AF.Ln,
                     bias=np.float32(EPS))
        e.drain()
        e.activation(out=r2B[:, 0:1], in_=statf3[:, 4:5], func=AF.Exp,
                     scale=np.float32(-0.5))
        e.activation(out=r2B[:, T + 1:T + 2], in_=statf3[:, 5:6], func=AF.Exp,
                     scale=np.float32(-0.5))
        e.drain().then_inc(sem["s2h"], 1)




def _emit_p4(nc, blk, sem, sbt, psums, dram, barrier):
    g = lambda n: sbt[n]
    xres, m2B, r2B = g("xres"), g("m2B"), g("r2B")
    w1qb, swc, sw127 = g("w1qb"), g("swc"), g("sw127")
    cwb, cbb = g("cwb"), g("cbb")
    cb1_c, sb2_c, gb_c, lrv = g("cb1_c"), g("sb2_c"), g("gb_c"), g("lrv")
    onesrow = g("onesrow")
    btmp, btmp2 = g("btmp"), g("btmp2")
    udq, h2w, xq1w, w2s = g("udq"), g("h2w"), g("xq1w"), g("w2s")
    qt, mx4, mn4 = g("qt"), g("mx4"), g("mn4")
    sc13, scr2, oout = g("sc13"), g("scr2"), g("oout")
    s1bb, uib, s2bb, ds2b = g("s1bb"), g("uib"), g("s2bb"), g("ds2b")
    statf, identb = g("statf"), g("identb")
    pP0, pP1, pA, pY0, pY1 = (psums[k] for k in ("pP0", "pP1", "pA", "pY0", "pY1"))
    w2q_d, out_fm = dram["w2q_d"], dram["out_fm"]
    TP = T + 2
    NW = 5
    WJ = [512, 512, 512, 512, 16]
    HJ = [512, 512, 512, 512, 10]
    CJ = [510, 510, 510, 510, 8]
    w2bank = [pA, pA, pA, pA, pP0, pP1, pY0, pY1]

    def w2bk(co):
        if co < 4:
            return pA[:, co * 512:(co + 1) * 512]
        return w2bank[co][:, 0:512]

    @blk.sync
    def _(e):
        for s_, v_ in (("ty2", 64), ("to", 64), ("vxr", 64), ("t2s", 4),
                       ("t2h", 1), ("dmh", 256), ("vsc3", 64), ("tsel", 1)):
            e.wait_ge(sem[s_], v_)
        for j in range(NW):
            C = CJ[j]
            for ci in range(HCC):
                n = 32 * j + ci
                e.wait_ge(sem["dw2o"], 64 * (ci + 1))
                if n >= 2:
                    e.wait_ge(sem["tw2"], n - 1)
                e.dma_start(out=w2s[:, (ci % 2) * D:(ci % 2 + 1) * D],
                            in_=w2q_d[ci]).then_inc(sem["dwl"], 16)
            for co in range(DC):
                e.wait_ge(sem["vo4"], 8 * j + co + 1)
                e.dma_start(out=out_fm[co][:, 510 * j:510 * j + C],
                            in_=oout[:, (co % 4) * 512:(co % 4) * 512 + C]
                            ).then_inc(sem["dout"], 16)

    @blk.tensor
    def _(e):
        for j in range(NW):
            W, C = WJ[j], CJ[j]
            if j > 0:
                e.wait_ge(sem["vo4"], 8 * j)
            if j == 0:
                e.wait_ge(sem["v2h"], 2)
            e.wait_ge(sem["vh4"], 10 * j + 9)
            for tsub in range(4):
                e.matmul(pY0[:, tsub * 128:(tsub + 1) * 128],
                         mx4[:, tsub * 128:(tsub + 1) * 128], identb[:, :],
                         start=True, stop=True)
                mm = e.matmul(pY1[:, tsub * 128:(tsub + 1) * 128],
                              mn4[:, tsub * 128:(tsub + 1) * 128], identb[:, :],
                              start=True, stop=True)
            mm.then_inc(sem["tbc2"], 1)
            e.wait_ge(sem["vax"], j + 1)
            for tsub in range(4):
                e.matmul(pY0[0:1, tsub * 128:(tsub + 1) * 128],
                         btmp[:, 0 + tsub:1 + tsub], identb[:, :],
                         start=True, stop=True)
                mm = e.matmul(pY1[0:1, tsub * 128:(tsub + 1) * 128],
                              btmp[:, 4 + tsub:5 + tsub], identb[:, :],
                              start=True, stop=True)
            mm.then_inc(sem["tbc2"], 1)
            e.wait_ge(sem["vbc"], 4 * j + 1)
            e.matmul(pY0[:, 0:512], onesrow[:, :], mx4[0:1, 0:512],
                     start=True, stop=True)
            e.matmul(pY1[:, 0:512], onesrow[:, :], mn4[0:1, 0:512],
                     start=True, stop=True).then_inc(sem["tbc2"], 1)
            e.wait_ge(sem["vbc"], 4 * j + 2)
            if j > 0:
                e.wait_ge(sem["tw2"], 32 * j)
            for co in range(HCC):
                ub = (pA, pA, pA, pA, pY0, pY1)[co % 6]
                uo = (co % 6) * 512 if co % 6 < 4 else 0
                if co >= 6:
                    e.wait_ge(sem["vud"], 32 * j + co - 5)
                if co % 6 >= 4 and co < 6 and j > 0:
                    e.wait_ge(sem["vo4"], 8 * j)
                for ci in range(DC):
                    mm = e.matmul(ub[:, uo:uo + W],
                                  w1qb[:, ci * HID + co * 128:
                                       ci * HID + (co + 1) * 128],
                                  xq1w[:, ci * 512:ci * 512 + W],
                                  start=(ci == 0), stop=(ci == DC - 1))
                mm.then_inc(sem["tu"], 1)
            e.wait_ge(sem["vbc"], 4 * j + 2)
            e.wait_ge(sem["vh4"], 10 * j + 10)
            for tsub in range(4):
                e.matmul(pY0[:, tsub * 128:(tsub + 1) * 128],
                         mx4[:, tsub * 128:(tsub + 1) * 128], identb[:, :],
                         start=True, stop=True)
                mm = e.matmul(pY1[:, tsub * 128:(tsub + 1) * 128],
                              mn4[:, tsub * 128:(tsub + 1) * 128], identb[:, :],
                              start=True, stop=True)
            mm.then_inc(sem["tbc2"], 1)
            e.wait_ge(sem["vax2"], j + 1)
            for tsub in range(4):
                e.matmul(pY0[0:1, tsub * 128:(tsub + 1) * 128],
                         btmp[:, 0 + tsub:1 + tsub], identb[:, :],
                         start=True, stop=True)
                mm = e.matmul(pY1[0:1, tsub * 128:(tsub + 1) * 128],
                              btmp[:, 4 + tsub:5 + tsub], identb[:, :],
                              start=True, stop=True)
            mm.then_inc(sem["tbc2"], 1)
            e.wait_ge(sem["vbc"], 4 * j + 3)
            e.matmul(pY0[:, 0:512], onesrow[:, :], mx4[0:1, 0:512],
                     start=True, stop=True)
            e.matmul(pY1[:, 0:512], onesrow[:, :], mn4[0:1, 0:512],
                     start=True, stop=True).then_inc(sem["tbc2"], 1)
            e.wait_ge(sem["vbc"], 4 * j + 4)
            e.wait_ge(sem["vud"], 32 * (j + 1))
            for ci in range(HCC):
                e.wait_ge(sem["dwl"], 16 * (32 * j + ci + 1))
                e.wait_ge(sem["vxq2"], 32 * j + ci + 1)
                for co in range(DC):
                    mm = e.matmul(w2bk(co)[:, 0:C],
                                  w2s[:, (ci % 2) * D + co * 128:
                                      (ci % 2) * D + (co + 1) * 128],
                                  udq[:, ci * 512:ci * 512 + C],
                                  start=(ci == 0), stop=(ci == HCC - 1))
                mm.then_inc(sem["tw2"], 1)

    @blk.vector
    def _(e):
        def h2trees(j):
            W = WJ[j]
            dr = (j == 4)
            base = 510 * j
            e.wait_ge(sem["s2r"], min(j + 1, 4))
            if j in (0, 4):
                e.wait_ge(sem["s2h"], 1)
            if j > 0:
                e.wait_ge(sem["tbc2"], 6 * j - 1)
                e.wait_ge(sem["tu"], HCC * j)
            HW = HJ[j]
            for cc in range(DC):
                xsl = slice(cc * TP + base, cc * TP + base + HW)
                e.tensor_tensor(out=btmp[:, 0:HW], in0=xres[:, xsl],
                                in1=m2B[:, base:base + HW], op=ALU.subtract)
                if dr:
                    e.drain()
                e.tensor_tensor(out=btmp2[:, 0:HW], in0=btmp[:, 0:HW],
                                in1=r2B[:, base:base + HW], op=ALU.mult)
                if dr:
                    e.drain()
                e.tensor_scalar(out=btmp2[:, 0:HW], in0=btmp2[:, 0:HW],
                                scalar1=cb1_c[:, cc:cc + 1], scalar2=None,
                                op0=ALU.mult)
                if dr:
                    e.drain()
                e.tensor_scalar(out=h2w[:, cc * 512:cc * 512 + HW],
                                in0=btmp2[:, 0:HW], scalar1=sb2_c[:, cc:cc + 1],
                                scalar2=None, op0=ALU.add)
                if dr:
                    e.drain()
            if j == 0:
                e.drain()
                for cc in range(DC):
                    e.tensor_scalar(out=h2w[:, cc * 512:cc * 512 + 1],
                                    in0=h2w[:, cc * 512:cc * 512 + 1],
                                    scalar1=lrv[:, 0:1], scalar2=None,
                                    op0=ALU.mult)
                e.drain()
            if j == 4:
                e.drain()
                for cc in range(DC):
                    e.tensor_scalar(out=h2w[:, cc * 512 + 9:cc * 512 + 10],
                                    in0=h2w[:, cc * 512 + 9:cc * 512 + 10],
                                    scalar1=lrv[:, 1:2], scalar2=None,
                                    op0=ALU.mult)
                    e.memset(h2w[:, cc * 512 + 10:cc * 512 + 16], 0.0)
                e.drain()
            e.drain().then_inc(sem["vh4"], 8)
            e.tensor_tensor(out=sc13[:, 0:2048], in0=h2w[:, 0:2048],
                            in1=h2w[:, 2048:4096], op=ALU.max)
            e.tensor_tensor(out=sc13[:, 2048:4096], in0=h2w[:, 0:2048],
                            in1=h2w[:, 2048:4096], op=ALU.min)
            if dr:
                e.drain()
            e.tensor_tensor(out=sc13[:, 0:1024], in0=sc13[:, 0:1024],
                            in1=sc13[:, 1024:2048], op=ALU.max)
            e.tensor_tensor(out=sc13[:, 2048:3072], in0=sc13[:, 2048:3072],
                            in1=sc13[:, 3072:4096], op=ALU.min)
            if dr:
                e.drain()
            e.tensor_tensor(out=mx4[:, 0:512], in0=sc13[:, 0:512],
                            in1=sc13[:, 512:1024], op=ALU.max)
            e.tensor_tensor(out=mn4[:, 0:512], in0=sc13[:, 2048:2560],
                            in1=sc13[:, 2560:3072], op=ALU.min)
            e.drain().then_inc(sem["vh4"], 1)

        h2trees(0)
        for j in range(NW):
            W, C = WJ[j], CJ[j]
            dr = (j == 4)
            base = 510 * j

            def mdrain():
                if dr:
                    e.drain()
            e.wait_ge(sem["tbc2"], 6 * j + 1)
            for tsub in range(4):
                e.tensor_reduce(out=statf[:, tsub:tsub + 1],
                                in_=pY0[:, tsub * 128:(tsub + 1) * 128],
                                axis=AX.X, op=ALU.max, apply_absolute_value=True)
            for tsub in range(4):
                e.tensor_reduce(out=statf[:, 4 + tsub:5 + tsub],
                                in_=pY1[:, tsub * 128:(tsub + 1) * 128],
                                axis=AX.X, op=ALU.max, apply_absolute_value=True)
            e.drain()
            e.tensor_tensor(out=statf[:, 0:4], in0=statf[:, 0:4],
                            in1=statf[:, 4:8], op=ALU.max)
            e.drain()
            e.tensor_scalar(out=statf[:, 0:4], in0=statf[:, 0:4],
                            scalar1=1e-5, scalar2=None, op0=ALU.max)
            e.drain()
            e.reciprocal(out=statf[:, 8:12], in_=statf[:, 0:4])
            e.drain()
            e.tensor_scalar(out=btmp[:, 0:4], in0=statf[:, 8:12],
                            scalar1=127.0, scalar2=None, op0=ALU.mult)
            e.tensor_scalar(out=btmp[:, 4:8], in0=statf[:, 0:4],
                            scalar1=sw127[:, 0:1], scalar2=None, op0=ALU.mult)
            e.drain().then_inc(sem["vax"], 1)
            e.wait_ge(sem["tbc2"], 6 * j + 2)
            e.tensor_copy(out=mx4[0:1, 0:512], in_=pY0[0:1, 0:512])
            e.tensor_copy(out=mn4[0:1, 0:512], in_=pY1[0:1, 0:512])
            e.drain().then_inc(sem["vbc"], 1)
            e.wait_ge(sem["tbc2"], 6 * j + 3)
            e.tensor_copy(out=s1bb[:, :], in_=pY0[:, 0:512])
            e.tensor_copy(out=uib[:, :], in_=pY1[:, 0:512])
            e.drain()
            if j > 0:
                e.wait_ge(sem["tu"], 32 * j)
            for cc in range(DC):
                e.tensor_tensor(out=qt[:, 0:W], in0=h2w[:, cc * 512:cc * 512 + W],
                                in1=s1bb[:, 0:W], op=ALU.mult)
                mdrain()
                e.tensor_copy(out=xq1w[:, cc * 512:cc * 512 + W], in_=qt[:, 0:W])
                mdrain()
            e.drain().then_inc(sem["vbc"], 1)
            if j > 0:
                e.wait_ge(sem["tw2"], 32 * j)
            for co in range(HCC + 1):
                if co < HCC:
                    ub = (pA, pA, pA, pA, pY0, pY1)[co % 6]
                    uo = (co % 6) * 512 if co % 6 < 4 else 0
                    e.wait_ge(sem["tu"], 32 * j + co + 1)
                    e.tensor_tensor(out=udq[:, co * 512:co * 512 + W],
                                    in0=ub[:, uo:uo + W],
                                    in1=uib[:, 0:W],
                                    op=ALU.mult).then_inc(sem["vud"], 1)
                cv = co - 1
                if cv >= 0 and cv % 2 == 0:
                    e.wait_ge(sem["gcv"], 64 * j + 2 * (cv + 1))
                    if cv >= 2:
                        e.wait_ge(sem["ssl"], 32 * j + cv - 1)
                    e.tensor_tensor(out=scr2[:, 0:C],
                                    in0=scr2[:, 0:C],
                                    in1=sc13[:, 0:C], op=ALU.add)
                    mdrain()
                    e.tensor_tensor(out=scr2[:, 0:C],
                                    in0=scr2[:, 0:C],
                                    in1=sc13[:, 2048:2048 + C],
                                    op=ALU.add).then_inc(sem["vcv"], 1)
                if cv >= 2:
                    sv = cv - 2
                    if sv == 0:
                        e.tensor_copy(out=mx4[:, 0:C], in_=udq[:, 0:C])
                        e.tensor_copy(out=mn4[:, 0:C], in_=udq[:, 0:C])
                    else:
                        e.tensor_tensor(out=mx4[:, 0:C], in0=mx4[:, 0:C],
                                        in1=udq[:, sv * 512:sv * 512 + C],
                                        op=ALU.max)
                        e.tensor_tensor(out=mn4[:, 0:C], in0=mn4[:, 0:C],
                                        in1=udq[:, sv * 512:sv * 512 + C],
                                        op=ALU.min)
                    mdrain()
            e.wait_ge(sem["ssl"], 32 * (j + 1))
            for sv in (30, 31):
                e.tensor_tensor(out=mx4[:, 0:C], in0=mx4[:, 0:C],
                                in1=udq[:, sv * 512:sv * 512 + C], op=ALU.max)
                e.tensor_tensor(out=mn4[:, 0:C], in0=mn4[:, 0:C],
                                in1=udq[:, sv * 512:sv * 512 + C], op=ALU.min)
                mdrain()
            e.drain().then_inc(sem["vh4"], 1)
            e.wait_ge(sem["tbc2"], 6 * j + 4)
            for tsub in range(4):
                e.tensor_reduce(out=statf[:, tsub:tsub + 1],
                                in_=pY0[:, tsub * 128:(tsub + 1) * 128],
                                axis=AX.X, op=ALU.max, apply_absolute_value=True)
            for tsub in range(4):
                e.tensor_reduce(out=statf[:, 4 + tsub:5 + tsub],
                                in_=pY1[:, tsub * 128:(tsub + 1) * 128],
                                axis=AX.X, op=ALU.max, apply_absolute_value=True)
            e.drain()
            e.tensor_tensor(out=statf[:, 0:4], in0=statf[:, 0:4],
                            in1=statf[:, 4:8], op=ALU.max)
            e.drain()
            e.tensor_scalar(out=statf[:, 0:4], in0=statf[:, 0:4],
                            scalar1=1e-5, scalar2=None, op0=ALU.max)
            e.drain()
            e.reciprocal(out=statf[:, 8:12], in_=statf[:, 0:4])
            e.drain()
            e.tensor_scalar(out=btmp[:, 0:4], in0=statf[:, 8:12],
                            scalar1=127.0, scalar2=None, op0=ALU.mult)
            e.tensor_scalar(out=btmp[:, 4:8], in0=statf[:, 0:4],
                            scalar1=sw127[:, 1:2], scalar2=None, op0=ALU.mult)
            e.drain().then_inc(sem["vax2"], 1)
            e.wait_ge(sem["tbc2"], 6 * j + 5)
            e.tensor_copy(out=mx4[0:1, 0:512], in_=pY0[0:1, 0:512])
            e.tensor_copy(out=mn4[0:1, 0:512], in_=pY1[0:1, 0:512])
            e.drain().then_inc(sem["vbc"], 1)
            e.wait_ge(sem["tbc2"], 6 * j + 6)
            e.tensor_copy(out=s2bb[:, :], in_=pY0[:, 0:512])
            e.tensor_copy(out=ds2b[:, :], in_=pY1[:, 0:512])
            e.drain()
            for i in range(HCC):
                e.tensor_tensor(out=qt[:, 0:C], in0=udq[:, i * 512:i * 512 + C],
                                in1=s2bb[:, 0:C], op=ALU.mult)
                mdrain()
                e.tensor_copy(out=udq[:, i * 512:i * 512 + C], in_=qt[:, 0:C])
                mdrain()
                e.drain().then_inc(sem["vxq2"], 1)
            e.nop().then_inc(sem["vbc"], 1)
            if j + 1 < NW:
                h2trees(j + 1)
            e.wait_ge(sem["tw2"], 32 * (j + 1))
            for co in range(DC):
                if co >= 4:
                    e.wait_ge(sem["dout"], 16 * (8 * j + co - 3))
                osl = slice((co % 4) * 512, (co % 4) * 512 + C)
                e.tensor_tensor(out=oout[:, osl], in0=w2bk(co)[:, 0:C],
                                in1=ds2b[:, 0:C], op=ALU.mult)
                mdrain()
                e.scalar_tensor_tensor(out=oout[:, osl],
                                       in0=oout[:, osl],
                                       scalar=gb_c[:, co:co + 1],
                                       in1=xres[:, co * TP + 1 + base:
                                               co * TP + 1 + base + C],
                                       op0=ALU.mult, op1=ALU.add)
                e.drain().then_inc(sem["vo4"], 1)

    @blk.gpsimd
    def _(e):
        for j in range(NW):
            W, C = WJ[j], CJ[j]
            dr = (j == 4)

            def gdrain():
                if dr:
                    e.drain()
            for cv in range(1, HCC, 2):
                e.wait_ge(sem["gcv"], 64 * j + 2 * (cv + 1))
                if cv >= 2:
                    e.wait_ge(sem["ssl"], 32 * j + cv - 1)
                e.tensor_tensor(out=scr2[:, 512:512 + C],
                                in0=scr2[:, 512:512 + C],
                                in1=sc13[:, 512:512 + C], op=ALU.add)
                gdrain()
                e.tensor_tensor(out=scr2[:, 512:512 + C],
                                in0=scr2[:, 512:512 + C],
                                in1=sc13[:, 2048 + 512:2048 + 512 + C],
                                op=ALU.add).then_inc(sem["gvc"], 1)

        for s_, v_ in (("dout", 16 * 8 * NW), ("dwl", 16 * HCC * NW),
                       ("vo4", 8 * NW), ("tw2", HCC * NW), ("vh4", 10 * NW),
                       ("vax", NW), ("vax2", NW), ("tbc2", 6 * NW),
                       ("vbc", 4 * NW), ("tu", HCC * NW), ("vud", HCC * NW),
                       ("gcv", 2 * HCC * NW), ("vcv", HCC * NW // 2), ("gvc", HCC * NW // 2),
                       ("ssl", HCC * NW), ("vxq2", HCC * NW)):
            e.wait_ge(sem[s_], v_)

    @blk.scalar
    def _(e):
        e.wait_ge(sem["dws"], 2048 + 192)
        for j in range(NW):
            C = CJ[j]
            for co in range(HCC + 1):
                if co < HCC:
                    e.wait_ge(sem["vud"], 32 * j + co + 1)
                    if co >= 2:
                        if co % 2 == 0:
                            e.wait_ge(sem["vcv"], 16 * j + co // 2)
                        else:
                            e.wait_ge(sem["gvc"], 16 * j + (co - 1) // 2)
                    sl2 = (co % 2) * 512
                    e.activation(out=scr2[:, sl2:sl2 + C],
                                 in_=udq[:, co * 512 + 1:co * 512 + 1 + C],
                                 func=AF.Identity,
                                 scale=cwb[:, 3 * co + 1:3 * co + 2],
                                 bias=cbb[:, co:co + 1])
                    e.activation(out=sc13[:, sl2:sl2 + C],
                                 in_=udq[:, co * 512 + 0:co * 512 + C],
                                 func=AF.Identity,
                                 scale=cwb[:, 3 * co:3 * co + 1])
                    e.activation(out=sc13[:, 2048 + sl2:2048 + sl2 + C],
                                 in_=udq[:, co * 512 + 2:co * 512 + 2 + C],
                                 func=AF.Identity,
                                 scale=cwb[:, 3 * co + 2:3 * co + 3])
                    e.drain().then_inc(sem["gcv"], 2)
                sl = co - 1
                if sl >= 0:
                    if sl % 2 == 0:
                        e.wait_ge(sem["vcv"], 16 * j + sl // 2 + 1)
                    else:
                        e.wait_ge(sem["gvc"], 16 * j + (sl - 1) // 2 + 1)
                    e.activation(out=udq[:, sl * 512:sl * 512 + C],
                                 in_=scr2[:, (sl % 2) * 512:(sl % 2) * 512 + C],
                                 func=AF.Silu)
                    e.drain().then_inc(sem["ssl"], 1)



def _prep_core_inputs(inputs, core):
    inp = inputs
    x = np.asarray(inp["L0_attn_x"])[0]
    t0 = core * T
    xs = x[t0:t0 + T]
    d = {}
    d["x_fm"] = np.ascontiguousarray(xs.T.reshape(DC, 128, T), np.float32)
    d["x_cm"] = np.ascontiguousarray(
        xs.reshape(NCH, Q, DC, 128).transpose(3, 0, 2, 1).reshape(
            128, NCH, DC * Q), np.float32)
    halo = np.asarray(x[t0 - 1]) if core > 0 else np.zeros(D, np.float32)
    d["x_halo"] = np.ascontiguousarray(halo.reshape(DC, 128).T, np.float32)
    d["halovalid"] = np.full((128, 1), 1.0 if core > 0 else 0.0, np.float32)
    lr = np.ones((128, 2), np.float32)
    if core == 0:
        lr[:, 0] = 0.0
    if core == NCORE - 1:
        lr[:, 1] = 0.0
    d["lrvalid"] = lr
    selm = np.zeros((16, 2), np.float32)
    if core > 0:
        selm[2 * (core - 1) + 1, 0] = 1.0
    if core < NCORE - 1:
        selm[2 * (core + 1), 1] = 1.0
    d["selmat"] = selm
    ex = np.full(8, 1e6, np.float32)
    for j in range(core):
        ex[j] = 2048.0 * (core - 1 - j)
    d["exps2048"] = ex.reshape(1, 8)
    d["td_r"] = np.asarray(inp["L0_attn_time_decay"]).reshape(1, H).astype(np.float32)
    d["fa_r"] = np.asarray(inp["L0_attn_time_faaaa"]).reshape(1, H).astype(np.float32)
    d["c_fm"] = np.ascontiguousarray(
        np.asarray(inp["L0_c"]).reshape(DC, 128).T, np.float32)
    wm = np.asarray(inp["L0_wmod"]).astype(np.float32)
    d["wmod_r"] = np.ascontiguousarray(
        wm.reshape(DC, 128, 12, 512).transpose(2, 1, 0, 3).reshape(12, 128, DC * 512))
    for nm, key in (("maa_k", "L0_attn_time_maa_k"), ("maa_v", "L0_attn_time_maa_v"),
                    ("maa_r", "L0_attn_time_maa_r"), ("br_c", "L0_attn_br"),
                    ("bk_c", "L0_attn_bk"), ("bo_c", "L0_attn_bo")):
        d[nm] = np.ascontiguousarray(
            np.asarray(inp[key]).reshape(DC, 128).T, np.float32)
    d["bv_r"] = np.asarray(inp["L0_attn_bv"]).reshape(1, D).astype(np.float32)
    for nm, key in (("wrT", "L0_attn_Wr"), ("wkT", "L0_attn_Wk"),
                    ("wvT", "L0_attn_Wv"), ("woT", "L0_attn_Wo")):
        d[nm] = np.ascontiguousarray(
            np.asarray(inp[key]).T.reshape(DC, 128, D), np.float32)
    d["w1T"] = np.ascontiguousarray(
        np.asarray(inp["L0_ffn_w1"]).T.reshape(DC, 128, HID), np.float32)
    d["w2T"] = np.ascontiguousarray(
        np.asarray(inp["L0_ffn_w2"]).T.reshape(HCC, 128, D), np.float32)
    cw = np.asarray(inp["L0_ffn_conv_w"])[:, 0, :]
    d["cw_c"] = np.ascontiguousarray(
        cw.reshape(HCC, 128, 3).transpose(1, 0, 2).reshape(128, HCC * 3), np.float32)
    d["cb_c"] = np.ascontiguousarray(
        np.asarray(inp["L0_ffn_conv_b"]).reshape(HCC, 128).T, np.float32)
    return d


def kernel(**inputs):
    nc = build(upto="p4")
    in_maps = [_prep_core_inputs(inputs, core) for core in range(NCORE)]
    res = run_bass_kernel_spmd(nc, in_maps, core_ids=list(range(NCORE)))
    outs = []
    for core in range(NCORE):
        o = np.asarray(res.results[core]["out_fm"])
        outs.append(o.reshape(D, T).T)
    return np.concatenate(outs, axis=0).reshape(1, NCORE * T, D).astype(np.float32)


# version 26 (final)
# speedup vs baseline: 1.0062x; 1.0062x over previous
import contextlib
import numpy as np
import ml_dtypes
import concourse.bass as bass
import concourse.mybir as mybir
from concourse.bass_isa import ReduceOp
from concourse.bass_utils import run_bass_kernel_spmd

F32 = mybir.dt.float32
BF16 = mybir.dt.bfloat16
I32 = mybir.dt.int32
AF = mybir.ActivationFunctionType
ALU = mybir.AluOpType
AX = mybir.AxisListType

NCORE = 8
T = 2048
D = 1024
DC = 8
H = 16
NHD = 64
Q = 256
NCH = T // Q
SLAB = 512
NSLAB = T // SLAB
HID = 4096
HCC = 32
EPS = 1e-5
HB = SLAB + 1

ENGS = ("sync", "tensor", "vector", "scalar", "gpsimd")


def build(upto="p1", dbg=(), loopback=False):
    nc = bass.Bass("TRN2", target_bir_lowering=False,
                   detect_race_conditions=False)
    es = contextlib.ExitStack()

    def din(name, shape, dt=F32):
        return nc.dram_tensor(name, shape, dt, kind="ExternalInput").ap()

    dram = {}
    for name, shape in [
            ("x_sm", [128, NSLAB, DC * SLAB]), ("x_halo", [128, DC]),
            ("halovalid", [128, 1]),
            ("lrvalid", [128, 2]), ("selmat", [16, 2]), ("exps2048", [1, 8]),
            ("td_r", [1, H]), ("fa_r", [1, H]), ("c_fm", [128, DC]),
            ("wmod_r", [12, 128, DC * 512]), ("maa_k", [128, DC]),
            ("maa_v", [128, DC]), ("maa_r", [128, DC]), ("br_c", [128, DC]),
            ("bk_c", [128, DC]), ("bo_c", [128, DC]), ("bv_r", [1, D]),
            ("wrT", [DC, 128, D]), ("wkT", [DC, 128, D]), ("wvT", [DC, 128, D]),
            ("woT", [DC, 128, D]), ("w1T", [DC, 128, HID]), ("w2T", [HCC, 128, D]),
            ("cw_c", [128, HCC * 3]), ("cb_c", [128, HCC]),
            ("x_cm", [128, NCH, DC * Q])]:
        dram[name] = din(name, shape)
    dram["out_fm"] = nc.dram_tensor("out_fm", [DC, 128, T], F32,
                                    kind="ExternalOutput").ap()
    dram["sloc"] = nc.dram_tensor("sloc", [128, 512], F32).ap()
    dram["sglob"] = nc.dram_tensor("sglob", [NCORE * 128, 512], F32,
                                   addr_space="Shared").ap()
    dram["rows2"] = nc.dram_tensor("rows2", [2, D], BF16).ap()
    dram["rowsg"] = nc.dram_tensor("rowsg", [16, D], BF16,
                                   addr_space="Shared").ap()
    dram["w2q_d"] = nc.dram_tensor("w2q_d", [HCC, 128, D], BF16).ap()
    dram["y_d"] = nc.dram_tensor(
        "y_d", [128, NCH, DC * Q], BF16,
        kind="ExternalOutput" if "y_d" in dbg else "Internal").ap()
    dram["rw_d"] = nc.dram_tensor(
        "rw_d", [128, NCH, DC * Q], BF16,
        kind="ExternalOutput" if "rw_d" in dbg else "Internal").ap()

    dbg_shapes = {
        "modrow": ([1, 6144], F32), "amask": ([128, H * 2 * Q], BF16),
        "wkF": ([128, DC * Q], BF16), "wbF": ([128, DC * Q], BF16),
        "e2h": ([128, DC], F32), "ws_pp": ([128, DC], F32),
        "hbuf": ([128, DC * HB], BF16),
        "kfs": ([128, DC * SLAB], BF16), "vTs": ([128, 4 * D], BF16),
        "rbf_sl": ([128, DC * SLAB], BF16),
        "sst": ([128, DC * NHD], F32),
        "ga_c": ([128, DC], F32), "gb_c": ([128, DC], F32),
        "wsmT": ([128, DC * NCH], F32), "combW": ([128, DC * 8], F32),
        "u128": ([128, H], F32), "negE128": ([128, H], F32),
        "sa_c": ([128, DC], F32), "ca1_c": ([128, DC], F32),
        "siluc": ([128, DC], BF16),
        "sb2_c": ([128, DC], F32), "cb1_c": ([128, DC], F32),
        "xres": ([128, DC * (T + 2)], BF16), "m2B": ([128, T + 2], BF16),
        "r2B": ([128, T + 2], BF16), "sgin": ([128, 512], F32),
        "w1qb": ([128, DC * HID], BF16), "wscl": ([128, 2], F32),
        "swc": ([128, 2], F32), "ybf3": ([128, DC * Q], BF16),
        "h2w": ([128, DC * SLAB], BF16), "udq": ([128, HCC * SLAB], BF16),
        "s1bb": ([128, SLAB], BF16), "uib": ([128, SLAB], BF16),
        "s2bb": ([128, SLAB], BF16), "ds2b": ([128, SLAB], BF16),
    }
    dbg_out = {}
    for nm in dbg:
        if nm in ("y_d", "rw_d"):
            continue
        shp, dt = dbg_shapes[nm]
        dbg_out[nm] = nc.dram_tensor("dbg_" + nm, shp, dt, kind="ExternalOutput").ap()

    sbt = {}
    with es:
        def _SB(stack, name, cols, dt=F32, parts=128):
            sbt[name] = stack.enter_context(nc.sbuf_tensor(name, [parts, cols], dt))
            return sbt[name]
        SB = lambda *a, **k: _SB(es, *a, **k)

        onesrow = SB("onesrow", 128, BF16, parts=1)
        ones64 = SB("ones64", 64, BF16, parts=1)
        allones = SB("allones", 128, BF16)
        identb = SB("identb", 128, BF16)
        negE128 = SB("negE128", H)
        u128 = SB("u128", H)
        e2h = SB("e2h", DC)
        ws_pp = SB("ws_pp", DC)
        wsmT = SB("wsmT", DC * NCH)
        combW = SB("combW", DC * 8)
        siluc = SB("siluc", DC, BF16)
        sc128 = SB("sc128", DC)
        sc128b = SB("sc128b", DC)
        sa_c = SB("sa_c", DC); ca1_c = SB("ca1_c", DC)
        ga_c = SB("ga_c", DC); sb2_c = SB("sb2_c", DC)
        cb1_c = SB("cb1_c", DC); gb_c = SB("gb_c", DC)
        gabo_c = SB("gabo_c", DC)
        maa_k = SB("maa_ks", DC); maa_v = SB("maa_vs", DC); maa_r = SB("maa_rs", DC)
        br_c = SB("br_s", DC); bk_c = SB("bk_s", DC); bo_c = SB("bo_s", DC)
        padt = SB("padt", 24 + 3 * (D // 2), parts=1)
        hvalid = SB("hvalid", 1)
        lrv = SB("lrv", 2)
        bvF = SB("bvF", D, BF16)
        mB = SB("mB", SLAB, BF16)
        mF = SB("mF", SLAB)
        rstdB = SB("rstdB", SLAB, BF16)
        statf = SB("statf", SLAB)
        statf2 = SB("statf2", SLAB)
        statf3 = SB("statf3", SLAB)
        btmp = SB("btmp", SLAB, BF16)
        btmp2 = SB("btmp2", SLAB, BF16)
        sst = SB("sst", DC * NHD)
        wstat = SB("wstat", 4 * 512)
        wqi = SB("wqi", 512, I32)
        sbt["wqi2"] = wqi
        wacc = SB("wacc", 128)
        wabs = SB("wabs", 512, BF16)
        sbt["wabs"] = wabs
        sbt["allonesF"] = SB("allonesF", 128)
        sbt["s128"] = SB("s128", 2)
        sbt["swc"] = SB("swc", 2)
        sbt["wscl"] = SB("wscl", 2)
        sbt["sw127"] = SB("sw127", 2)
        esR = contextlib.ExitStack()
        SBR = lambda *a, **k: _SB(esR, *a, **k)
        sbfT = SBR("sbfT", DC * NHD, BF16)
        wkF = SBR("wkF", DC * Q, BF16)
        wbF = SBR("wbF", DC * Q, BF16)
        amask = SBR("amask", H * 2 * Q, BF16)
        hbuf = SBR("hbuf", DC * HB, BF16)

        sem = {}
        for s_ in ["bar", "dxb", "dcst", "dwgt", "dwmod", "gv", "vme", "s0", "v0",
                   "vgrow", "drow", "t0",
                   "vsq", "tstat", "vstat", "vh", "vmix", "gmix",
                   "tr", "tk", "tv", "sr", "sk", "vvT",
                   "vkwk", "tkwkT", "vkwkTe", "tA", "vA",
                   "ty", "sy", "tds", "vstate", "vrw", "dyo", "dro", "dbgd",
                   "dwmf", "t0f", "vmef", "dbgd2",
                   "dsl", "ccs", "dsg", "vcmb", "dld", "ty2", "vy2", "to",
                   "so3", "vxr", "dr2", "ccr", "drg", "tsel", "vrow", "dmh",
                   "gq3", "t2s", "v2s", "s2r", "t2h", "v2h", "s2h",
                   "dws", "sws", "vwsc", "dwq", "vq1", "gq2", "dw2o",
                   "vh4", "vax", "tbc", "vbc", "tu", "vud", "gcv", "vcv",
                   "ssl", "vax2", "tbc2", "vbc2", "vxq2", "dwl", "tw2",
                   "vo4", "dout", "vsc3", "gmn", "dtx", "gvc"]:
            sem[s_] = es.enter_context(nc.semaphore(s_))

        blk = es.enter_context(nc.Block())

        pP0 = es.enter_context(nc.psum_tensor("pP0", [128, 512], F32))
        pP1 = es.enter_context(nc.psum_tensor("pP1", [128, 512], F32))
        pA = es.enter_context(nc.psum_tensor("pA", [128, 2048], F32))
        pY0 = es.enter_context(nc.psum_tensor("pY0", [128, 512], F32))
        pY1 = es.enter_context(nc.psum_tensor("pY1", [128, 512], F32))
        vDS = nc.virtual_psum_tensor("vDS", [128, 512], F32)
        nc.back_virtual_tensor_with(vDS, pP0)
        psums = dict(pP0=pP0, pP1=pP1, pA=pA, pY0=pY0, pY1=pY1, vDS=vDS)

        barn = [0]
        def barrier():
            barn[0] += 1
            tgt = 5 * barn[0]
            for eng_name in ENGS:
                def mk(e, tgt=tgt):
                    e.drain().then_inc(sem["bar"], 1)
                    e.wait_ge(sem["bar"], tgt)
                getattr(blk, eng_name)(mk)

        esP0 = contextlib.ExitStack()
        SBP0 = lambda *a, **k: _SB(esP0, *a, **k)
        tdr = SBP0("tdr", H, parts=1)
        erow = SBP0("erow", H, parts=1)
        negE = SBP0("negE", H, parts=1)
        negE_hi = SBP0("negE_hi", H, BF16, parts=1)
        negE_lo = SBP0("negE_lo", H, BF16, parts=1)
        wsrow = SBP0("wsrow", H, parts=1)
        ws_hi = SBP0("ws_hi", H, BF16, parts=1)
        ws_lo = SBP0("ws_lo", H, BF16, parts=1)
        urow = SBP0("urow", H, parts=1)
        urow_b = SBP0("urow_b", H, BF16, parts=1)
        expsrow = SBP0("expsrow", 8, parts=1)
        expsrow_b = SBP0("expsrow_b", 8, BF16, parts=1)
        iotQr = SBP0("iotQr", Q, parts=1)
        iotQr2 = SBP0("iotQr2", Q, parts=1)
        argk = SBP0("argk", H * Q, parts=1)
        argb = SBP0("argb", H * Q, parts=1)
        wkrow_b = SBP0("wkrow_b", H * Q, BF16, parts=1)
        wbrow_b = SBP0("wbrow_b", H * Q, BF16, parts=1)
        modrow = SBP0("modrow", 6144, parts=1)
        garow = SBP0("garow", 2 * D, parts=1)
        rtmp = SBP0("rtmp", 2 * D, parts=1)

        identi = SBP0("identi", 128, I32)
        iot8 = SBP0("iot8", NCH)
        expsB = SBP0("expsB", 8)
        halosq = SBP0("halosq", DC, BF16)
        crow = SBP0("crow", DC)
        chalo = SBP0("chalo", DC)
        chalob = SBP0("chalob", DC, BF16)
        deltf = SBP0("deltf", 4 * Q)
        tri01 = SBP0("tri01", 2 * Q, BF16)
        i01 = SBP0("i01", 2 * Q, BF16)
        ftmpa = SBP0("ftmpa", 2 * Q)
        wmbuf = SBP0("wmbuf", 2 * DC * 512, BF16)
        wmf32 = SBP0("wmf32", DC * 512)
        sc128c = SBP0("sc128c", DC)


        T0_NEGE, T0_U, T0_E2H, T0_WS2H, T0_EXPSB, T0_BV = 9, 10, 11, 12, 13, 14
        T0_WKF = lambda cc: 15 + cc
        T0_WBF = lambda cc: 23 + cc
        T0_HS, T0_HQ = 31, 32

        @blk.sync
        def _(e):
            for dst, src in [
                    (tdr[:, :], dram["td_r"]), (urow[:, :], dram["fa_r"]),
                    (crow[:, :], dram["c_fm"]), (chalo[:, :], dram["x_halo"]),
                    (hvalid[:, :], dram["halovalid"]), (lrv[:, :], dram["lrvalid"]),
                    (maa_k[:, :], dram["maa_k"]), (maa_v[:, :], dram["maa_v"]),
                    (maa_r[:, :], dram["maa_r"]), (br_c[:, :], dram["br_c"]),
                    (bk_c[:, :], dram["bk_c"]), (bo_c[:, :], dram["bo_c"]),
                    (expsrow[:, :], dram["exps2048"]),
                    (padt[:, 0:8], dram["exps2048"]),
                    (padt[:, 8:16], dram["exps2048"]),
                    (padt[:, 16:24], dram["exps2048"])]:
                e.dma_start(out=dst, in_=src).then_inc(sem["dcst"], 16)
            e.wait_ge(sem["dcst"], 16 * 16)

        @blk.gpsimd
        def _(e):
            e.iota(identi[:, :], pattern=[[1, 128]], base=0, channel_multiplier=-1)
            e.iota(deltf[:, 0:Q], pattern=[[1, Q]], base=-1, channel_multiplier=-1,
                   allow_small_or_imprecise_dtypes=True)
            e.iota(deltf[:, Q:2 * Q], pattern=[[1, Q]], base=-129, channel_multiplier=-1,
                   allow_small_or_imprecise_dtypes=True)
            e.iota(iotQr[:, :], pattern=[[1, Q]], base=0, channel_multiplier=0,
                   allow_small_or_imprecise_dtypes=True)
            e.iota(iotQr2[:, :], pattern=[[-1, Q]], base=Q - 1, channel_multiplier=0,
                   allow_small_or_imprecise_dtypes=True)
            e.iota(iot8[:, :], pattern=[[Q, NCH]], base=0, channel_multiplier=0,
                   allow_small_or_imprecise_dtypes=True).then_inc(sem["gv"], 1)
            e.dma_start(out=bvF[0:1, :], in_=dram["bv_r"]).then_inc(sem["dwgt"], 16)
            for j in range(3):
                e.dma_start(out=padt[:, 24 + j * D // 2:24 + (j + 1) * D // 2],
                            in_=dram["bv_r"][0:1, 0:D // 2]).then_inc(sem["dwgt"], 16)

        @blk.vector
        def _(e):
            e.memset(onesrow[:, :], 1.0)
            e.memset(ones64[:, :], 1.0)
            e.memset(allones[:, :], 1.0)
            e.memset(pA[:, :], 0.0)
            e.memset(pY0[:, :], 0.0)
            e.memset(pY1[:, :], 0.0)
            e.memset(pP0[:, :], 0.0)
            e.memset(pP1[:, :], 0.0)
            e.memset(sst[:, :], 0.0)
            e.memset(sbfT[:, :], 0.0)
            e.wait_ge(sem["gv"], 1)
            e.tensor_scalar(out=identb[:, :], in0=identi[:, :], scalar1=0,
                            scalar2=None, op0=ALU.is_equal)

        @blk.scalar
        def _(e):
            e.wait_ge(sem["dcst"], 16 * 16)
            e.activation(out=erow[:, :], in_=tdr[:, :], func=AF.Exp)
            e.drain()
        barrier()

        @blk.vector
        def _(e):
            e.tensor_scalar(out=negE[:, :], in0=erow[:, :], scalar1=-1.0,
                            scalar2=None, op0=ALU.mult)
            e.drain()
            e.tensor_copy(out=negE_hi[:, :], in_=negE[:, :])
            e.tensor_copy(out=urow_b[:, :], in_=urow[:, :])
            e.tensor_copy(out=expsrow_b[:, :], in_=expsrow[:, :])
            e.drain()
            for h in range(H):
                e.tensor_scalar(out=argk[:, h * Q:(h + 1) * Q], in0=iotQr2[:, :],
                                scalar1=negE[:, h:h + 1], scalar2=None, op0=ALU.mult)
                e.tensor_scalar(out=argb[:, h * Q:(h + 1) * Q], in0=iotQr[:, :],
                                scalar1=negE[:, h:h + 1], scalar2=None, op0=ALU.mult)
            e.tensor_scalar(out=tri01[:, :], in0=deltf[:, 0:2 * Q], scalar1=0.0,
                            scalar2=None, op0=ALU.is_ge)
            e.tensor_scalar(out=i01[:, :], in0=deltf[:, 0:2 * Q], scalar1=-1.0,
                            scalar2=None, op0=ALU.is_equal)
            e.tensor_scalar(out=deltf[:, 2 * Q:4 * Q], in0=deltf[:, 0:2 * Q],
                            scalar1=-1.0, scalar2=None, op0=ALU.max)
            e.drain().then_inc(sem["v0"], 1)

        @blk.scalar
        def _(e):
            e.wait_ge(sem["v0"], 1)
            e.activation(out=wkrow_b[:, :], in_=argk[:, :], func=AF.Exp)
            e.activation(out=wbrow_b[:, :], in_=argb[:, :], func=AF.Exp)
            e.activation(out=sc128[:, :], in_=crow[:, :], func=AF.Exp,
                         scale=np.float32(-1.0)).then_inc(sem["s0"], 1)

        @blk.vector
        def _(e):
            e.wait_ge(sem["s0"], 1)
            e.tensor_scalar(out=sc128b[:, :], in0=sc128[:, :], scalar1=1.0,
                            scalar2=None, op0=ALU.add)
            e.drain()
            e.reciprocal(out=sc128[:, :], in_=sc128b[:, :])
            e.drain()
            e.tensor_tensor(out=sc128b[:, :], in0=sc128[:, :], in1=crow[:, :],
                            op=ALU.mult)
            e.drain()
            e.tensor_copy(out=siluc[:, :], in_=sc128b[:, :])
            e.tensor_copy(out=chalob[:, :], in_=chalo[:, :])
            e.drain()
            e.tensor_tensor(out=halosq[:, :], in0=chalob[:, :], in1=chalob[:, :],
                            op=ALU.mult)
            e.drain()
        barrier()

        @blk.gpsimd
        def _(e):
            for i, seg in enumerate((0, 1, 2, 3, 6, 7, 8, 9)):
                par = i % 2
                if i >= 2:
                    e.wait_ge(sem["t0"], i - 1)
                for j in range(4):
                    e.dma_start(out=wmbuf[:, par * DC * 512 + j * 1024:
                                        par * DC * 512 + (j + 1) * 1024],
                                in_=dram["wmod_r"][seg][:, j * 1024:(j + 1) * 1024]
                                ).then_inc(sem["dwmod"], 16)

        @blk.sync
        def _(e):
            for j, seg in enumerate((4, 5, 10, 11)):
                if j >= 1:
                    e.wait_ge(sem["t0f"], j)
                for jj in range(4):
                    e.dma_start(out=wmf32[:, jj * 1024:(jj + 1) * 1024],
                                in_=dram["wmod_r"][seg][:, jj * 1024:(jj + 1) * 1024]
                                ).then_inc(sem["dwmf"], 16)

        @blk.tensor
        def _(e):
            for i, seg in enumerate((0, 1, 2, 3, 6, 7, 8, 9)):
                par = i % 2
                e.wait_ge(sem["dwmod"], 64 * (i + 1))
                if i >= 1:
                    e.wait_ge(sem["vme"], i)
                for cc in range(DC):
                    mm = e.matmul(pA[0:1, 0:512], siluc[:, cc:cc + 1],
                                  wmbuf[:, par * DC * 512 + cc * 512:
                                        par * DC * 512 + (cc + 1) * 512],
                                  start=(cc == 0), stop=(cc == 7))
                mm.then_inc(sem["t0"], 1)
            for j, seg in enumerate((4, 5, 10, 11)):
                e.wait_ge(sem["dwmf"], 64 * (j + 1))
                if j >= 1:
                    e.wait_ge(sem["vmef"], j)
                for cc in range(DC):
                    mm = e.matmul(pA[0:1, 512:1024], sc128b[:, cc:cc + 1],
                                  wmf32[:, cc * 512:(cc + 1) * 512],
                                  start=(cc == 0), stop=(cc == 7))
                mm.then_inc(sem["t0f"], 1)
            e.matmul(pP1[:, 0:H], onesrow[:, :], negE_hi[:, :],
                     start=True, stop=True).then_inc(sem["t0"], 1)
            e.matmul(pP1[:, 16:16 + H], onesrow[:, :], urow_b[:, :],
                     start=True, stop=True).then_inc(sem["t0"], 1)
            e.nop().then_inc(sem["t0"], 1)
            e.nop().then_inc(sem["t0"], 1)
            e.matmul(pP1[:, 48:56], onesrow[:, :], expsrow_b[:, :],
                     start=True, stop=True).then_inc(sem["t0"], 1)
            e.wait_ge(sem["dwgt"], 16 * 4)
            e.wait_ge(sem["vmef"], 4)
            for sg in range(2):
                mm = e.matmul(pA[:, 512 + sg * 512:1024 + sg * 512], onesrow[:, :],
                              bvF[0:1, sg * 512:(sg + 1) * 512], start=True, stop=True)
            mm.then_inc(sem["t0"], 1)
            for cc in range(DC):
                if cc >= 1:
                    e.wait_ge(sem["v0"], 101 + cc)
                for hof in range(2):
                    h = 2 * cc + hof
                    mm = e.matmul(pP0[hof * 64:hof * 64 + 64,
                                      (cc % 2) * Q:(cc % 2 + 1) * Q],
                                  ones64[:, :], wkrow_b[:, h * Q:(h + 1) * Q],
                                  start=True, stop=True, tile_position=(0, hof * 64))
                mm.then_inc(sem["t0"], 1)
            for cc in range(DC):
                if cc >= 1:
                    e.wait_ge(sem["v0"], 109 + cc)
                for hof in range(2):
                    h = 2 * cc + hof
                    mm = e.matmul(pY1[hof * 64:hof * 64 + 64,
                                      (cc % 2) * Q:(cc % 2 + 1) * Q],
                                  ones64[:, :], wbrow_b[:, h * Q:(h + 1) * Q],
                                  start=True, stop=True, tile_position=(0, hof * 64))
                mm.then_inc(sem["t0"], 1)
            mm = e.matmul(pY0[:, 0:8], allones[:, :], chalob[:, :],
                          start=True, stop=True)
            mm.then_inc(sem["t0"], 1)
            mm = e.matmul(pY0[:, 8:16], allones[:, :], halosq[:, :],
                          start=True, stop=True)
            mm.then_inc(sem["t0"], 1)

        @blk.vector
        def _(e):
            for i, seg in enumerate((0, 1, 2, 3, 6, 7, 8, 9)):
                e.wait_ge(sem["t0"], i + 1)
                e.tensor_copy(out=modrow[:, seg * 512:(seg + 1) * 512],
                              in_=pA[0:1, 0:512]).then_inc(sem["vme"], 1)
            for j, seg in enumerate((4, 5, 10, 11)):
                e.wait_ge(sem["t0f"], j + 1)
                e.tensor_copy(out=modrow[:, seg * 512:(seg + 1) * 512],
                              in_=pA[0:1, 512:1024]).then_inc(sem["vmef"], 1)
            for j, base in enumerate((2048, 5120)):
                sl = slice(base, base + D)
                dsl = slice(j * D, (j + 1) * D)
                e.tensor_scalar(out=rtmp[:, dsl], in0=modrow[:, sl], scalar1=0.0,
                                scalar2=None, op0=ALU.is_ge)
                e.tensor_scalar(out=rtmp[:, dsl], in0=rtmp[:, dsl], scalar1=2.0,
                                scalar2=-1.0, op0=ALU.mult, op1=ALU.add)
                e.tensor_tensor(out=garow[:, dsl], in0=modrow[:, sl], in1=rtmp[:, dsl],
                                op=ALU.mult)
                e.tensor_scalar(out=garow[:, dsl], in0=garow[:, dsl], scalar1=0.25,
                                scalar2=None, op0=ALU.max)
                e.tensor_tensor(out=garow[:, dsl], in0=garow[:, dsl], in1=rtmp[:, dsl],
                                op=ALU.mult)
            e.drain().then_inc(sem["vgrow"], 1)
            e.wait_ge(sem["t0"], T0_EXPSB)
            e.tensor_copy(out=negE128[:, :], in_=pP1[:, 0:H])
            e.drain().then_inc(sem["v0"], 1)
            e.tensor_copy(out=u128[:, :], in_=pP1[:, 16:16 + H]).then_inc(sem["v0"], 1)
            for cc in range(DC):
                for hof in range(2):
                    e.tensor_copy(out=e2h[hof * 64:hof * 64 + 64, cc:cc + 1],
                                  in_=negE128[hof * 64:hof * 64 + 64,
                                              2 * cc + hof:2 * cc + hof + 1])
            e.drain().then_inc(sem["v0"], 1)
            e.nop().then_inc(sem["v0"], 1)
            e.tensor_copy(out=expsB[:, :], in_=pP1[:, 48:56]).then_inc(sem["v0"], 1)
            e.wait_ge(sem["t0"], T0_BV)
            e.tensor_copy(out=bvF[:, 0:512], in_=pA[:, 512:1024])
            e.tensor_copy(out=bvF[:, 512:1024], in_=pA[:, 1024:1536]
                          ).then_inc(sem["v0"], 95)
            for cc in range(DC):
                e.wait_ge(sem["t0"], T0_WKF(cc))
                e.tensor_copy(out=wkF[:, cc * Q:(cc + 1) * Q],
                              in_=pP0[:, (cc % 2) * Q:(cc % 2 + 1) * Q]
                              ).then_inc(sem["v0"], 1)
            for cc in range(DC):
                e.wait_ge(sem["t0"], T0_WBF(cc))
                e.tensor_copy(out=wbF[:, cc * Q:(cc + 1) * Q],
                              in_=pY1[:, (cc % 2) * Q:(cc % 2 + 1) * Q]
                              ).then_inc(sem["v0"], 1)

        @blk.scalar
        def _(e):
            e.wait_ge(sem["vme"], 8)
            e.wait_ge(sem["vmef"], 4)
            e.wait_ge(sem["vgrow"], 1)
            for dst, base in [(sa_c, 0), (ca1_c, 1024), (sb2_c, 3072), (cb1_c, 4096)]:
                for cc in range(DC):
                    e.dma_start(out=dst[:, cc:cc + 1],
                                in_=modrow[:, base + cc * 128:base + (cc + 1) * 128]
                                ).then_inc(sem["drow"], 16)
            for j, dst in enumerate([ga_c, gb_c]):
                for cc in range(DC):
                    e.dma_start(out=dst[:, cc:cc + 1],
                                in_=garow[:, j * D + cc * 128:j * D + (cc + 1) * 128]
                                ).then_inc(sem["drow"], 16)
            e.wait_ge(sem["v0"], 4)
            for cc in range(DC):
                for hof in range(2):
                    sl = slice(hof * 64, hof * 64 + 64)
                    e.activation(out=ws_pp[sl, cc:cc + 1], in_=negE128[sl, 2 * cc + hof:
                                                                      2 * cc + hof + 1],
                                 func=AF.Exp, scale=np.float32(Q))
                    e.activation(out=wsmT[sl, cc * NCH:(cc + 1) * NCH],
                                 in_=iot8[sl, :], func=AF.Exp,
                                 scale=e2h[sl, cc:cc + 1])
            e.wait_ge(sem["v0"], 6)
            for cc in range(DC):
                for hof in range(2):
                    sl = slice(hof * 64, hof * 64 + 64)
                    e.activation(out=combW[sl, cc * 8:(cc + 1) * 8], in_=expsB[sl, :],
                                 func=AF.Exp, scale=e2h[sl, cc:cc + 1])
            e.drain().then_inc(sem["s0"], 1)
        barrier()

        @blk.scalar
        def _(e):
            for i in range(H * 2):
                h, ph = i // 2, i % 2
                par = i % 2
                if i >= 2:
                    e.wait_ge(sem["v0"], 199 + i)
                e.activation(out=ftmpa[:, par * Q:(par + 1) * Q],
                             in_=deltf[:, 2 * Q + ph * Q:2 * Q + (ph + 1) * Q],
                             func=AF.Exp, scale=negE128[:, h:h + 1]
                             ).then_inc(sem["s0"], 1)

        @blk.vector
        def _(e):
            e.drain().then_inc(sem["v0"], 83)
            for i in range(H * 2):
                h, ph = i // 2, i % 2
                par = i % 2
                e.wait_ge(sem["s0"], 2 + i + 1)
                e.tensor_tensor(out=btmp[:, 0:Q], in0=ftmpa[:, par * Q:(par + 1) * Q],
                                in1=tri01[:, ph * Q:(ph + 1) * Q], op=ALU.mult)
                e.drain()
                e.scalar_tensor_tensor(
                    out=amask[:, h * 2 * Q + ph * Q:h * 2 * Q + (ph + 1) * Q],
                    in0=i01[:, ph * Q:(ph + 1) * Q], scalar=u128[:, h:h + 1],
                    in1=btmp[:, 0:Q], op0=ALU.mult, op1=ALU.add
                    ).then_inc(sem["v0"], 1)
            e.wait_ge(sem["drow"], 16 * 48)
            e.tensor_scalar(out=ca1_c[:, :], in0=ca1_c[:, :], scalar1=1.0,
                            scalar2=None, op0=ALU.add)
            e.tensor_scalar(out=cb1_c[:, :], in0=cb1_c[:, :], scalar1=1.0,
                            scalar2=None, op0=ALU.add)
            e.tensor_tensor(out=gabo_c[:, :], in0=ga_c[:, :], in1=bo_c[:, :],
                            op=ALU.mult)
            e.wait_ge(sem["t0"], T0_HQ)
            e.tensor_reduce(out=statf[:, 4:5], in_=pY0[:, 0:8], axis=AX.X, op=ALU.add)
            e.tensor_reduce(out=statf[:, 5:6], in_=pY0[:, 8:16], axis=AX.X, op=ALU.add)
            e.drain()
            e.tensor_scalar(out=statf[:, 0:1], in0=statf[:, 4:5], scalar1=1.0 / D,
                            scalar2=None, op0=ALU.mult)
            e.tensor_scalar(out=statf[:, 1:2], in0=statf[:, 5:6], scalar1=1.0 / D,
                            scalar2=None, op0=ALU.mult)
            e.drain()
            e.tensor_tensor(out=statf[:, 2:3], in0=statf[:, 0:1], in1=statf[:, 0:1],
                            op=ALU.mult)
            e.drain()
            e.tensor_tensor(out=statf[:, 1:2], in0=statf[:, 1:2], in1=statf[:, 2:3],
                            op=ALU.subtract)
            e.drain().then_inc(sem["v0"], 1)

        @blk.scalar
        def _(e):
            e.wait_ge(sem["v0"], 233)
            e.activation(out=statf[:, 3:4], in_=statf[:, 1:2], func=AF.Ln, bias=np.float32(EPS))
            e.drain()
            e.activation(out=statf[:, 2:3], in_=statf[:, 3:4], func=AF.Exp,
                         scale=np.float32(-0.5)).then_inc(sem["s0"], 40)

        @blk.vector
        def _(e):
            e.wait_ge(sem["s0"], 74)
            e.tensor_scalar(out=sc128[:, :], in0=chalo[:, :], scalar1=statf[:, 0:1],
                            scalar2=statf[:, 2:3], op0=ALU.subtract, op1=ALU.mult)
            e.drain()
            for cc in range(DC):
                e.tensor_scalar(out=sc128c[:, cc:cc + 1], in0=sc128[:, cc:cc + 1],
                                scalar1=ca1_c[:, cc:cc + 1], scalar2=sa_c[:, cc:cc + 1],
                                op0=ALU.mult, op1=ALU.add)
            e.drain()
            e.tensor_scalar(out=sc128c[:, :], in0=sc128c[:, :], scalar1=hvalid[:, 0:1],
                            scalar2=None, op0=ALU.mult)
            e.drain()
            for cc in range(DC):
                e.tensor_copy(out=hbuf[:, cc * HB:cc * HB + 1],
                              in_=sc128c[:, cc:cc + 1])
        if "modrow" in dbg_out:
            @blk.sync
            def _(e):
                e.wait_ge(sem["drow"], 16 * 48)
                e.dma_start(out=dbg_out["modrow"], in_=modrow[:, :]
                            ).then_inc(sem["dbgd2"], 16)
                e.wait_ge(sem["dbgd2"], 16)
        barrier()

        esP0.close()
        esP1 = contextlib.ExitStack()
        SBP1 = lambda *a, **k: _SB(esP1, *a, **k)
        wr_b = SBP1("wr_b", DC * D, BF16)
        wk_b = SBP1("wk_b", DC * D, BF16)
        wv_b = SBP1("wv_b", DC * D, BF16)
        xbf = SBP1("xbf", 2 * DC * SLAB, BF16)
        xsq = SBP1("xsq", DC * SLAB, BF16)
        xmr = SBP1("xmr", DC * SLAB, BF16)
        xmk = SBP1("xmk", DC * SLAB, BF16)
        xmv = SBP1("xmv", DC * SLAB, BF16)
        kfs = SBP1("kfs", DC * SLAB, BF16)
        rbf_sl = SBP1("rbf_sl", DC * SLAB, BF16)
        vTs = SBP1("vTs", 4 * D, BF16)
        abuf = SBP1("abuf", H * Q, BF16)
        ybf_c = SBP1("ybf_c", DC * Q, BF16)
        kwk = nc.virtual_sbuf_tensor("kwk", [128, DC * SLAB], BF16)
        nc.back_virtual_tensor_with(kwk, xsq)
        kwkT0 = nc.virtual_sbuf_tensor("kwkT0", [128, DC * SLAB], BF16)
        nc.back_virtual_tensor_with(kwkT0, xmr)
        kwkT1 = nc.virtual_sbuf_tensor("kwkT1", [128, DC * SLAB], BF16)
        nc.back_virtual_tensor_with(kwkT1, xmv)
        rwb = SBP1("rwb", DC * Q, BF16)
        for nm, t in (("wr_b", wr_b), ("wk_b", wk_b), ("wv_b", wv_b),
                      ("xbf", xbf), ("xsq", xsq), ("xmr", xmr), ("xmk", xmk),
                      ("xmv", xmv), ("kfs", kfs), ("rbf_sl", rbf_sl),
                      ("vTs", vTs), ("abuf", abuf), ("ybf_c", ybf_c),
                      ("kwk", kwk), ("kwkT0", kwkT0), ("kwkT1", kwkT1),
                      ("rwb", rwb)):
            sbt[nm] = t
        if upto in ("p1", "p3", "p4"):
            _emit_p1(blk, sem, sbt, psums, dram, upto in ("p3", "p4"))

        if upto in ("p3", "p4"):
            barrier()
            esP1.close()
            esR.close()
            xres = SB("xres", DC * (T + 2), BF16)
            m2B = SB("m2B", T + 2, BF16)
            r2B = SB("r2B", T + 2, BF16)
            w1qb = SB("w1qb", DC * HID, BF16)
            cwb = SB("cwb", HCC * 3)
            cbb = SB("cbb", HCC)
            for nm, t in (("xres", xres), ("m2B", m2B), ("r2B", r2B),
                          ("w1qb", w1qb), ("cwb", cwb), ("cbb", cbb)):
                sbt[nm] = t
            esP3 = contextlib.ExitStack()
            SBP3 = lambda *a, **k: _SB(esP3, *a, **k)
            for nm, cols, dt in (
                    ("wo_b", DC * D, BF16), ("sg", 4 * 512, F32),
                    ("sgin", 512, F32), ("sgmb", 2 * 512, BF16),
                    ("ybf3", DC * Q, BF16),
                    ("ybl", 2 * DC * Q, BF16), ("rwb3", 2 * DC * Q, BF16),
                    ("xf3", DC * Q, F32),
                    ("xh", 16, BF16),
                    ("xhsq", 16, BF16)):
                sbt[nm] = SBP3(nm, cols, dt)
            sbt["sq3"] = sbt["rwb3"]
            rg_sb = SBP3("rg_sb", D, BF16, parts=16)
            selb = SBP3("selb", 2, BF16, parts=16)
            rowpair = SBP3("rowpair", D, BF16, parts=2)
            padb = SBP3("padb", 400, BF16, parts=2)
            sbt["rg_sb"] = rg_sb; sbt["selb"] = selb
            sbt["rowpair"] = rowpair; sbt["padb"] = padb
            _emit_p3(nc, blk, sem, sbt, psums, dram, barrier, loopback)
            if upto == "p4":
                esP3.close()

        if upto == "p4":
            esP4 = contextlib.ExitStack()
            SBP4 = lambda *a, **k: _SB(esP4, *a, **k)
            for nm, cols, dt in (
                    ("udq", HCC * SLAB, BF16), ("h2w", DC * SLAB, BF16),
                    ("w2s", 2 * D, BF16),
                    ("sc13", 8 * SLAB, BF16),
                    ("scr2", 2 * SLAB, F32),
                    ("oout", 4 * SLAB, F32),
                    ("s1bb", SLAB, BF16), ("uib", SLAB, BF16)):
                sbt[nm] = SBP4(nm, cols, dt)
            sbt["s2bb"] = sbt["s1bb"]
            sbt["ds2b"] = sbt["uib"]
            sbt["xq1w"] = sbt["h2w"]
            sbt["qt"] = sbt["wqi"]
            sbt["mx4"] = sbt["btmp2"]
            sbt["mn4"] = sbt["btmp"]
            _emit_p4(nc, blk, sem, sbt, psums, dram, barrier)

        finals = dict(vstate=8 * NCH, sy=4 * NCH, sr=8 * NSLAB, sk=8 * NSLAB,
                      vvT=8 * NSLAB, vA=8 * NCH, vkwkTe=2 * NCH, ty=8 * NCH,
                      vh=8 * NSLAB, dyo=32 * NCH, dro=32 * NCH)
        p3_finals = dict(dsl=64, dsg=128, vcmb=3, vsc3=64, dld=512,
                         ty2=64, vy2=64, to=64, so3=64, vxr=64, dr2=256,
                         drg=64, tsel=1, vrow=1, dmh=256, gq3=32, t2s=4,
                         v2s=4, s2r=4, t2h=1, v2h=2, s2h=1, dws=2240,
                         sws=128, vwsc=4, dwq=2048, vq1=128, gq2=64,
                         dw2o=2048, tbc=2)
        if not loopback:
            p3_finals.update(ccs=1, ccr=1)

        @blk.gpsimd
        def _(e):
            if upto in ("p1", "p3", "p4"):
                for s_, v_ in finals.items():
                    e.wait_ge(sem[s_], v_)
                for s_, v_ in (("dcst", 256), ("dwgt", 448), ("dwmod", 512),
                               ("dwmf", 256), ("dxb", 256), ("drow", 768)):
                    e.wait_ge(sem[s_], v_)
            if upto in ("p3", "p4"):
                for s_, v_ in p3_finals.items():
                    e.wait_ge(sem[s_], v_)

        if dbg_out or "y_d" in dbg or "rw_d" in dbg:
            @blk.gpsimd
            def _(e):
                n = 0
                for nm, dst in dbg_out.items():
                    if nm == "modrow":
                        continue
                    e.dma_start(out=dst, in_=sbt[nm][:, :]).then_inc(sem["dbgd"], 16)
                    n += 1
                if n:
                    e.wait_ge(sem["dbgd"], 16 * n)
        if upto == "p1":
            esP1.close()
            esR.close()
        elif upto == "p3":
            esP3.close()
        elif upto == "p4":
            esP4.close()
    return nc


def _emit_p1(blk, sem, sbt, psums, dram, do_stats=False):
    g = lambda n: sbt[n]
    xbf, xsq = g("xbf"), g("xsq")
    mB, mF, rstdB = g("mB"), g("mF"), g("rstdB")
    statf2, statf3 = g("statf2"), g("statf3")
    hbuf, btmp, btmp2 = g("hbuf"), g("btmp"), g("btmp2")
    xmr, xmk, xmv = g("xmr"), g("xmk"), g("xmv")
    kfs, rbf_sl, vTs = g("kfs"), g("rbf_sl"), g("vTs")
    kwk, kwkT0, kwkT1 = g("kwk"), g("kwkT0"), g("kwkT1")
    abuf, rwb, ybf_c = g("abuf"), g("rwb"), g("ybf_c")
    sst, sbfT = g("sst"), g("sbfT")
    allones, identb = g("allones"), g("identb")
    wr_b, wk_b, wv_b = g("wr_b"), g("wk_b"), g("wv_b")
    wkF, wbF, amask = g("wkF"), g("wbF"), g("amask")
    maa_r, maa_k, maa_v = g("maa_rs"), g("maa_ks"), g("maa_vs")
    br_c, bk_c, bvF = g("br_s"), g("bk_s"), g("bvF")
    ca1_c, sa_c, ws_pp = g("ca1_c"), g("sa_c"), g("ws_pp")
    pP0, pP1, pA, pY0, pY1 = (psums[k] for k in ("pP0", "pP1", "pA", "pY0", "pY1"))
    vDS = psums["vDS"]
    x_sm, y_d, rw_d = dram["x_sm"], dram["y_d"], dram["rw_d"]

    @blk.gpsimd
    def _(e):
        for s in range(2):
            e.dma_start(out=xbf[:, (s % 2) * DC * SLAB:
                                (s % 2 + 1) * DC * SLAB],
                        in_=x_sm[:, s]).then_inc(sem["dxb"], 16)
            for i in range(3):
                e.dma_start(out=g("wstat")[0:1, s * 64 + i * 16:s * 64 + i * 16 + 16],
                            in_=x_sm[0:1, s, 0:16]).then_inc(sem["dxb"], 16)
        for wsb, wnm in ((wr_b, "wrT"), (wk_b, "wkT"), (wv_b, "wvT")):
            for cc in range(DC):
                e.dma_start(out=wsb[:, cc * D:(cc + 1) * D],
                            in_=dram[wnm][cc]).then_inc(sem["dwgt"], 16)
        for s in range(NSLAB):
            if s + 2 < NSLAB:
                sn = s + 2
                e.wait_ge(sem["vh"], 8 * (s + 1))
                e.wait_ge(sem["tstat"], s + 1)
                e.dma_start(out=xbf[:, (sn % 2) * DC * SLAB:
                                    (sn % 2 + 1) * DC * SLAB],
                            in_=x_sm[:, sn]).then_inc(sem["dxb"], 16)
                for i in range(3):
                    e.dma_start(out=g("wstat")[0:1, sn * 64 + i * 16:
                                               sn * 64 + i * 16 + 16],
                                in_=x_sm[0:1, sn, 0:16]).then_inc(sem["dxb"], 16)
        if do_stats:
            for k in range(128):
                if k >= 4:
                    e.wait_ge(sem["sws"], k - 3)
                wt = _w1tile(dram, k) if k < 64 else _w2tile(dram, k - 64)
                e.dma_start(out=g("wstat")[:, (k % 4) * 512:(k % 4 + 1) * 512],
                            in_=wt).then_inc(sem["dws"], 16)

    @blk.sync
    def _(e):
        for c in range(NCH):
            e.wait_ge(sem["sy"], 4 * c + 4)
            e.dma_start(out=y_d[:, c], in_=ybf_c[:, :]).then_inc(sem["dyo"], 16)
            e.wait_ge(sem["vrw"], 8 * c + 8)
            e.dma_start(out=rw_d[:, c], in_=rwb[:, :]).then_inc(sem["dro"], 16)
            e.dma_start(out=g("wabs")[0:1, 0:16],
                        in_=ybf_c[0:1, 0:16]).then_inc(sem["dyo"], 16)
            e.dma_start(out=g("wabs")[0:1, 16:32],
                        in_=rwb[0:1, 0:16]).then_inc(sem["dro"], 16)

    @blk.tensor
    def _(e):
        e.wait_ge(sem["dwgt"], 192)
        for s in range(NSLAB):
            par = s % 2
            xo = par * DC * SLAB
            if s > 0:
                e.wait_ge(sem["vstate"], 8 * 2 * s)
                e.wait_ge(sem["sk"], 8 * s)
            e.wait_ge(sem["dxb"], 64 * (s + 1))
            for cc in range(DC):
                mm = e.matmul(pP0[:, 0:SLAB], allones[:, :],
                              xbf[:, xo + cc * SLAB:xo + (cc + 1) * SLAB],
                              start=(cc == 0), stop=(cc == DC - 1))
            e.wait_ge(sem["vsq"], DC * s + DC)
            for cc in range(DC):
                mm = e.matmul(pP1[:, 0:SLAB], allones[:, :],
                              xsq[:, cc * SLAB:(cc + 1) * SLAB],
                              start=(cc == 0), stop=(cc == DC - 1))
            mm.then_inc(sem["tstat"], 1)
            e.wait_ge(sem["vstat"], s + 1)
            if s > 0:
                e.wait_ge(sem["vrw"], 16 * s)
                e.wait_ge(sem["tA"], 16 * s)
            for co in range(DC):
                bank = (pP0, pP1)[co % 2]
                if co >= 2:
                    e.wait_ge(sem["sr"], 8 * s + co - 1)
                for ci in range(DC):
                    if co == 0:
                        e.wait_ge(sem["vmix"], 16 * s + 2 * ci + 1)
                    mm = e.matmul(bank[:, 0:SLAB],
                                  wr_b[:, ci * D + co * 128:ci * D + (co + 1) * 128],
                                  xmr[:, ci * SLAB:(ci + 1) * SLAB],
                                  start=(ci == 0), stop=(ci == DC - 1))
                mm.then_inc(sem["tr"], 1)
            e.wait_ge(sem["dwgt"], 320)
            e.wait_ge(sem["sr"], 8 * s + 8)
            if s > 0:
                e.wait_ge(sem["vkwk"], 16 * s)
            for co in range(DC):
                bank = (pP0, pP1)[co % 2]
                if co >= 2:
                    e.wait_ge(sem["sk"], 8 * s + co - 1)
                for ci in range(DC):
                    if co == 0:
                        e.wait_ge(sem["vmix"], 16 * s + 2 * ci + 2)
                    mm = e.matmul(bank[:, 0:SLAB],
                                  wk_b[:, ci * D + co * 128:ci * D + (co + 1) * 128],
                                  xmk[:, ci * SLAB:(ci + 1) * SLAB],
                                  start=(ci == 0), stop=(ci == DC - 1))
                mm.then_inc(sem["tk"], 1)
            e.wait_ge(sem["dwgt"], 448)
            e.wait_ge(sem["vA"], 8 * 2 * s)
            for i in range(8):
                tsub, seg = i // 2, i % 2
                qo = (i % 4) * 512
                if i >= 4:
                    e.wait_ge(sem["vvT"], 8 * s + i - 3)
                for ci in range(DC):
                    if i == 0:
                        e.wait_ge(sem["gmix"], 8 * s + ci + 1)
                    mm = e.matmul(pA[:, qo:qo + 512],
                                  xmv[:, ci * SLAB + tsub * 128:
                                      ci * SLAB + (tsub + 1) * 128],
                                  wv_b[:, ci * D + seg * 512:ci * D + (seg + 1) * 512],
                                  start=(ci == 0), stop=(ci == DC - 1))
                mm.then_inc(sem["tv"], 1)
            for sc in range(2):
                c = 2 * s + sc
                e.wait_ge(sem["vvT"], 8 * s + 8)
                if c > 0:
                    e.wait_ge(sem["vA"], 8 * c)
                for cc in range(DC):
                    e.wait_ge(sem["vkwk"], 8 * c + cc + 1)
                    mm = e.matmul(pA[:, cc * 128:(cc + 1) * 128],
                                  kwk[:, cc * Q:cc * Q + 128], identb[:, :],
                                  start=True, stop=True)
                mm.then_inc(sem["tkwkT"], 1)
                for cc in range(DC):
                    mm = e.matmul(pA[:, 1024 + cc * 128:1024 + (cc + 1) * 128],
                                  kwk[:, cc * Q + 128:(cc + 1) * Q], identb[:, :],
                                  start=True, stop=True)
                mm.then_inc(sem["tkwkT"], 1)
                e.wait_ge(sem["vkwkTe"], 2 * c + 1)
                e.wait_ge(sem["vvT"], 8 * s + 8)
                e.wait_ge(sem["vstate"], 8 * c)
                e.wait_ge(sem["sk"], 8 * s + 8)
                for cc in range(DC):
                    for hof in range(2):
                        h = 2 * cc + hof
                        for qh in range(2):
                            if cc == 0 and hof == 0 and qh == 1:
                                e.wait_ge(sem["vkwkTe"], 2 * c + 2)
                            kT = (kwkT0, kwkT1)[qh]
                            tsub = sc * 2 + qh
                            mm = e.matmul(
                                vDS[hof * 64:hof * 64 + 64, cc * 64:(cc + 1) * 64],
                                kT[:, cc * 128 + hof * 64:cc * 128 + hof * 64 + 64],
                                vTs[:, tsub * D + h * NHD:tsub * D + (h + 1) * NHD],
                                start=(qh == 0), stop=(qh == 1),
                                tile_position=(0, hof * 64))
                    mm.then_inc(sem["tds"], 1)
                def emit_A(w):
                    if w >= 2:
                        e.wait_ge(sem["vA"], 8 * c + w - 1)
                    half = (w % 2) * 1024
                    for hof in range(2):
                        base = half + hof * 512
                        for ph in range(2):
                            mm = e.matmul(
                                pA[:, base + ph * Q:base + (ph + 1) * Q],
                                kfs[hof * 64:hof * 64 + 64,
                                    w * SLAB + sc * Q + ph * 128:
                                    w * SLAB + sc * Q + (ph + 1) * 128],
                                rbf_sl[hof * 64:hof * 64 + 64,
                                       w * SLAB + sc * Q:w * SLAB + (sc + 1) * Q],
                                start=True, stop=True, tile_position=(hof * 64, 0))
                    mm.then_inc(sem["tA"], 1)

                def emit_y(p):
                    gi = p // 2
                    bank = (pY0, pY1)[gi % 2]
                    colb = (p % 2) * Q
                    e.wait_ge(sem["vA"], 8 * c + p + 1)
                    e.wait_ge(sem["vrw"], 8 * c + p + 1)
                    if gi >= 2:
                        e.wait_ge(sem["sy"], 4 * c + gi - 1)
                    for hof in range(2):
                        h = 2 * p + hof
                        for ph in range(2):
                            tsub = sc * 2 + ph
                            e.matmul(bank[hof * 64:hof * 64 + 64, colb:colb + Q],
                                     vTs[:, tsub * D + h * NHD:tsub * D + (h + 1) * NHD],
                                     abuf[:, (p % 4) * 1024 + hof * 2 * Q + ph * Q:
                                          (p % 4) * 1024 + hof * 2 * Q + (ph + 1) * Q],
                                     start=(ph == 0), stop=False,
                                     tile_position=(0, hof * 64))
                        mm = e.matmul(bank[hof * 64:hof * 64 + 64, colb:colb + Q],
                                      sbfT[hof * 64:hof * 64 + 64, p * NHD:(p + 1) * NHD],
                                      rwb[hof * 64:hof * 64 + 64, p * Q:(p + 1) * Q],
                                      start=False, stop=True,
                                      tile_position=(hof * 64, hof * 64))
                    mm.then_inc(sem["ty"], 1)

                e.wait_ge(sem["sy"], 4 * c)
                emit_A(0)
                emit_A(1)
                emit_A(2)
                for p in range(DC):
                    if p + 3 < DC:
                        emit_A(p + 3)
                    emit_y(p)
        if do_stats:
            e.wait_ge(sem["vA"], 8 * NCH)
            e.wait_ge(sem["vwsc"], 1)
            e.matmul(pA[:, 0:1], g("allonesF")[:, :], g("s128")[:, 0:1],
                     start=True, stop=True).then_inc(sem["tbc"], 1)

    @blk.vector
    def _(e):
        for s in range(NSLAB):
            par = s % 2
            xo = par * DC * SLAB
            e.wait_ge(sem["dxb"], 64 * (s + 1))
            if s > 0:
                e.wait_ge(sem["tkwkT"], 4 * s)
            e.tensor_tensor(out=xsq[:, :],
                            in0=xbf[:, xo:xo + DC * SLAB],
                            in1=xbf[:, xo:xo + DC * SLAB],
                            op=ALU.mult).then_inc(sem["vsq"], DC)
            e.wait_ge(sem["tstat"], s + 1)
            e.tensor_scalar(out=mF[:, :], in0=pP0[:, 0:SLAB], scalar1=1.0 / D,
                            scalar2=None, op0=ALU.mult)
            e.tensor_copy(out=mB[:, :], in_=mF[:, :])
            e.tensor_scalar(out=statf2[:, :], in0=pP1[:, 0:SLAB], scalar1=1.0 / D,
                            scalar2=None, op0=ALU.mult)
            e.tensor_tensor(out=statf3[:, :], in0=mF[:, :], in1=mF[:, :], op=ALU.mult)
            e.tensor_tensor(out=statf2[:, :], in0=statf2[:, :], in1=statf3[:, :],
                            op=ALU.subtract).then_inc(sem["vstat"], 1)
            e.wait_ge(sem["s0"], 74 + s + 1)
            if s > 0:
                e.wait_ge(sem["gmix"], 8 * s)
            for cc in range(DC):
                hsl = slice(cc * HB + 1, cc * HB + 1 + SLAB)
                e.tensor_tensor(out=btmp[:, :],
                                in0=xbf[:, xo + cc * SLAB:xo + (cc + 1) * SLAB],
                                in1=mB[:, :], op=ALU.subtract)
                e.tensor_tensor(out=btmp2[:, :], in0=btmp[:, :], in1=rstdB[:, :],
                                op=ALU.mult)
                e.tensor_scalar(out=btmp2[:, :], in0=btmp2[:, :],
                                scalar1=ca1_c[:, cc:cc + 1], scalar2=None,
                                op0=ALU.mult)
                e.tensor_scalar(out=hbuf[:, hsl], in0=btmp2[:, :],
                                scalar1=sa_c[:, cc:cc + 1], scalar2=None,
                                op0=ALU.add).then_inc(sem["vh"], 1)
            if s > 0:
                e.wait_ge(sem["tr"], 8 * s)
                e.wait_ge(sem["tk"], 8 * s)
                e.wait_ge(sem["tds"], 16 * s)
            for cc in range(DC):
                e.tensor_tensor(out=btmp[:, :],
                                in0=hbuf[:, cc * HB:cc * HB + SLAB],
                                in1=hbuf[:, cc * HB + 1:cc * HB + 1 + SLAB],
                                op=ALU.subtract)
                for dst, mvec in ((xmr, maa_r), (xmk, maa_k)):
                    e.scalar_tensor_tensor(out=dst[:, cc * SLAB:(cc + 1) * SLAB],
                                           in0=btmp[:, :],
                                           scalar=mvec[:, cc:cc + 1],
                                           in1=hbuf[:, cc * HB + 1:
                                                   cc * HB + 1 + SLAB],
                                           op0=ALU.mult,
                                           op1=ALU.add).then_inc(sem["vmix"], 1)
            if s >= 1:
                e.wait_ge(sem["tv"], 8 * s)
            for cc in range(DC):
                e.tensor_tensor(out=btmp[:, :],
                                in0=hbuf[:, cc * HB:cc * HB + SLAB],
                                in1=hbuf[:, cc * HB + 1:cc * HB + 1 + SLAB],
                                op=ALU.subtract)
                e.scalar_tensor_tensor(out=xmv[:, cc * SLAB:(cc + 1) * SLAB],
                                       in0=btmp[:, :],
                                       scalar=maa_v[:, cc:cc + 1],
                                       in1=hbuf[:, cc * HB + 1:
                                               cc * HB + 1 + SLAB],
                                       op0=ALU.mult,
                                       op1=ALU.add).then_inc(sem["gmix"], 1)
            if s > 0:
                e.wait_ge(sem["tds"], 16 * s)
                e.wait_ge(sem["ty"], 16 * s)
            for i in range(8):
                tsub, seg = i // 2, i % 2
                qo = (i % 4) * 512
                e.wait_ge(sem["tv"], 8 * s + i + 1)
                e.tensor_tensor(out=vTs[:, tsub * D + seg * 512:
                                        tsub * D + (seg + 1) * 512],
                                in0=pA[:, qo:qo + 512],
                                in1=bvF[:, seg * 512:(seg + 1) * 512],
                                op=ALU.add).then_inc(sem["vvT"], 1)
            for sc in range(2):
                c = 2 * s + sc
                for cc in range(DC):
                    e.wait_ge(sem["sk"], 8 * s + cc + 1)
                    e.tensor_tensor(out=kwk[:, cc * Q:(cc + 1) * Q],
                                    in0=kfs[:, cc * SLAB + sc * Q:
                                            cc * SLAB + (sc + 1) * Q],
                                    in1=wkF[:, cc * Q:(cc + 1) * Q],
                                    op=ALU.mult).then_inc(sem["vkwk"], 1)
                e.wait_ge(sem["tkwkT"], 2 * c + 1)
                e.tensor_copy(out=kwkT0[:, 0:DC * 128], in_=pA[:, 0:1024]).then_inc(sem["vkwkTe"], 1)
                e.wait_ge(sem["tkwkT"], 2 * c + 2)
                e.tensor_copy(out=kwkT1[:, 0:DC * 128], in_=pA[:, 1024:2048]).then_inc(sem["vkwkTe"], 1)
                if c > 0:
                    e.wait_ge(sem["ty"], 8 * c)
                    e.wait_ge(sem["dro"], 32 * c)
                for cc in range(DC):
                    e.wait_ge(sem["sr"], 8 * s + cc + 1)
                    e.tensor_tensor(out=rwb[:, cc * Q:(cc + 1) * Q],
                                    in0=rbf_sl[:, cc * SLAB + sc * Q:
                                               cc * SLAB + (sc + 1) * Q],
                                    in1=wbF[:, cc * Q:(cc + 1) * Q],
                                    op=ALU.mult).then_inc(sem["vrw"], 1)
                for pw in range(4):
                    e.wait_ge(sem["tA"], 8 * c + 2 * pw + 2)
                    if pw >= 2:
                        e.wait_ge(sem["ty"], 8 * c + 2 * pw - 2)
                    sl4 = ((2 * pw) % 4) * 1024
                    e.tensor_tensor(out=abuf[:, sl4:sl4 + 2048],
                                    in0=pA[:, 0:2048],
                                    in1=amask[:, 2 * pw * 1024:(2 * pw + 2) * 1024],
                                    op=ALU.mult).then_inc(sem["vA"], 2)
                for cc in range(DC):
                    e.wait_ge(sem["tds"], 8 * c + cc + 1)
                    e.wait_ge(sem["ty"], 8 * c + cc + 1)
                    e.scalar_tensor_tensor(out=sbfT[:, cc * NHD:(cc + 1) * NHD],
                                           in0=sst[:, cc * NHD:(cc + 1) * NHD],
                                           scalar=ws_pp[:, cc:cc + 1],
                                           in1=vDS[:, cc * NHD:(cc + 1) * NHD],
                                           op0=ALU.mult, op1=ALU.add)
                    e.scalar_tensor_tensor(out=sst[:, cc * NHD:(cc + 1) * NHD],
                                           in0=sst[:, cc * NHD:(cc + 1) * NHD],
                                           scalar=ws_pp[:, cc:cc + 1],
                                           in1=vDS[:, cc * NHD:(cc + 1) * NHD],
                                           op0=ALU.mult, op1=ALU.add
                                           ).then_inc(sem["vstate"], 1)
            e.wait_ge(sem["gmix"], 8 * s + 8)
            for cc in range(DC):
                e.tensor_copy(out=hbuf[:, cc * HB:cc * HB + 1],
                              in_=hbuf[:, cc * HB + SLAB:cc * HB + SLAB + 1])
        if do_stats:
            e.memset(g("allonesF")[:, :], 1.0)
            e.wait_ge(sem["sws"], 64)
            e.tensor_reduce(out=g("s128")[:, 0:1], in_=g("wacc")[:, 0:64],
                            axis=AX.X, op=ALU.add)
            e.drain().then_inc(sem["vwsc"], 1)
            e.wait_ge(sem["tbc"], 1)
            e.tensor_scalar(out=g("swc")[:, 0:1], in0=pA[:, 0:1],
                            scalar1=1.0 / (D * HID), scalar2=1e-5,
                            op0=ALU.mult, op1=ALU.max)
            e.drain()
            e.reciprocal(out=g("wscl")[:, 0:1], in_=g("swc")[:, 0:1])
            e.drain()
            e.tensor_scalar(out=g("sw127")[:, 0:1], in0=g("swc")[:, 0:1],
                            scalar1=1.0 / 127.0, scalar2=None, op0=ALU.mult)
            e.drain().then_inc(sem["vwsc"], 1)

    @blk.scalar
    def _(e):
        for s in range(NSLAB):
            e.wait_ge(sem["vstat"], s + 1)
            e.activation(out=statf3[:, :], in_=statf2[:, :], func=AF.Ln, bias=np.float32(EPS))
            e.drain()
            e.activation(out=rstdB[:, :], in_=statf3[:, :], func=AF.Exp,
                         scale=np.float32(-0.5)).then_inc(sem["s0"], 1)
            if s > 0:
                e.wait_ge(sem["tA"], 16 * s)
                e.wait_ge(sem["vrw"], 16 * s)
            for co in range(DC):
                bank = (pP0, pP1)[co % 2]
                e.wait_ge(sem["tr"], 8 * s + co + 1)
                e.activation(out=rbf_sl[:, co * SLAB:(co + 1) * SLAB],
                             in_=bank[:, 0:SLAB], func=AF.Identity,
                             bias=br_c[:, co:co + 1]).then_inc(sem["sr"], 1)
            if s > 0:
                e.wait_ge(sem["vkwk"], 16 * s)
            for co in range(DC):
                bank = (pP0, pP1)[co % 2]
                e.wait_ge(sem["tk"], 8 * s + co + 1)
                e.activation(out=kfs[:, co * SLAB:(co + 1) * SLAB],
                             in_=bank[:, 0:SLAB], func=AF.Identity,
                             bias=bk_c[:, co:co + 1]).then_inc(sem["sk"], 1)
            for sc in range(2):
                c = 2 * s + sc
                for gi in range(4):
                    bank = (pY0, pY1)[gi % 2]
                    e.wait_ge(sem["ty"], 8 * c + 2 * gi + 2)
                    if c > 0:
                        e.wait_ge(sem["dyo"], 32 * c)
                    for j in range(2):
                        p = 2 * gi + j
                        cp = e.activation(out=ybf_c[:, p * Q:(p + 1) * Q],
                                          in_=bank[:, j * Q:(j + 1) * Q],
                                          func=AF.Identity)
                    cp.then_inc(sem["sy"], 1)
            if do_stats and s >= 1:
                for k in range(32 * (s - 1), 32 * s if s < NSLAB - 1 else 128):
                    e.wait_ge(sem["dws"], 16 * (k + 1))
                    e.activation(out=g("wabs")[:, :],
                                 in_=g("wstat")[:, (k % 4) * 512:(k % 4 + 1) * 512],
                                 func=AF.Abs, accum_out=g("wacc")[:, k:k + 1])
                    e.drain().then_inc(sem["sws"], 1)



RG8 = [[0, 1, 2, 3, 4, 5, 6, 7]]


def _w1tile(dram, k):
    return dram["w1T"][k // 8][:, (k % 8) * 512:(k % 8 + 1) * 512]


def _w2tile(dram, k):
    return dram["w2T"][k // 2][:, (k % 2) * 512:(k % 2 + 1) * 512]


def _emit_p3(nc, blk, sem, sbt, psums, dram, barrier, loopback):
    g = lambda n: sbt[n]
    xres, m2B, r2B = g("xres"), g("m2B"), g("r2B")
    w1qb, swc, wscl, sw127 = g("w1qb"), g("swc"), g("wscl"), g("sw127")
    wo_b, sg, sgin, sgmb = g("wo_b"), g("sg"), g("sgin"), g("sgmb")
    cwb, cbb = g("cwb"), g("cbb")
    ybf3, ybl, rwb3, xf3 = g("ybf3"), g("ybl"), g("rwb3"), g("xf3")
    sq3, xh, xhsq = g("sq3"), g("xh"), g("xhsq")
    wstat, wacc, s128 = g("wstat"), g("wacc"), g("s128")
    mB, rstdB = g("mB"), g("rstdB")
    wqi, wqi2 = g("wqi"), g("wqi2")
    rg_sb, selb, rowpair, padb = g("rg_sb"), g("selb"), g("rowpair"), g("padb")
    sst = g("sst")
    allones = g("allones")
    combW, wsmT = g("combW"), g("wsmT")
    ga_c, gabo_c = g("ga_c"), g("gabo_c")
    statf, statf2, statf3, mF = g("statf"), g("statf2"), g("statf3"), g("mF")
    allonesF = g("allonesF")
    pP0, pP1, pA, pY0, pY1 = (psums[k] for k in ("pP0", "pP1", "pA", "pY0", "pY1"))
    y_d, rw_d = dram["y_d"], dram["rw_d"]
    w1T, w2T, w2q_d, woT = dram["w1T"], dram["w2T"], dram["w2q_d"], dram["woT"]
    TP = T + 2

    def w1tile(k):
        return w1T[k // 8][:, (k % 8) * 512:(k % 8 + 1) * 512]

    def w2tile(k):
        return w2T[k // 2][:, (k % 2) * 512:(k % 2 + 1) * 512]

    @blk.sync
    def _(e):
        e.dma_start(out=dram["sloc"], in_=sst[:, :]).then_inc(sem["dsl"], 16)
        for i in range(3):
            e.dma_start(out=xf3[0:2, i * 8:i * 8 + 8],
                        in_=sst[0:2, 0:8]).then_inc(sem["dsl"], 16)
        if not loopback:
            e.wait_ge(sem["ccs"], 1)
            for j in range(4):
                e.dma_start(out=sg[:, j * 512:(j + 1) * 512],
                            in_=dram["sglob"][j * 128:(j + 1) * 128, :]
                            ).then_inc(sem["dsg"], 16)
            e.wait_ge(sem["vcmb"], 1)
            for j in range(4, NCORE):
                e.dma_start(out=sg[:, (j - 4) * 512:(j - 3) * 512],
                            in_=dram["sglob"][j * 128:(j + 1) * 128, :]
                            ).then_inc(sem["dsg"], 16)
        for m in range(NCH):
            bb = m % 2
            if m >= 2:
                e.wait_ge(sem["vy2"], 8 * (m - 1))
                e.wait_ge(sem["ty2"], 8 * (m - 1))
                e.wait_ge(sem["vxr"], 8 * (m - 1))
            e.dma_start(out=ybl[:, bb * 2048:bb * 2048 + 2048],
                        in_=y_d[:, m]).then_inc(sem["dld"], 16)
            e.dma_start(out=rwb3[:, bb * 2048:bb * 2048 + 2048],
                        in_=rw_d[:, m]).then_inc(sem["dld"], 16)
            if m > 0:
                e.wait_ge(sem["vxr"], 8 * m)
            e.dma_start(out=xf3[:, :],
                        in_=dram["x_cm"][:, m]).then_inc(sem["dld"], 16)
            e.dma_start(out=padb[0:1, 0:16],
                        in_=dram["y_d"][0:1, m, 0:16]).then_inc(sem["dld"], 16)
        if not loopback:
            e.wait_ge(sem["ccr"], 1)
            e.dma_start(out=rg_sb[:, :], in_=dram["rowsg"]).then_inc(sem["drg"], 16)
            for i in range(3):
                e.dma_start(out=padb[0:2, i * 128:i * 128 + 128],
                            in_=dram["rowsg"][0:2, 0:128]).then_inc(sem["drg"], 16)

    @blk.gpsimd
    def _(e):
        for ci in range(DC):
            e.dma_start(out=wo_b[:, ci * D:(ci + 1) * D],
                        in_=woT[ci]).then_inc(sem["dws"], 16)
        e.dma_start(out=selb[:, :], in_=dram["selmat"]).then_inc(sem["dws"], 16)
        e.dma_start(out=cwb[:, :], in_=dram["cw_c"]).then_inc(sem["dws"], 16)
        e.dma_start(out=cbb[:, :], in_=dram["cb_c"]).then_inc(sem["dws"], 16)
        e.dma_start(out=padb[0:2, 384:386],
                    in_=dram["selmat"][0:2, 0:2]).then_inc(sem["dws"], 16)
        if not loopback:
            e.wait_ge(sem["dsl"], 64)
            e.collective_compute("AllGather", ALU.bypass, replica_groups=RG8,
                                 ins=[dram["sloc"]], outs=[dram["sglob"]]
                                 ).then_inc(sem["ccs"], 1)
        e.wait_ge(sem["vwsc"], 2)
        for k in range(64):
            if k >= 4:
                e.wait_ge(sem["vq1"], k - 3)
            e.dma_start(out=wstat[:, (k % 4) * 512:(k % 4 + 1) * 512],
                        in_=w1tile(k)).then_inc(sem["dwq"], 16)
        e.wait_ge(sem["vwsc"], 4)
        e.wait_ge(sem["vq1"], 64)

        def w2out(e, ci):
            e.wait_ge(sem["gq2"], 2 * (ci + 1))
            e.dma_start(out=w2q_d[ci][:, 0:512],
                        in_=statf2[:, :]).then_inc(sem["dw2o"], 16)
            e.dma_start(out=w2q_d[ci][:, 512:1024],
                        in_=statf3[:, :]).then_inc(sem["dw2o"], 16)
            e.dma_start(out=rstdB[0:1, 0:16],
                        in_=statf2[0:1, 0:16]).then_inc(sem["dw2o"], 16)
            e.dma_start(out=rstdB[1:2, 16:32],
                        in_=statf2[0:1, 16:32]).then_inc(sem["dw2o"], 16)

        for k in range(64):
            if k >= 4:
                e.wait_ge(sem["vq1"], 64 + k - 3)
            e.dma_start(out=wstat[:, (k % 4) * 512:(k % 4 + 1) * 512],
                        in_=w2tile(k)).then_inc(sem["dwq"], 16)
            if k % 2 == 1 and k // 2 >= 1:
                w2out(e, k // 2 - 1)
        w2out(e, HCC - 1)
        if not loopback:
            e.wait_ge(sem["dr2"], 16 * 16)
            e.collective_compute("AllGather", ALU.bypass, replica_groups=RG8,
                                 ins=[dram["rows2"]], outs=[dram["rowsg"]]
                                 ).then_inc(sem["ccr"], 1)

    @blk.tensor
    def _(e):
        for m in range(NCH):
            bb = m % 2
            if m == 4:
                e.wait_ge(sem["vwsc"], 3)
                e.matmul(pA[:, 16:17], allonesF[:, :], s128[:, 1:2],
                         start=True, stop=True).then_inc(sem["tbc"], 1)
            e.wait_ge(sem["dld"], 16 * (4 * m + 4))
            if m > 0:
                e.wait_ge(sem["vy2"], 8 * m)
            for cc in range(DC):
                e.wait_ge(sem["vsc3"], 8 * m + cc + 1)
                bank = (pP0, pP1, pY0, pY1)[cc // 2]
                colb = (cc % 2) * Q
                for hof in range(2):
                    mm = e.matmul(bank[hof * 64:hof * 64 + 64, colb:colb + Q],
                                  sgmb[hof * 64:hof * 64 + 64,
                                       bb * 512 + cc * NHD:
                                       bb * 512 + (cc + 1) * NHD],
                                  rwb3[hof * 64:hof * 64 + 64,
                                       bb * 2048 + cc * Q:bb * 2048 + (cc + 1) * Q],
                                  start=True, stop=True,
                                  tile_position=(hof * 64, hof * 64))
                mm.then_inc(sem["ty2"], 1)
            if m == 0:
                e.wait_ge(sem["vwsc"], 2)
                e.wait_ge(sem["dws"], 2048 + 144)
            else:
                if m == 4:
                    e.wait_ge(sem["vwsc"], 4)
                e.wait_ge(sem["vxr"], 8 * m)
            for co in range(DC):
                for ci in range(DC):
                    if co == 0:
                        e.wait_ge(sem["vy2"], 8 * m + ci + 1)
                    mm = e.matmul(pA[:, co * Q:(co + 1) * Q],
                                  wo_b[:, ci * D + co * 128:ci * D + (co + 1) * 128],
                                  ybf3[:, ci * Q:(ci + 1) * Q],
                                  start=(ci == 0), stop=(ci == DC - 1))
                mm.then_inc(sem["to"], 1)
        for grp in range(4):
            e.wait_ge(sem["vxr"], 16 * (grp + 1))
            if grp == 0:
                e.wait_ge(sem["vy2"], 8 * NCH)
            else:
                e.wait_ge(sem["v2s"], grp)
            for cc in range(DC):
                mm = e.matmul(pP0[:, 0:SLAB], allones[:, :],
                              xres[:, cc * TP + 1 + grp * SLAB:
                                   cc * TP + 1 + (grp + 1) * SLAB],
                              start=(cc == 0), stop=(cc == DC - 1))
            e.wait_ge(sem["gq3"], 8 * (grp + 1))
            for cc in range(DC):
                mm = e.matmul(pP1[:, 0:SLAB], allones[:, :],
                              sq3[:, cc * SLAB:(cc + 1) * SLAB],
                              start=(cc == 0), stop=(cc == DC - 1))
            mm.then_inc(sem["t2s"], 1)
        e.wait_ge(sem["drg"], 64)
        e.wait_ge(sem["dws"], 2048 + 160)
        e.matmul(pY0[0:2, 0:512], selb[:, :], rg_sb[:, 0:512],
                 start=True, stop=True)
        e.matmul(pY1[0:2, 0:512], selb[:, :], rg_sb[:, 512:1024],
                 start=True, stop=True).then_inc(sem["tsel"], 1)
        e.wait_ge(sem["v2h"], 1)
        e.wait_ge(sem["v2s"], 4)
        for cc in range(DC):
            mm = e.matmul(pP0[:, 0:2], allones[:, :], xh[:, 2 * cc:2 * cc + 2],
                          start=(cc == 0), stop=(cc == DC - 1))
        for cc in range(DC):
            mm = e.matmul(pP1[:, 0:2], allones[:, :], xhsq[:, 2 * cc:2 * cc + 2],
                          start=(cc == 0), stop=(cc == DC - 1))
        mm.then_inc(sem["t2h"], 1)


    @blk.vector
    def _(e):
        if loopback:
            e.memset(sg[:, :], 0.0)
            e.drain().then_inc(sem["dsg"], 128)
            e.memset(rg_sb[:, :], 0.0)
            e.drain().then_inc(sem["drg"], 64)
        def quant_batch(m, half=None):
            rng = range(16) if half is None else range(8 * half, 8 * half + 8)
            for i in rng:
                k = 16 * m + i
                if k < 64:
                    ci, seg = k // 8, k % 8
                    e.wait_ge(sem["dwq"], 16 * (k + 1))
                    e.tensor_scalar(out=wqi[:, :],
                                    in0=wstat[:, (k % 4) * 512:(k % 4 + 1) * 512],
                                    scalar1=wscl[:, 0:1], scalar2=None,
                                    op0=ALU.mult).then_inc(sem["vq1"], 1)
                    e.tensor_scalar(out=w1qb[:, ci * HID + seg * 512:
                                           ci * HID + (seg + 1) * 512],
                                    in0=wqi[:, :], scalar1=1.0, scalar2=-1.0,
                                    op0=ALU.min, op1=ALU.max)
                else:
                    k2 = k - 64
                    ci, half = k2 // 2, k2 % 2
                    e.wait_ge(sem["dwq"], 1024 + 16 * (k2 + 1))
                    if half == 0 and ci >= 1:
                        e.wait_ge(sem["dw2o"], 64 * ci)
                    dst = (statf2, statf3)[half]
                    e.tensor_scalar(out=wqi2[:, :],
                                    in0=wstat[:, (k2 % 4) * 512:(k2 % 4 + 1) * 512],
                                    scalar1=wscl[:, 1:2], scalar2=None,
                                    op0=ALU.mult).then_inc(sem["vq1"], 1)
                    e.tensor_scalar(out=dst[:, :], in0=wqi2[:, :],
                                    scalar1=1.0, scalar2=-1.0,
                                    op0=ALU.min,
                                    op1=ALU.max).then_inc(sem["gq2"], 1)

        quant_batch(0)
        e.wait_ge(sem["dsg"], 64)
        for j in range(4):
            for cc in range(DC):
                if j == 0:
                    e.tensor_scalar(out=sgin[:, cc * NHD:(cc + 1) * NHD],
                                    in0=sg[:, cc * NHD:(cc + 1) * NHD],
                                    scalar1=combW[:, cc * 8:cc * 8 + 1],
                                    scalar2=None, op0=ALU.mult)
                else:
                    e.scalar_tensor_tensor(
                        out=sgin[:, cc * NHD:(cc + 1) * NHD],
                        in0=sg[:, j * 512 + cc * NHD:j * 512 + (cc + 1) * NHD],
                        scalar=combW[:, cc * 8 + j:cc * 8 + j + 1],
                        in1=sgin[:, cc * NHD:(cc + 1) * NHD],
                        op0=ALU.mult, op1=ALU.add)
        e.drain().then_inc(sem["vcmb"], 1)
        e.wait_ge(sem["dsg"], 128)
        for j in range(4, NCORE):
            for cc in range(DC):
                e.scalar_tensor_tensor(
                    out=sgin[:, cc * NHD:(cc + 1) * NHD],
                    in0=sg[:, (j - 4) * 512 + cc * NHD:(j - 4) * 512 + (cc + 1) * NHD],
                    scalar=combW[:, cc * 8 + j:cc * 8 + j + 1],
                    in1=sgin[:, cc * NHD:(cc + 1) * NHD],
                    op0=ALU.mult, op1=ALU.add)
        e.drain().then_inc(sem["vcmb"], 2)

        e.wait_ge(sem["vcmb"], 3)
        for m in range(NCH):
            bb = m % 2
            if m == 4:
                e.wait_ge(sem["sws"], 128)
                e.tensor_reduce(out=s128[:, 1:2], in_=wacc[:, 64:128],
                                axis=AX.X, op=ALU.add)
                e.drain().then_inc(sem["vwsc"], 1)
                e.wait_ge(sem["tbc"], 2)
                e.tensor_scalar(out=swc[:, 1:2], in0=pA[:, 16:17],
                                scalar1=1.0 / (D * HID), scalar2=1e-5,
                                op0=ALU.mult, op1=ALU.max)
                e.drain()
                e.reciprocal(out=wscl[:, 1:2], in_=swc[:, 1:2])
                e.drain()
                e.tensor_scalar(out=sw127[:, 1:2], in0=swc[:, 1:2],
                                scalar1=1.0 / 127.0, scalar2=None, op0=ALU.mult)
                e.drain().then_inc(sem["vwsc"], 1)
            e.wait_ge(sem["ty2"], 8 * m)
            for cc in range(DC):
                e.tensor_scalar(out=sgmb[:, bb * 512 + cc * NHD:
                                        bb * 512 + (cc + 1) * NHD],
                                in0=sgin[:, cc * NHD:(cc + 1) * NHD],
                                scalar1=wsmT[:, cc * 8 + m:cc * 8 + m + 1],
                                scalar2=None,
                                op0=ALU.mult).then_inc(sem["vsc3"], 1)
            if m > 1:
                quant_batch(m - 1, 0)
            e.wait_ge(sem["dld"], 16 * (4 * m + 4))
            if m > 0:
                e.wait_ge(sem["to"], 8 * m)
            for cc in range(DC):
                e.wait_ge(sem["ty2"], 8 * m + cc + 1)
                bank = (pP0, pP1, pY0, pY1)[cc // 2]
                colb = (cc % 2) * Q
                e.tensor_tensor(out=ybf3[:, cc * Q:(cc + 1) * Q],
                                in0=bank[:, colb:colb + Q],
                                in1=ybl[:, bb * 2048 + cc * Q:
                                        bb * 2048 + (cc + 1) * Q],
                                op=ALU.add).then_inc(sem["vy2"], 1)
            if m > 1:
                quant_batch(m - 1, 1)
            e.wait_ge(sem["dld"], 16 * 4 * (m + 1))
            for cc in range(DC):
                e.tensor_scalar(out=xf3[:, cc * Q:(cc + 1) * Q],
                                in0=xf3[:, cc * Q:(cc + 1) * Q],
                                scalar1=gabo_c[:, cc:cc + 1], scalar2=None,
                                op0=ALU.add)
            e.wait_ge(sem["to"], 8 * (m + 1))
            for cc in range(DC):
                e.scalar_tensor_tensor(out=xres[:, cc * TP + 1 + m * Q:
                                               cc * TP + 1 + (m + 1) * Q],
                                       in0=pA[:, cc * Q:(cc + 1) * Q],
                                       scalar=ga_c[:, cc:cc + 1],
                                       in1=xf3[:, cc * Q:(cc + 1) * Q],
                                       op0=ALU.mult,
                                       op1=ALU.add).then_inc(sem["vxr"], 1)
            e.nop().then_inc(sem["so3"], 8)
        quant_batch(NCH - 1)
        e.wait_ge(sem["dw2o"], 2048)
        for grp in range(4):
            if grp > 0:
                e.wait_ge(sem["t2s"], grp)
                e.wait_ge(sem["s2r"], grp)
            for cc in range(DC):
                sl = slice(cc * TP + 1 + grp * SLAB, cc * TP + 1 + (grp + 1) * SLAB)
                e.tensor_tensor(out=sq3[:, cc * SLAB:(cc + 1) * SLAB],
                                in0=xres[:, sl], in1=xres[:, sl],
                                op=ALU.mult).then_inc(sem["gq3"], 1)
            e.wait_ge(sem["t2s"], grp + 1)
            e.tensor_scalar(out=mF[:, :], in0=pP0[:, 0:SLAB], scalar1=1.0 / D,
                            scalar2=None, op0=ALU.mult)
            e.tensor_copy(out=m2B[:, 1 + grp * SLAB:1 + (grp + 1) * SLAB],
                          in_=mF[:, :])
            e.tensor_scalar(out=statf2[:, :], in0=pP1[:, 0:SLAB], scalar1=1.0 / D,
                            scalar2=None, op0=ALU.mult)
            e.tensor_tensor(out=statf3[:, :], in0=mF[:, :], in1=mF[:, :],
                            op=ALU.mult)
            e.tensor_tensor(out=statf2[:, :], in0=statf2[:, :], in1=statf3[:, :],
                            op=ALU.subtract).then_inc(sem["v2s"], 1)
        e.wait_ge(sem["tsel"], 1)
        e.tensor_copy(out=rowpair[0:2, 0:512], in_=pY0[0:2, 0:512])
        e.tensor_copy(out=rowpair[0:2, 512:1024], in_=pY1[0:2, 0:512])
        e.drain().then_inc(sem["vrow"], 1)
        e.wait_ge(sem["dmh"], 16 * 16)
        for cc in range(DC):
            e.tensor_copy(out=xh[:, 2 * cc:2 * cc + 1],
                          in_=xres[:, cc * TP:cc * TP + 1])
            e.tensor_copy(out=xh[:, 2 * cc + 1:2 * cc + 2],
                          in_=xres[:, cc * TP + T + 1:cc * TP + T + 2])
        e.drain()
        e.tensor_tensor(out=xhsq[:, :], in0=xh[:, :], in1=xh[:, :], op=ALU.mult)
        e.drain().then_inc(sem["v2h"], 1)
        e.wait_ge(sem["t2h"], 1)
        e.wait_ge(sem["s2r"], 4)
        e.tensor_scalar(out=statf3[:, 0:2], in0=pP0[:, 0:2], scalar1=1.0 / D,
                        scalar2=None, op0=ALU.mult)
        e.drain()
        e.tensor_copy(out=m2B[:, 0:1], in_=statf3[:, 0:1])
        e.tensor_copy(out=m2B[:, T + 1:T + 2], in_=statf3[:, 1:2])
        e.tensor_tensor(out=statf3[:, 2:4], in0=statf3[:, 0:2],
                        in1=statf3[:, 0:2], op=ALU.mult)
        e.tensor_scalar(out=statf2[:, 0:2], in0=pP1[:, 0:2], scalar1=1.0 / D,
                        scalar2=None, op0=ALU.mult)
        e.drain()
        e.tensor_tensor(out=statf2[:, 0:2], in0=statf2[:, 0:2],
                        in1=statf3[:, 2:4], op=ALU.subtract)
        e.drain().then_inc(sem["v2h"], 1)


    @blk.scalar
    def _(e):
        e.wait_ge(sem["vxr"], 8 * NCH)
        for cc in range(DC):
            e.dma_start(out=dram["rows2"][0:1, cc * 128:(cc + 1) * 128],
                        in_=xres[:, cc * TP + 1:cc * TP + 2]
                        ).then_inc(sem["dr2"], 16)
        for cc in range(DC):
            e.dma_start(out=dram["rows2"][1:2, cc * 128:(cc + 1) * 128],
                        in_=xres[:, cc * TP + T:cc * TP + T + 1]
                        ).then_inc(sem["dr2"], 16)
        for grp in range(4):
            e.wait_ge(sem["v2s"], grp + 1)
            e.activation(out=statf3[:, :], in_=statf2[:, :], func=AF.Ln,
                         bias=np.float32(EPS))
            e.drain()
            e.activation(out=r2B[:, 1 + grp * SLAB:1 + (grp + 1) * SLAB],
                         in_=statf3[:, :], func=AF.Exp,
                         scale=np.float32(-0.5)).then_inc(sem["s2r"], 1)
        e.wait_ge(sem["vrow"], 1)
        for cc in range(DC):
            e.dma_start(out=xres[:, cc * TP:cc * TP + 1],
                        in_=rowpair[0:1, cc * 128:(cc + 1) * 128]
                        ).then_inc(sem["dmh"], 16)
        for cc in range(DC):
            e.dma_start(out=xres[:, cc * TP + T + 1:cc * TP + T + 2],
                        in_=rowpair[1:2, cc * 128:(cc + 1) * 128]
                        ).then_inc(sem["dmh"], 16)
        e.wait_ge(sem["v2h"], 2)
        e.activation(out=statf3[:, 4:6], in_=statf2[:, 0:2], func=AF.Ln,
                     bias=np.float32(EPS))
        e.drain()
        e.activation(out=r2B[:, 0:1], in_=statf3[:, 4:5], func=AF.Exp,
                     scale=np.float32(-0.5))
        e.activation(out=r2B[:, T + 1:T + 2], in_=statf3[:, 5:6], func=AF.Exp,
                     scale=np.float32(-0.5))
        e.drain().then_inc(sem["s2h"], 1)




def _emit_p4(nc, blk, sem, sbt, psums, dram, barrier):
    g = lambda n: sbt[n]
    xres, m2B, r2B = g("xres"), g("m2B"), g("r2B")
    w1qb, swc, sw127 = g("w1qb"), g("swc"), g("sw127")
    cwb, cbb = g("cwb"), g("cbb")
    cb1_c, sb2_c, gb_c, lrv = g("cb1_c"), g("sb2_c"), g("gb_c"), g("lrv")
    onesrow = g("onesrow")
    btmp, btmp2 = g("btmp"), g("btmp2")
    udq, h2w, xq1w, w2s = g("udq"), g("h2w"), g("xq1w"), g("w2s")
    qt, mx4, mn4 = g("qt"), g("mx4"), g("mn4")
    sc13, scr2, oout = g("sc13"), g("scr2"), g("oout")
    s1bb, uib, s2bb, ds2b = g("s1bb"), g("uib"), g("s2bb"), g("ds2b")
    statf, identb = g("statf"), g("identb")
    pP0, pP1, pA, pY0, pY1 = (psums[k] for k in ("pP0", "pP1", "pA", "pY0", "pY1"))
    w2q_d, out_fm = dram["w2q_d"], dram["out_fm"]
    TP = T + 2
    NW = 5
    WJ = [512, 512, 512, 512, 16]
    HJ = [512, 512, 512, 512, 10]
    CJ = [510, 510, 510, 510, 8]
    w2bank = [pA, pA, pA, pA, pP0, pP1, pY0, pY1]

    def w2bk(co):
        if co < 4:
            return pA[:, co * 512:(co + 1) * 512]
        return w2bank[co][:, 0:512]

    @blk.sync
    def _(e):
        for s_, v_ in (("ty2", 64), ("to", 64), ("vxr", 64), ("t2s", 4),
                       ("t2h", 1), ("dmh", 256), ("vsc3", 64), ("tsel", 1)):
            e.wait_ge(sem[s_], v_)
        for j in range(NW):
            C = CJ[j]
            for ci in range(HCC):
                n = 32 * j + ci
                e.wait_ge(sem["dw2o"], 64 * (ci + 1))
                if n >= 2:
                    e.wait_ge(sem["tw2"], n - 1)
                e.dma_start(out=w2s[:, (ci % 2) * D:(ci % 2 + 1) * D],
                            in_=w2q_d[ci]).then_inc(sem["dwl"], 16)
            for co in range(DC):
                e.wait_ge(sem["vo4"], 8 * j + co + 1)
                e.dma_start(out=out_fm[co][:, 510 * j:510 * j + C],
                            in_=oout[:, (co % 4) * 512:(co % 4) * 512 + C]
                            ).then_inc(sem["dout"], 16)

    @blk.tensor
    def _(e):
        for j in range(NW):
            W, C = WJ[j], CJ[j]
            if j > 0:
                e.wait_ge(sem["vo4"], 8 * j)
            if j == 0:
                e.wait_ge(sem["v2h"], 2)
            e.wait_ge(sem["vh4"], 10 * j + 9)
            for tsub in range(4):
                e.matmul(pY0[:, tsub * 128:(tsub + 1) * 128],
                         mx4[:, tsub * 128:(tsub + 1) * 128], identb[:, :],
                         start=True, stop=True)
                mm = e.matmul(pY1[:, tsub * 128:(tsub + 1) * 128],
                              mn4[:, tsub * 128:(tsub + 1) * 128], identb[:, :],
                              start=True, stop=True)
            mm.then_inc(sem["tbc2"], 1)
            e.wait_ge(sem["vax"], j + 1)
            for tsub in range(4):
                e.matmul(pY0[0:1, tsub * 128:(tsub + 1) * 128],
                         btmp[:, 0 + tsub:1 + tsub], identb[:, :],
                         start=True, stop=True)
                mm = e.matmul(pY1[0:1, tsub * 128:(tsub + 1) * 128],
                              btmp[:, 4 + tsub:5 + tsub], identb[:, :],
                              start=True, stop=True)
            mm.then_inc(sem["tbc2"], 1)
            e.wait_ge(sem["vbc"], 4 * j + 1)
            e.matmul(pY0[:, 0:512], onesrow[:, :], mx4[0:1, 0:512],
                     start=True, stop=True)
            e.matmul(pY1[:, 0:512], onesrow[:, :], mn4[0:1, 0:512],
                     start=True, stop=True).then_inc(sem["tbc2"], 1)
            e.wait_ge(sem["vbc"], 4 * j + 2)
            if j > 0:
                e.wait_ge(sem["tw2"], 32 * j)
            for co in range(HCC):
                ub = (pA, pA, pA, pA, pY0, pY1)[co % 6]
                uo = (co % 6) * 512 if co % 6 < 4 else 0
                if co >= 6:
                    e.wait_ge(sem["vud"], 32 * j + co - 5)
                if co % 6 >= 4 and co < 6 and j > 0:
                    e.wait_ge(sem["vo4"], 8 * j)
                for ci in range(DC):
                    mm = e.matmul(ub[:, uo:uo + W],
                                  w1qb[:, ci * HID + co * 128:
                                       ci * HID + (co + 1) * 128],
                                  xq1w[:, ci * 512:ci * 512 + W],
                                  start=(ci == 0), stop=(ci == DC - 1))
                mm.then_inc(sem["tu"], 1)
            e.wait_ge(sem["vbc"], 4 * j + 2)
            e.wait_ge(sem["vh4"], 10 * j + 10)
            for tsub in range(4):
                e.matmul(pY0[:, tsub * 128:(tsub + 1) * 128],
                         mx4[:, tsub * 128:(tsub + 1) * 128], identb[:, :],
                         start=True, stop=True)
                mm = e.matmul(pY1[:, tsub * 128:(tsub + 1) * 128],
                              mn4[:, tsub * 128:(tsub + 1) * 128], identb[:, :],
                              start=True, stop=True)
            mm.then_inc(sem["tbc2"], 1)
            e.wait_ge(sem["vax2"], j + 1)
            for tsub in range(4):
                e.matmul(pY0[0:1, tsub * 128:(tsub + 1) * 128],
                         btmp[:, 0 + tsub:1 + tsub], identb[:, :],
                         start=True, stop=True)
                mm = e.matmul(pY1[0:1, tsub * 128:(tsub + 1) * 128],
                              btmp[:, 4 + tsub:5 + tsub], identb[:, :],
                              start=True, stop=True)
            mm.then_inc(sem["tbc2"], 1)
            e.wait_ge(sem["vbc"], 4 * j + 3)
            e.matmul(pY0[:, 0:512], onesrow[:, :], mx4[0:1, 0:512],
                     start=True, stop=True)
            e.matmul(pY1[:, 0:512], onesrow[:, :], mn4[0:1, 0:512],
                     start=True, stop=True).then_inc(sem["tbc2"], 1)
            e.wait_ge(sem["vbc"], 4 * j + 4)
            e.wait_ge(sem["vud"], 32 * (j + 1))
            for ci in range(HCC):
                e.wait_ge(sem["dwl"], 16 * (32 * j + ci + 1))
                e.wait_ge(sem["vxq2"], 32 * j + ci + 1)
                for co in range(DC):
                    mm = e.matmul(w2bk(co)[:, 0:C],
                                  w2s[:, (ci % 2) * D + co * 128:
                                      (ci % 2) * D + (co + 1) * 128],
                                  udq[:, ci * 512:ci * 512 + C],
                                  start=(ci == 0), stop=(ci == HCC - 1))
                mm.then_inc(sem["tw2"], 1)

    @blk.vector
    def _(e):
        def h2trees(j):
            W = WJ[j]
            dr = (j == 4)
            base = 510 * j
            e.wait_ge(sem["s2r"], min(j + 1, 4))
            if j in (0, 4):
                e.wait_ge(sem["s2h"], 1)
            if j > 0:
                e.wait_ge(sem["tbc2"], 6 * j - 1)
                e.wait_ge(sem["tu"], HCC * j)
            HW = HJ[j]
            for cc in range(DC):
                xsl = slice(cc * TP + base, cc * TP + base + HW)
                e.tensor_tensor(out=btmp[:, 0:HW], in0=xres[:, xsl],
                                in1=m2B[:, base:base + HW], op=ALU.subtract)
                if dr:
                    e.drain()
                e.tensor_tensor(out=btmp2[:, 0:HW], in0=btmp[:, 0:HW],
                                in1=r2B[:, base:base + HW], op=ALU.mult)
                if dr:
                    e.drain()
                e.tensor_scalar(out=btmp2[:, 0:HW], in0=btmp2[:, 0:HW],
                                scalar1=cb1_c[:, cc:cc + 1], scalar2=None,
                                op0=ALU.mult)
                if dr:
                    e.drain()
                e.tensor_scalar(out=h2w[:, cc * 512:cc * 512 + HW],
                                in0=btmp2[:, 0:HW], scalar1=sb2_c[:, cc:cc + 1],
                                scalar2=None, op0=ALU.add)
                if dr:
                    e.drain()
            if j == 0:
                e.drain()
                for cc in range(DC):
                    e.tensor_scalar(out=h2w[:, cc * 512:cc * 512 + 1],
                                    in0=h2w[:, cc * 512:cc * 512 + 1],
                                    scalar1=lrv[:, 0:1], scalar2=None,
                                    op0=ALU.mult)
                e.drain()
            if j == 4:
                e.drain()
                for cc in range(DC):
                    e.tensor_scalar(out=h2w[:, cc * 512 + 9:cc * 512 + 10],
                                    in0=h2w[:, cc * 512 + 9:cc * 512 + 10],
                                    scalar1=lrv[:, 1:2], scalar2=None,
                                    op0=ALU.mult)
                    e.memset(h2w[:, cc * 512 + 10:cc * 512 + 16], 0.0)
                e.drain()
            e.drain().then_inc(sem["vh4"], 8)
            e.tensor_tensor(out=sc13[:, 0:2048], in0=h2w[:, 0:2048],
                            in1=h2w[:, 2048:4096], op=ALU.max)
            e.tensor_tensor(out=sc13[:, 2048:4096], in0=h2w[:, 0:2048],
                            in1=h2w[:, 2048:4096], op=ALU.min)
            if dr:
                e.drain()
            e.tensor_tensor(out=sc13[:, 0:1024], in0=sc13[:, 0:1024],
                            in1=sc13[:, 1024:2048], op=ALU.max)
            e.tensor_tensor(out=sc13[:, 2048:3072], in0=sc13[:, 2048:3072],
                            in1=sc13[:, 3072:4096], op=ALU.min)
            if dr:
                e.drain()
            e.tensor_tensor(out=mx4[:, 0:512], in0=sc13[:, 0:512],
                            in1=sc13[:, 512:1024], op=ALU.max)
            e.tensor_tensor(out=mn4[:, 0:512], in0=sc13[:, 2048:2560],
                            in1=sc13[:, 2560:3072], op=ALU.min)
            e.drain().then_inc(sem["vh4"], 1)

        h2trees(0)
        for j in range(NW):
            W, C = WJ[j], CJ[j]
            dr = (j == 4)
            base = 510 * j

            def mdrain():
                if dr:
                    e.drain()
            e.wait_ge(sem["tbc2"], 6 * j + 1)
            for tsub in range(4):
                e.tensor_reduce(out=statf[:, tsub:tsub + 1],
                                in_=pY0[:, tsub * 128:(tsub + 1) * 128],
                                axis=AX.X, op=ALU.max, apply_absolute_value=True)
            for tsub in range(4):
                e.tensor_reduce(out=statf[:, 4 + tsub:5 + tsub],
                                in_=pY1[:, tsub * 128:(tsub + 1) * 128],
                                axis=AX.X, op=ALU.max, apply_absolute_value=True)
            e.drain()
            e.tensor_tensor(out=statf[:, 0:4], in0=statf[:, 0:4],
                            in1=statf[:, 4:8], op=ALU.max)
            e.drain()
            e.tensor_scalar(out=statf[:, 0:4], in0=statf[:, 0:4],
                            scalar1=1e-5, scalar2=None, op0=ALU.max)
            e.drain()
            e.reciprocal(out=statf[:, 8:12], in_=statf[:, 0:4])
            e.drain()
            e.tensor_scalar(out=btmp[:, 0:4], in0=statf[:, 8:12],
                            scalar1=127.0, scalar2=None, op0=ALU.mult)
            e.tensor_scalar(out=btmp[:, 4:8], in0=statf[:, 0:4],
                            scalar1=sw127[:, 0:1], scalar2=None, op0=ALU.mult)
            e.drain().then_inc(sem["vax"], 1)
            e.wait_ge(sem["tbc2"], 6 * j + 2)
            e.tensor_copy(out=mx4[0:1, 0:512], in_=pY0[0:1, 0:512])
            e.tensor_copy(out=mn4[0:1, 0:512], in_=pY1[0:1, 0:512])
            e.drain().then_inc(sem["vbc"], 1)
            e.wait_ge(sem["tbc2"], 6 * j + 3)
            e.tensor_copy(out=s1bb[:, :], in_=pY0[:, 0:512])
            e.tensor_copy(out=uib[:, :], in_=pY1[:, 0:512])
            e.drain()
            if j > 0:
                e.wait_ge(sem["tu"], 32 * j)
            for cc in range(DC):
                e.tensor_tensor(out=qt[:, 0:W], in0=h2w[:, cc * 512:cc * 512 + W],
                                in1=s1bb[:, 0:W], op=ALU.mult)
                mdrain()
                e.tensor_copy(out=xq1w[:, cc * 512:cc * 512 + W], in_=qt[:, 0:W])
                mdrain()
            e.drain().then_inc(sem["vbc"], 1)
            if j > 0:
                e.wait_ge(sem["tw2"], 32 * j)
            for co in range(HCC + 1):
                if co < HCC:
                    ub = (pA, pA, pA, pA, pY0, pY1)[co % 6]
                    uo = (co % 6) * 512 if co % 6 < 4 else 0
                    e.wait_ge(sem["tu"], 32 * j + co + 1)
                    e.tensor_tensor(out=udq[:, co * 512:co * 512 + W],
                                    in0=ub[:, uo:uo + W],
                                    in1=uib[:, 0:W],
                                    op=ALU.mult).then_inc(sem["vud"], 1)
                cv = co - 1
                if cv >= 0 and cv % 2 == 0:
                    e.wait_ge(sem["gcv"], 64 * j + 2 * (cv + 1))
                    if cv >= 2:
                        e.wait_ge(sem["ssl"], 32 * j + cv - 1)
                    e.tensor_tensor(out=scr2[:, 0:C],
                                    in0=scr2[:, 0:C],
                                    in1=sc13[:, 0:C], op=ALU.add)
                    mdrain()
                    e.tensor_tensor(out=scr2[:, 0:C],
                                    in0=scr2[:, 0:C],
                                    in1=sc13[:, 2048:2048 + C],
                                    op=ALU.add).then_inc(sem["vcv"], 1)
                if cv >= 2:
                    sv = cv - 2
                    if sv == 0:
                        e.tensor_copy(out=mx4[:, 0:C], in_=udq[:, 0:C])
                        e.tensor_copy(out=mn4[:, 0:C], in_=udq[:, 0:C])
                    else:
                        e.tensor_tensor(out=mx4[:, 0:C], in0=mx4[:, 0:C],
                                        in1=udq[:, sv * 512:sv * 512 + C],
                                        op=ALU.max)
                        e.tensor_tensor(out=mn4[:, 0:C], in0=mn4[:, 0:C],
                                        in1=udq[:, sv * 512:sv * 512 + C],
                                        op=ALU.min)
                    mdrain()
            e.wait_ge(sem["ssl"], 32 * (j + 1))
            for sv in (30, 31):
                e.tensor_tensor(out=mx4[:, 0:C], in0=mx4[:, 0:C],
                                in1=udq[:, sv * 512:sv * 512 + C], op=ALU.max)
                e.tensor_tensor(out=mn4[:, 0:C], in0=mn4[:, 0:C],
                                in1=udq[:, sv * 512:sv * 512 + C], op=ALU.min)
                mdrain()
            e.drain().then_inc(sem["vh4"], 1)
            e.wait_ge(sem["tbc2"], 6 * j + 4)
            for tsub in range(4):
                e.tensor_reduce(out=statf[:, tsub:tsub + 1],
                                in_=pY0[:, tsub * 128:(tsub + 1) * 128],
                                axis=AX.X, op=ALU.max, apply_absolute_value=True)
            for tsub in range(4):
                e.tensor_reduce(out=statf[:, 4 + tsub:5 + tsub],
                                in_=pY1[:, tsub * 128:(tsub + 1) * 128],
                                axis=AX.X, op=ALU.max, apply_absolute_value=True)
            e.drain()
            e.tensor_tensor(out=statf[:, 0:4], in0=statf[:, 0:4],
                            in1=statf[:, 4:8], op=ALU.max)
            e.drain()
            e.tensor_scalar(out=statf[:, 0:4], in0=statf[:, 0:4],
                            scalar1=1e-5, scalar2=None, op0=ALU.max)
            e.drain()
            e.reciprocal(out=statf[:, 8:12], in_=statf[:, 0:4])
            e.drain()
            e.tensor_scalar(out=btmp[:, 0:4], in0=statf[:, 8:12],
                            scalar1=127.0, scalar2=None, op0=ALU.mult)
            e.tensor_scalar(out=btmp[:, 4:8], in0=statf[:, 0:4],
                            scalar1=sw127[:, 1:2], scalar2=None, op0=ALU.mult)
            e.drain().then_inc(sem["vax2"], 1)
            e.wait_ge(sem["tbc2"], 6 * j + 5)
            e.tensor_copy(out=mx4[0:1, 0:512], in_=pY0[0:1, 0:512])
            e.tensor_copy(out=mn4[0:1, 0:512], in_=pY1[0:1, 0:512])
            e.drain().then_inc(sem["vbc"], 1)
            e.wait_ge(sem["tbc2"], 6 * j + 6)
            e.tensor_copy(out=s2bb[:, :], in_=pY0[:, 0:512])
            e.tensor_copy(out=ds2b[:, :], in_=pY1[:, 0:512])
            e.drain()
            for i in range(HCC):
                e.tensor_tensor(out=qt[:, 0:C], in0=udq[:, i * 512:i * 512 + C],
                                in1=s2bb[:, 0:C], op=ALU.mult)
                mdrain()
                e.tensor_copy(out=udq[:, i * 512:i * 512 + C], in_=qt[:, 0:C])
                mdrain()
                e.drain().then_inc(sem["vxq2"], 1)
            e.nop().then_inc(sem["vbc"], 1)
            if j + 1 < NW:
                h2trees(j + 1)
            e.wait_ge(sem["tw2"], 32 * (j + 1))
            for co in range(DC):
                if co >= 4:
                    e.wait_ge(sem["dout"], 16 * (8 * j + co - 3))
                osl = slice((co % 4) * 512, (co % 4) * 512 + C)
                e.tensor_tensor(out=oout[:, osl], in0=w2bk(co)[:, 0:C],
                                in1=ds2b[:, 0:C], op=ALU.mult)
                mdrain()
                e.scalar_tensor_tensor(out=oout[:, osl],
                                       in0=oout[:, osl],
                                       scalar=gb_c[:, co:co + 1],
                                       in1=xres[:, co * TP + 1 + base:
                                               co * TP + 1 + base + C],
                                       op0=ALU.mult, op1=ALU.add)
                e.drain().then_inc(sem["vo4"], 1)

    @blk.gpsimd
    def _(e):
        for j in range(NW):
            W, C = WJ[j], CJ[j]
            dr = (j == 4)

            def gdrain():
                if dr:
                    e.drain()
            for cv in range(1, HCC, 2):
                e.wait_ge(sem["gcv"], 64 * j + 2 * (cv + 1))
                if cv >= 2:
                    e.wait_ge(sem["ssl"], 32 * j + cv - 1)
                e.tensor_tensor(out=scr2[:, 512:512 + C],
                                in0=scr2[:, 512:512 + C],
                                in1=sc13[:, 512:512 + C], op=ALU.add)
                gdrain()
                e.tensor_tensor(out=scr2[:, 512:512 + C],
                                in0=scr2[:, 512:512 + C],
                                in1=sc13[:, 2048 + 512:2048 + 512 + C],
                                op=ALU.add).then_inc(sem["gvc"], 1)

        for s_, v_ in (("dout", 16 * 8 * NW), ("dwl", 16 * HCC * NW),
                       ("vo4", 8 * NW), ("tw2", HCC * NW), ("vh4", 10 * NW),
                       ("vax", NW), ("vax2", NW), ("tbc2", 6 * NW),
                       ("vbc", 4 * NW), ("tu", HCC * NW), ("vud", HCC * NW),
                       ("gcv", 2 * HCC * NW), ("vcv", HCC * NW // 2), ("gvc", HCC * NW // 2),
                       ("ssl", HCC * NW), ("vxq2", HCC * NW)):
            e.wait_ge(sem[s_], v_)

    @blk.scalar
    def _(e):
        e.wait_ge(sem["dws"], 2048 + 192)
        for j in range(NW):
            C = CJ[j]
            for co in range(HCC + 1):
                if co < HCC:
                    e.wait_ge(sem["vud"], 32 * j + co + 1)
                    if co >= 2:
                        if co % 2 == 0:
                            e.wait_ge(sem["vcv"], 16 * j + co // 2)
                        else:
                            e.wait_ge(sem["gvc"], 16 * j + (co - 1) // 2)
                    sl2 = (co % 2) * 512
                    e.activation(out=scr2[:, sl2:sl2 + C],
                                 in_=udq[:, co * 512 + 1:co * 512 + 1 + C],
                                 func=AF.Identity,
                                 scale=cwb[:, 3 * co + 1:3 * co + 2],
                                 bias=cbb[:, co:co + 1])
                    e.activation(out=sc13[:, sl2:sl2 + C],
                                 in_=udq[:, co * 512 + 0:co * 512 + C],
                                 func=AF.Identity,
                                 scale=cwb[:, 3 * co:3 * co + 1])
                    e.activation(out=sc13[:, 2048 + sl2:2048 + sl2 + C],
                                 in_=udq[:, co * 512 + 2:co * 512 + 2 + C],
                                 func=AF.Identity,
                                 scale=cwb[:, 3 * co + 2:3 * co + 3])
                    e.drain().then_inc(sem["gcv"], 2)
                sl = co - 1
                if sl >= 0:
                    if sl % 2 == 0:
                        e.wait_ge(sem["vcv"], 16 * j + sl // 2 + 1)
                    else:
                        e.wait_ge(sem["gvc"], 16 * j + (sl - 1) // 2 + 1)
                    e.activation(out=udq[:, sl * 512:sl * 512 + C],
                                 in_=scr2[:, (sl % 2) * 512:(sl % 2) * 512 + C],
                                 func=AF.Silu)
                    e.drain().then_inc(sem["ssl"], 1)



def _prep_core_inputs(inputs, core):
    inp = inputs
    x = np.asarray(inp["L0_attn_x"])[0]
    t0 = core * T
    xs = x[t0:t0 + T]
    d = {}
    d["x_sm"] = np.ascontiguousarray(
        xs.reshape(NSLAB, SLAB, DC, 128).transpose(3, 0, 2, 1).reshape(
            128, NSLAB, DC * SLAB), np.float32)
    d["x_cm"] = np.ascontiguousarray(
        xs.reshape(NCH, Q, DC, 128).transpose(3, 0, 2, 1).reshape(
            128, NCH, DC * Q), np.float32)
    halo = np.asarray(x[t0 - 1]) if core > 0 else np.zeros(D, np.float32)
    d["x_halo"] = np.ascontiguousarray(halo.reshape(DC, 128).T, np.float32)
    d["halovalid"] = np.full((128, 1), 1.0 if core > 0 else 0.0, np.float32)
    lr = np.ones((128, 2), np.float32)
    if core == 0:
        lr[:, 0] = 0.0
    if core == NCORE - 1:
        lr[:, 1] = 0.0
    d["lrvalid"] = lr
    selm = np.zeros((16, 2), np.float32)
    if core > 0:
        selm[2 * (core - 1) + 1, 0] = 1.0
    if core < NCORE - 1:
        selm[2 * (core + 1), 1] = 1.0
    d["selmat"] = selm
    ex = np.full(8, 1e6, np.float32)
    for j in range(core):
        ex[j] = 2048.0 * (core - 1 - j)
    d["exps2048"] = ex.reshape(1, 8)
    d["td_r"] = np.asarray(inp["L0_attn_time_decay"]).reshape(1, H).astype(np.float32)
    d["fa_r"] = np.asarray(inp["L0_attn_time_faaaa"]).reshape(1, H).astype(np.float32)
    d["c_fm"] = np.ascontiguousarray(
        np.asarray(inp["L0_c"]).reshape(DC, 128).T, np.float32)
    wm = np.asarray(inp["L0_wmod"]).astype(np.float32)
    d["wmod_r"] = np.ascontiguousarray(
        wm.reshape(DC, 128, 12, 512).transpose(2, 1, 0, 3).reshape(12, 128, DC * 512))
    for nm, key in (("maa_k", "L0_attn_time_maa_k"), ("maa_v", "L0_attn_time_maa_v"),
                    ("maa_r", "L0_attn_time_maa_r"), ("br_c", "L0_attn_br"),
                    ("bk_c", "L0_attn_bk"), ("bo_c", "L0_attn_bo")):
        d[nm] = np.ascontiguousarray(
            np.asarray(inp[key]).reshape(DC, 128).T, np.float32)
    d["bv_r"] = np.asarray(inp["L0_attn_bv"]).reshape(1, D).astype(np.float32)
    for nm, key in (("wrT", "L0_attn_Wr"), ("wkT", "L0_attn_Wk"),
                    ("wvT", "L0_attn_Wv"), ("woT", "L0_attn_Wo")):
        d[nm] = np.ascontiguousarray(
            np.asarray(inp[key]).T.reshape(DC, 128, D), np.float32)
    d["w1T"] = np.ascontiguousarray(
        np.asarray(inp["L0_ffn_w1"]).T.reshape(DC, 128, HID), np.float32)
    d["w2T"] = np.ascontiguousarray(
        np.asarray(inp["L0_ffn_w2"]).T.reshape(HCC, 128, D), np.float32)
    cw = np.asarray(inp["L0_ffn_conv_w"])[:, 0, :]
    d["cw_c"] = np.ascontiguousarray(
        cw.reshape(HCC, 128, 3).transpose(1, 0, 2).reshape(128, HCC * 3), np.float32)
    d["cb_c"] = np.ascontiguousarray(
        np.asarray(inp["L0_ffn_conv_b"]).reshape(HCC, 128).T, np.float32)
    return d


def kernel(**inputs):
    nc = build(upto="p4")
    in_maps = [_prep_core_inputs(inputs, core) for core in range(NCORE)]
    res = run_bass_kernel_spmd(nc, in_maps, core_ids=list(range(NCORE)))
    outs = []
    for core in range(NCORE):
        o = np.asarray(res.results[core]["out_fm"])
        outs.append(o.reshape(D, T).T)
    return np.concatenate(outs, axis=0).reshape(1, NCORE * T, D).astype(np.float32)
